# version 01
# baseline (speedup 1.0000x reference)
import numpy as np
import ml_dtypes
from contextlib import ExitStack

import concourse.bass as bass
import concourse.mybir as mybir
from concourse.bass_utils import run_bass_kernel_spmd

F32 = mybir.dt.float32
BF16 = mybir.dt.bfloat16
I32 = mybir.dt.int32
AF = mybir.ActivationFunctionType
ALU = mybir.AluOpType
AX = mybir.AxisListType

NCORES = 8
B = 2
T = 8192
D = 1024
TLOC = 2048
PAD = 8
WS = TLOC + PAD
NDT = D // 128
GRP = [[0, 1, 2, 3], [4, 5, 6, 7]]
GRP8 = [[0, 1, 2, 3, 4, 5, 6, 7]]


def chunks(total, lim, start=0):
    out = []
    c0 = start
    while c0 < start + total:
        w = min(lim, start + total - c0)
        out.append((c0, w))
        c0 += w
    return out


SELF_SYNC = ("vector", "scalar", "gpsimd")


class Ctx:
    ENGINES = ("sync", "tensor", "vector", "scalar", "gpsimd")
    DMAQ = ("sync", "gpsimd", "scalar")

    def __init__(self, nc):
        self.nc = nc
        self.es = ExitStack()
        self.q = {e: [] for e in self.ENGINES}
        self.sems = {}
        self.cnt = {e: 0 for e in self.ENGINES}
        self.dmacnt = {q: 0 for q in self.DMAQ}
        self.sbuf = {}
        self.dram = {}
        self.track_w = {}
        self.track_r = {}
        self.pending = {}
        self.uid = 0
        for e in self.ENGINES:
            self.sems[e] = self.es.enter_context(nc.semaphore(f"ms_{e}"))
        for qn in self.DMAQ:
            self.sems["dma_" + qn] = self.es.enter_context(nc.semaphore(f"dq_{qn}"))
        self.sems["cc"] = self.es.enter_context(nc.semaphore("cc_sem"))
        self.cccnt = 0
        self.applied = {e: {} for e in self.ENGINES}

    def sb(self, name, shape, dtype, side="left"):
        if name not in self.sbuf:
            self.uid += 1
            cm = self.nc.sbuf_tensor(f"{name}__{self.uid}", list(shape), dtype,
                                     side=side)
            h = cm.__enter__()
            self.sbuf[name] = (cm, h, self.uid)
        return self.sbuf[name][1]

    def free(self, names):
        present = [(self.sbuf[n][2], n) for n in names if n in self.sbuf]
        for _, n in sorted(present, reverse=True):
            cm, h, _u = self.sbuf.pop(n)
            cm.__exit__(None, None, None)
            if n in self.track_w:
                src, v = self.track_w.pop(n)
                self.pending[src] = max(self.pending.get(src, 0), v)
            for src, v in self.track_r.pop(n, {}).items():
                self.pending[src] = max(self.pending.get(src, 0), v)

    def dram_t(self, name, shape, dtype, **kw):
        if name not in self.dram:
            self.dram[name] = self.nc.dram_tensor(name, list(shape), dtype, **kw)
        return self.dram[name]

    def psum_t(self, name, shape, dtype=F32):
        if name not in self.sbuf:
            self.uid += 1
            cm = self.nc.psum_tensor(name, list(shape), dtype)
            h = cm.__enter__()
            self.sbuf[name] = (cm, h, self.uid)
        return self.sbuf[name][1]

    def _wait_into(self, waits, src, v):
        waits[src] = max(waits.get(src, 0), v)

    def _emit_waits(self, eng, waits):
        ap = self.applied[eng]
        for src, v in waits.items():
            if src == eng and eng not in SELF_SYNC:
                continue
            if ap.get(src, -1) >= v:
                continue
            sem = self.sems[src]

            def f(e, sem=sem, v=v):
                e.wait_ge(sem, v)
            self.q[eng].append(f)
            ap[src] = v

    def _bump_dma(self, src, v):
        if src.startswith("dma_"):
            return self.dmacnt[src[4:]]
        return v

    def _pre(self, eng, reads, writes):
        waits = {}
        for src, v in self.pending.items():
            self._wait_into(waits, src, self._bump_dma(src, v))
        for b in reads:
            if b in self.track_w:
                src, v = self.track_w[b]
                self._wait_into(waits, src, self._bump_dma(src, v))
        for b in writes:
            if b in self.track_w:
                src, v = self.track_w[b]
                self._wait_into(waits, src, self._bump_dma(src, v))
            for src, v in self.track_r.get(b, {}).items():
                self._wait_into(waits, src, self._bump_dma(src, v))
        self._emit_waits(eng, waits)

    def _post(self, key, reads, writes, val):
        for b in reads:
            self.track_r.setdefault(b, {})[key] = val
        for b in writes:
            self.track_w[b] = (key, val)
            self.track_r[b] = {}

    def op(self, eng, fn, reads=(), writes=()):
        self._pre(eng, reads, writes)
        self.cnt[eng] += 1
        sem = self.sems[eng]

        def f(e, fn=fn, sem=sem):
            r = fn(e)
            ins = r[-1] if isinstance(r, (list, tuple)) else r
            ins.then_inc(sem, 1)
        self.q[eng].append(f)
        self._post(eng, reads, writes, self.cnt[eng])
        return self.cnt[eng]

    def dma(self, qn, out_ap, in_ap, reads=(), writes=(), **kw):
        self._pre(qn, reads, writes)
        self.dmacnt[qn] += 16
        sem = self.sems["dma_" + qn]
        tgt = self.dmacnt[qn]

        def f(e, out_ap=out_ap, in_ap=in_ap, sem=sem, kw=kw):
            e.dma_start(out=out_ap, in_=in_ap, **kw).then_inc(sem, 16)
        self.q[qn].append(f)
        self._post("dma_" + qn, reads, writes, tgt)
        return tgt

    def idma(self, out_ap, in_, in_off_ap, reads=(), writes=(), **kw):
        qn = "gpsimd"
        self._pre(qn, reads, writes)
        self.dmacnt[qn] += 16
        sem = self.sems["dma_" + qn]
        tgt = self.dmacnt[qn]

        def f(e, out_ap=out_ap, in_=in_, in_off_ap=in_off_ap, sem=sem, kw=kw):
            e.indirect_dma_start(
                out=out_ap, out_offset=None, in_=in_,
                in_offset=bass.IndirectOffsetOnAxis(ap=in_off_ap, axis=0),
                **kw).then_inc(sem, 16)
        self.q[qn].append(f)
        self._post("dma_" + qn, reads, writes, tgt)
        return tgt

    def collective(self, kind, alu, groups, in_ap, out_ap, reads=(), writes=()):
        self._pre("gpsimd", reads, writes)
        self.cccnt += 1
        sem = self.sems["cc"]
        tgt = self.cccnt

        def f(e, in_ap=in_ap, out_ap=out_ap, sem=sem):
            e.collective_compute(kind, alu, replica_groups=groups,
                                 ins=[in_ap], outs=[out_ap]).then_inc(sem, 1)
        self.q["gpsimd"].append(f)
        self._post("cc", reads, writes, tgt)
        return tgt

    def drain_all(self):
        waits = {}
        for qn in self.DMAQ:
            if self.dmacnt[qn]:
                self._wait_into(waits, "dma_" + qn, self.dmacnt[qn])
        if self.cccnt:
            self._wait_into(waits, "cc", self.cccnt)
        for e in ("tensor", "vector", "scalar", "gpsimd"):
            if self.cnt[e]:
                self._wait_into(waits, e, self.cnt[e])
        self._emit_waits("sync", waits)

    def finalize(self, block):
        for eng in self.ENGINES:
            ops = list(self.q[eng])

            def run(engine, ops=ops):
                for f in ops:
                    f(engine)
            getattr(block, eng)(run)


def vop(ctx, meth, out_ap, *args, eng="vector", reads=(), writes=(), **kw):
    return ctx.op(eng, lambda e: getattr(e, meth)(out_ap, *args, **kw),
                  reads=reads, writes=writes)


def sact(ctx, out_ap, in_ap, func, reads=(), writes=(), **kw):
    return ctx.op("scalar", lambda e: e.activation(out_ap, in_ap, func, **kw),
                  reads=reads, writes=writes)


def mm(ctx, psum_ap, parts, reads=(), writes=(), start=True, stop=True):
    n = len(parts)

    def f(t):
        insts = []
        for i, (l, r) in enumerate(parts):
            insts.append(t.matmul(psum_ap, l, r,
                                  start=(start and i == 0),
                                  stop=(stop and i == n - 1)))
        return insts
    return ctx.op("tensor", f, reads=reads, writes=writes)


def mmn(ctx, items, reads=(), writes=()):
    def f(t):
        return [t.matmul(p, l, r, start=True, stop=True) for (p, l, r) in items]
    return ctx.op("tensor", f, reads=reads, writes=writes)



def _t(a):
    return np.ascontiguousarray(np.asarray(a).T)


def _c(a):
    return np.ascontiguousarray(np.asarray(a))


def host_prep(inputs):
    per_core = [{} for _ in range(NCORES)]
    shared = {}
    x = np.asarray(inputs["L0_attn_x"])
    for c in range(NCORES):
        b, r = c // 4, c % 4
        t0 = r * TLOC
        xs = np.zeros((D, WS), np.float32)
        xs[:, PAD:] = x[b, t0:t0 + TLOC, :].T
        if r > 0:
            xs[:, PAD - 3:PAD] = x[b, t0 - 3:t0, :].T
        per_core[c]["x_fm"] = _c(xs)
        m = np.zeros((128, 3), np.float32)
        if r >= 1:
            m[:, r - 1] = 1.0
        per_core[c]["scan_sel"] = m
    shared["l0_inW"] = _c(inputs["L0_attn_in_W"])
    shared["l0_inb"] = _c(inputs["L0_attn_in_b"]).reshape(4096, 1)
    shared["l0_rmsw"] = _c(inputs["L0_attn_rms_w"]).reshape(D, 1)
    shared["l0_convw"] = _c(inputs["L0_attn_conv_w"])
    shared["l0_convb"] = _c(inputs["L0_attn_conv_b"]).reshape(2048, 1)
    shared["l0_Alog"] = _c(inputs["L0_attn_A_log"])
    shared["l0_dbcW"] = _c(inputs["L0_attn_dbc_W"])
    shared["l0_dbcb"] = _c(inputs["L0_attn_dbc_b"]).reshape(160, 1)
    shared["l0_dupW"] = _c(inputs["L0_attn_dup_W"])
    shared["l0_dupb"] = _c(inputs["L0_attn_dup_b"]).reshape(2048, 1)
    shared["l0_outW"] = _c(inputs["L0_attn_out_W"])
    shared["l0_outb"] = _c(inputs["L0_attn_out_b"]).reshape(D, 1)
    shared["l0_WrT"] = _t(inputs["L0_ffn_Wr"])
    shared["l0_br"] = _c(np.broadcast_to(
        np.asarray(inputs["L0_ffn_br"]).reshape(1, 8), (128, 8)))
    shared["l0_W1T"] = _c(np.transpose(np.asarray(inputs["L0_ffn_W1"]), (0, 2, 1)))
    shared["l0_W2T"] = _c(np.transpose(np.asarray(inputs["L0_ffn_W2"]), (0, 2, 1)))
    shared["l0_b1"] = _c(inputs["L0_ffn_b1"])
    shared["l0_b2"] = _c(inputs["L0_ffn_b2"])
    H1 = 16
    shared["l1_RT"] = None
    R64 = np.zeros((64, 64), np.float32)
    for j in range(32):
        R64[2 * j, 2 * j + 1] = -1.0
        R64[2 * j + 1, 2 * j] = 1.0
    Rp = np.zeros((128, 128), np.float32)
    Rp[0:64, 0:64] = R64
    Rp[64:128, 64:128] = R64
    shared["l1_RT"] = _c(Rp.T)
    half1 = 32
    invf1 = 1.0 / 10000.0 ** (np.arange(half1) / half1)
    rows = invf1[(np.arange(64) // 2)]
    ang1 = np.arange(T)[None, :] * np.concatenate([rows, rows])[:, None]
    shared["l1_cosT"] = _c(np.cos(ang1).astype(np.float32))
    shared["l1_sinT"] = _c(np.sin(ang1).astype(np.float32))
    cm = np.zeros((128, 4 * 512), np.float32)
    for m in range(4):
        for p in range(128):
            cm[p, m * 512:(m + 1) * 512] = (np.arange(512) >= m * 128 + p)
    shared["l1_cmask"] = _c(cm)
    shared["l1_bo"] = _c(inputs["L1_attn_bo"]).reshape(D, 1)
    shared["l1_cw2"] = _c(np.asarray(inputs["L1_ffn_combine_weights"], np.float32))
    top = np.asarray(inputs["L1_ffn_top_idx"]).astype(np.int64)
    Rmax = 1
    rounds_per_core = []
    for c in range(NCORES):
        b, r = c // 4, c % 4
        t0 = r * TLOC + b * T
        tbl = [[] for _ in range(TLOC)]
        if b == 0:
            for e in range(8):
                col = top[:, e]
                for slot, tokg in enumerate(col):
                    if t0 <= tokg < t0 + TLOC:
                        tbl[tokg - t0].append(e * TLOC + slot)
        Rmax = max(Rmax, max((len(x) for x in tbl), default=1))
        rounds_per_core.append(tbl)
    OOB = NCORES * TLOC
    for c in range(NCORES):
        tbl = rounds_per_core[c]
        rix = np.full((Rmax, TLOC), OOB, np.int64)
        for t_loc, lst in enumerate(tbl):
            for ri, s in enumerate(lst):
                rix[ri, t_loc] = s
        per_core[c]["l1_rix"] = _c(rix.reshape(Rmax, 16, 128).astype(np.int32))
        per_core[c]["l1_gidx"] = _c(top[:, c].astype(np.int32).reshape(16, 128))
        es_ = np.zeros((128, 8), np.float32)
        es_[:, c] = 1.0
        per_core[c]["l1_esel"] = es_
        per_core[c]["l1_W1_my"] = _c(np.asarray(inputs["L1_ffn_W1"])[c])
        per_core[c]["l1_W2_my"] = _c(np.asarray(inputs["L1_ffn_W2"])[c])
        per_core[c]["l1_b1_my"] = _c(np.asarray(inputs["L1_ffn_b1"])[c]).reshape(1, 4096)
        per_core[c]["l1_b2_my"] = _c(np.asarray(inputs["L1_ffn_b2"])[c]).reshape(1, 1024)
        r_ = c % 4
        per_core[c]["l1_WqT_my"] = _c(_t(inputs["L1_attn_Wq"])[:, r_ * 256:(r_ + 1) * 256])
        per_core[c]["l1_WkT_my"] = _c(_t(inputs["L1_attn_Wk"])[:, r_ * 256:(r_ + 1) * 256])
        per_core[c]["l1_WvT_my"] = _c(_t(inputs["L1_attn_Wv"])[:, r_ * 256:(r_ + 1) * 256])
        per_core[c]["l1_WoT_my"] = _c(_t(inputs["L1_attn_Wo"])[r_ * 256:(r_ + 1) * 256, :])
        per_core[c]["l1_bq_my"] = _c(np.asarray(inputs["L1_attn_bq"])[r_ * 256:(r_ + 1) * 256].reshape(2, 128).T)
        per_core[c]["l1_bk_my"] = _c(np.asarray(inputs["L1_attn_bk"])[r_ * 256:(r_ + 1) * 256].reshape(2, 128).T)
        per_core[c]["l1_bv_my"] = _c(np.asarray(inputs["L1_attn_bv"])[r_ * 256:(r_ + 1) * 256].reshape(1, 256))
    shared["l2_lnw"] = _c(inputs["L2_attn_ln_w"]).reshape(D, 1)
    shared["l2_lnb"] = _c(inputs["L2_attn_ln_b"]).reshape(D, 1)
    shared["l2_gnw"] = _c(inputs["L2_attn_g_norm_w"]).reshape(1, 512)
    shared["l2_w1"] = _c(inputs["L2_ffn_w1"])
    shared["l2_w1T"] = _t(inputs["L2_ffn_w1"])
    shared["l2_w2"] = _c(inputs["L2_ffn_w2"])
    shared["l2_w2T"] = _t(inputs["L2_ffn_w2"])
    shared["l2_b1r"] = _c(inputs["L2_ffn_b1"]).reshape(1, 4096)
    shared["l2_b2"] = _c(inputs["L2_ffn_b2"]).reshape(D, 1)
    for c in range(NCORES):
        h = c % 4
        per_core[c]["l2_Wq_my"] = _c(np.asarray(inputs["L2_attn_Wq"])[:, h * 256:(h + 1) * 256])
        per_core[c]["l2_Wk_my"] = _c(np.asarray(inputs["L2_attn_Wk"])[:, h * 256:(h + 1) * 256])
        per_core[c]["l2_Wv_my"] = _c(np.asarray(inputs["L2_attn_Wv"])[:, h * 512:(h + 1) * 512])
        per_core[c]["l2_Wg_my"] = _c(np.asarray(inputs["L2_attn_Wg"])[:, h * 512:(h + 1) * 512])
        per_core[c]["l2_Wo_my"] = _c(np.asarray(inputs["L2_attn_Wo"])[h * 512:(h + 1) * 512, :])
        gamma = 1.0 - 2.0 ** (-5.0 - h)
        i = np.arange(64)
        Dm_ = np.where(i[:, None] >= i[None, :], gamma ** (i[:, None] - i[None, :]), 0.0)
        per_core[c]["l2_DmaskT"] = _c(Dm_.T.astype(np.float32))
        per_core[c]["l2_crossT"] = _c(np.tile((gamma ** (i + 1.0)).astype(np.float32)[None, :], (128, 1)))
        per_core[c]["l2_kdec"] = _c((gamma ** (64 - 1.0 - i)).astype(np.float32).reshape(64, 1))
        per_core[c]["l2_gC"] = np.full((128, 1), gamma ** 64, np.float32)
    invf2 = 1.0 / 10000.0 ** (np.arange(128) * 2.0 / 256.0)
    tpos = np.arange(T)
    ang2 = tpos[None, :] * invf2[:, None]
    shared["l2_cosT"] = _c(np.cos(ang2).astype(np.float32))
    shared["l2_sinT"] = _c(np.sin(ang2).astype(np.float32))
    g3 = lambda n: np.asarray(inputs[n], np.float32)
    shared["l3_w1"] = _c(g3("L3_attn_w1"))
    shared["l3_a1"] = _c(g3("L3_attn_a1"))
    shared["l3_g1"] = _c(g3("L3_attn_g1"))
    for nm in ("x_r", "x_w", "x_k", "x_v", "x_a", "x_g", "w0", "a0", "k_k", "k_a"):
        shared[f"l3_{nm}"] = _c(g3(f"L3_attn_{nm}").reshape(D, 1))
    shared["l3_lnb_r"] = _c(g3("L3_attn_ln_b").reshape(1, D))
    shared["l3_lnw_r"] = _c(g3("L3_attn_ln_w").reshape(1, D))
    mskU = np.triu(np.ones((128, 128), np.float32), 1)
    mskUI = np.triu(np.ones((128, 128), np.float32), 0)
    shared["l3_mskU"] = _c(mskU)
    shared["l3_mskUI"] = _c(mskUI)
    shared["l3_mskL"] = _c(np.tril(np.ones((128, 128), np.float32), -1))
    hselT = np.zeros((2, 128), np.float32)
    hselT[0, 0:64] = 1.0
    hselT[1, 64:128] = 1.0
    shared["l3_hsel"] = _c(hselT)
    shared["l3_hselC"] = _c(hselT.T)
    shared["l3f_WrT"] = _c(_t(np.asarray(inputs["L3_ffn_Wr"], np.float32)))
    W1_3 = np.asarray(inputs["L3_ffn_W1"], np.float32)
    perm = np.empty(4096, np.int64)
    for mp in range(16):
        perm[mp * 256:mp * 256 + 128] = np.arange(mp * 128, (mp + 1) * 128)
        perm[mp * 256 + 128:(mp + 1) * 256] = 2048 + np.arange(mp * 128, (mp + 1) * 128)
    shared["l3f_W1T"] = _c(W1_3.transpose(0, 2, 1)[:, :, perm])
    W2_3 = np.asarray(inputs["L3_ffn_W2"], np.float32)
    shared["l3f_W2T"] = _c(W2_3.transpose(0, 2, 1))
    for c in range(NCORES):
        r_ = c % 4
        cs = slice(r_ * 256, (r_ + 1) * 256)
        per_core[c]["l3_Wr_my"] = _c(g3("L3_attn_Wr")[:, cs])
        per_core[c]["l3_Wk_my"] = _c(g3("L3_attn_Wk")[:, cs])
        per_core[c]["l3_Wv_my"] = _c(g3("L3_attn_Wv")[:, cs])
        per_core[c]["l3_w2_my"] = _c(g3("L3_attn_w2")[:, cs])
        per_core[c]["l3_a2_my"] = _c(g3("L3_attn_a2")[:, cs])
        per_core[c]["l3_g2_my"] = _c(g3("L3_attn_g2")[:, cs])
        per_core[c]["l3_WoT_my"] = _c(g3("L3_attn_Wo")[cs, :])
        per_core[c]["l3_w0_my"] = _c(g3("L3_attn_w0").reshape(-1)[cs].reshape(2, 128).T)
        per_core[c]["l3_a0_my"] = _c(g3("L3_attn_a0").reshape(-1)[cs].reshape(2, 128).T)
        per_core[c]["l3_kk_my"] = _c(g3("L3_attn_k_k").reshape(-1)[cs].reshape(2, 128).T)
        per_core[c]["l3_ka_my"] = _c(g3("L3_attn_k_a").reshape(-1)[cs].reshape(2, 128).T)
        rk = g3("L3_attn_r_k").reshape(16, 64)[r_ * 4:(r_ + 1) * 4].reshape(-1)
        per_core[c]["l3_rk_my"] = _c(rk.reshape(2, 128).T)
        per_core[c]["l3_lnw_my"] = _c(g3("L3_attn_ln_w").reshape(-1)[cs].reshape(1, 256))
        per_core[c]["l3_lnb_my"] = _c(g3("L3_attn_ln_b").reshape(-1)[cs].reshape(1, 256))
    seln = np.zeros((16, 16 * 128), np.float32)
    for n in range(16):
        seln[n, n * 128:(n + 1) * 128] = 1.0
    shared["seln"] = seln
    shared["ident"] = np.eye(128, dtype=np.float32)
    return per_core, shared



def fm_norm(ctx, P, src, srckeys, dst, dstkeys, eps, sub_mean, mul_pp, mulkey,
            width=WS, lo=None, lokeys=None, aff=None, affkey=None, src_f32=True):
    sq = [ctx.sb(f"nsq{j}", [128, 512], BF16) for j in range(2)]
    ones_col = ctx.sb("ones_col", [128, 1], BF16)
    ones_colf = ctx.sb("ones_colf", [128, 1], F32)
    ones1 = ctx.sb("ones1", [1, 128], BF16)
    s1 = ctx.sb("ns1", [1, WS], F32)
    s2 = ctx.sb("ns2", [1, WS], F32)
    for ci, (c0, w) in enumerate(chunks(width, 512)):
        for i in range(NDT):
            jk = f"nsq{i % 2}"
            sact(ctx, sq[i % 2][:, 0:w], src[i][:, c0:c0 + w], AF.Square,
                 reads=(srckeys[i],), writes=(jk,))
            mm(ctx, P["psB"][0:1, 1024:1024 + w], [(ones_col[:], sq[i % 2][:, 0:w])],
               reads=(jk, "ones_col"), writes=("psB2",),
               start=(i == 0), stop=(i == NDT - 1))
        vop(ctx, "tensor_copy", s2[:, c0:c0 + w], P["psB"][0:1, 1024:1024 + w],
            reads=("psB2",), writes=("ns2",))
        if sub_mean:
            oc = ones_colf if src_f32 else ones_col
            mm(ctx, P["psB"][0:1, 1024:1024 + w],
               [(oc[:], src[i][:, c0:c0 + w]) for i in range(NDT)],
               reads=tuple(srckeys) + ("ones_colf", "ones_col"), writes=("psB2",))
            vop(ctx, "tensor_copy", s1[:, c0:c0 + w], P["psB"][0:1, 1024:1024 + w],
                reads=("psB2",), writes=("ns1",))
    rstd = ctx.sb("nrstd", [1, WS], F32)
    tmp = ctx.sb("ntmp", [1, WS], F32)
    if sub_mean:
        vop(ctx, "tensor_scalar_mul", s1[:, 0:width], s1[:, 0:width], 1.0 / D,
            reads=("ns1",), writes=("ns1",))
        vop(ctx, "tensor_mul", tmp[:, 0:width], s1[:, 0:width], s1[:, 0:width],
            reads=("ns1",), writes=("ntmp",))
        vop(ctx, "scalar_tensor_tensor", rstd[:, 0:width], s2[:, 0:width], 1.0 / D,
            tmp[:, 0:width], ALU.mult, ALU.subtract,
            reads=("ns2", "ntmp"), writes=("nrstd",))
    else:
        vop(ctx, "tensor_scalar_mul", rstd[:, 0:width], s2[:, 0:width], 1.0 / D,
            reads=("ns2",), writes=("nrstd",))
    vop(ctx, "tensor_scalar_add", rstd[:, 0:width], rstd[:, 0:width], eps,
        reads=("nrstd",), writes=("nrstd",))
    sact(ctx, rstd[:, 0:width], rstd[:, 0:width], AF.Ln, reads=("nrstd",),
         writes=("nrstd",))
    sact(ctx, rstd[:, 0:width], rstd[:, 0:width], AF.Exp, scale=-0.5,
         reads=("nrstd",), writes=("nrstd",))
    mubf = ctx.sb("nmubf", [1, WS], BF16)
    rsbf = ctx.sb("nrsbf", [1, WS], BF16)
    if sub_mean:
        vop(ctx, "tensor_copy", mubf[:, 0:width], s1[:, 0:width], reads=("ns1",),
            writes=("nmubf",))
    vop(ctx, "tensor_copy", rsbf[:, 0:width], rstd[:, 0:width], reads=("nrstd",),
        writes=("nrsbf",))
    mu_bc = ctx.sb("nmubc", [128, WS], BF16)
    rs_bc = ctx.sb("nrsbc", [128, WS], BF16)
    for c0, w in chunks(width, 512):
        if sub_mean:
            mm(ctx, P["psB"][:, 0:w], [(ones1[:], mubf[:, c0:c0 + w])],
               reads=("nmubf", "ones1"), writes=("psB0",))
            vop(ctx, "tensor_copy", mu_bc[:, c0:c0 + w], P["psB"][:, 0:w],
                reads=("psB0",), writes=("nmubc",))
        mm(ctx, P["psB"][:, 512:512 + w], [(ones1[:], rsbf[:, c0:c0 + w])],
           reads=("nrsbf", "ones1"), writes=("psB1",))
        vop(ctx, "tensor_copy", rs_bc[:, c0:c0 + w], P["psB"][:, 512:512 + w],
            reads=("psB1",), writes=("nrsbc",))
    tmpn = ctx.sb("nnorm", [128, WS], F32)
    tmpf = ctx.sb("nnorm2", [128, WS], F32)
    for i in range(NDT):
        if sub_mean:
            vop(ctx, "tensor_sub", tmpn[:, 0:width], src[i][:, 0:width],
                mu_bc[:, 0:width], reads=(srckeys[i], "nmubc"), writes=("nnorm",))
            if aff is not None:
                vop(ctx, "tensor_mul", tmpf[:, 0:width], tmpn[:, 0:width],
                    rs_bc[:, 0:width], reads=("nnorm", "nrsbc"), writes=("nnorm2",))
                vop(ctx, "tensor_scalar", dst[i][:, 0:width], tmpf[:, 0:width],
                    aff[0][i], aff[1][i], ALU.mult, ALU.add,
                    reads=("nnorm2", affkey), writes=(dstkeys[i],))
            elif lo is None:
                vop(ctx, "tensor_mul", dst[i][:, 0:width], tmpn[:, 0:width],
                    rs_bc[:, 0:width], reads=("nnorm", "nrsbc"), writes=(dstkeys[i],))
            else:
                vop(ctx, "tensor_mul", tmpf[:, 0:width], tmpn[:, 0:width],
                    rs_bc[:, 0:width], reads=("nnorm", "nrsbc"), writes=("nnorm2",))
                vop(ctx, "tensor_copy", dst[i][:, 0:width], tmpf[:, 0:width],
                    reads=("nnorm2",), writes=(dstkeys[i],))
                vop(ctx, "tensor_sub", lo[i][:, 0:width], tmpf[:, 0:width],
                    dst[i][:, 0:width], reads=("nnorm2", dstkeys[i]),
                    writes=(lokeys[i],))
        else:
            vop(ctx, "scalar_tensor_tensor", dst[i][:, 0:width], src[i][:, 0:width],
                mul_pp[i], rs_bc[:, 0:width], ALU.mult, ALU.mult,
                reads=(srckeys[i], "nrsbc", mulkey), writes=(dstkeys[i],))
    ctx.free(["nsq0", "nsq1", "ns1", "ns2", "nrstd", "ntmp", "nmubf", "nrsbf",
              "nmubc", "nrsbc", "nnorm", "nnorm2"])



def emit_l0(ctx, P, x_in, dbg=()):
    AI = lambda name, shape, dt=F32: ctx.dram_t(name, shape, dt, kind="ExternalInput").ap()
    inW = AI("l0_inW", [1024, 4096])
    inb = AI("l0_inb", [4096, 1])
    rmsw = AI("l0_rmsw", [D, 1])
    convw = AI("l0_convw", [2048, 4])
    convb = AI("l0_convb", [2048, 1])
    Alog = AI("l0_Alog", [2048, 16])
    dbcW = AI("l0_dbcW", [2048, 160])
    dbcb = AI("l0_dbcb", [160, 1])
    dupW = AI("l0_dupW", [128, 2048])
    dupb = AI("l0_dupb", [2048, 1])
    outW = AI("l0_outW", [2048, 1024])
    outb = AI("l0_outb", [D, 1])
    WrT = AI("l0_WrT", [1024, 8])
    br_in = AI("l0_br", [128, 8])
    W1T = AI("l0_W1T", [8, 1024, 4096])
    W2T = AI("l0_W2T", [8, 4096, 1024])
    b1_in = AI("l0_b1", [8, 4096])
    b2_in = AI("l0_b2", [8, 1024])
    seln_in = AI("seln", [16, 2048])
    ssel_in = AI("scan_sel", [128, 3])
    ident_in = AI("ident", [128, 128])

    x1_dr = ctx.dram_t("l0_x1_dr", [16, 128, WS], F32).ap()
    x1s_dr = ctx.dram_t("l0_x1s_dr", [16, 128, TLOC], F32).ap()
    x1sb_dr = ctx.dram_t("l0_x1sb_dr", [16, 128, TLOC], BF16).ap()
    gs_dr = ctx.dram_t("l0_gs_dr", [16, 128, TLOC], BF16).ap()
    delta_dr = ctx.dram_t("l0_delta_dr", [16, 128, TLOC], F32).ap()
    y1_dr = ctx.dram_t("l0_y1_dr", [16, 128, TLOC], F32).ap()
    y2_dr = ctx.dram_t("l0_y2_dr", [16, 128, TLOC], F32).ap()
    h_dr = ctx.dram_t("l0_h_dr", [32, 128, TLOC], BF16).ap()
    mix_dr = ctx.dram_t("l0_mix_dr", [8, 128, TLOC], F32).ap()
    cc_in = ctx.dram_t("l0_cc_in", [128, 1024], F32).ap()
    cc_out = ctx.dram_t("l0_cc_out", [512, 1024], F32).ap()

    def ldc(name, rows, cols, src_slices, dt=F32, qn="sync"):
        tl = ctx.sb(name, [rows, cols], dt)
        for (dst_sl, src_ap) in src_slices:
            ctx.dma(qn, tl[dst_sl] if dst_sl else tl[:], src_ap, writes=(name,))
        return tl

    def col_slices(src, n, per=1):
        return [((slice(None), slice(i * per, (i + 1) * per)),
                 src[i * 128:(i + 1) * 128, :]) for i in range(n)]

    sb_rmsw = ldc("rmsw", 128, NDT, col_slices(rmsw, NDT))
    sb_inb = ldc("inb", 128, 32, col_slices(inb, 32))
    sb_convw = ldc("convw", 128, 64, col_slices(convw, 16, 4))
    sb_convb = ldc("convb", 128, 16, col_slices(convb, 16))
    sb_Alog = ldc("Alog", 128, 256, col_slices(Alog, 16, 16))
    sb_dbcb = ldc("dbcb", 128, 1, [(None, dbcb[0:128, :])])
    sb_bcbB = ldc("bcbB", 16, 1, [(None, dbcb[128:144, :])])
    sb_bcbC = ldc("bcbC", 16, 1, [(None, dbcb[144:160, :])])
    sb_dupb = ldc("dupb", 128, 16, col_slices(dupb, 16))
    sb_outb = ldc("outb", 128, NDT, col_slices(outb, NDT))
    sb_seln = ldc("c_seln", 16, 2048, [(None, seln_in[:])], BF16, qn="gpsimd")
    sb_ssel = ldc("ssel", 128, 3, [(None, ssel_in[:])])
    sb_br = ldc("br", 128, 8, [(None, br_in[:])])
    sb_WrT = ldc("sWrT", 128, 64,
                 [((slice(None), slice(k * 8, k * 8 + 8)), WrT[k * 128:(k + 1) * 128, :])
                  for k in range(8)], BF16, qn="gpsimd")
    sb_ident = ldc("c_ident", 128, 128, [(None, ident_in[:])], BF16, qn="gpsimd")

    sb_A = ctx.sb("A", [128, 256], F32)
    sact(ctx, sb_A[:], sb_Alog[:], AF.Exp, reads=("Alog",), writes=("A",))
    ctx.op("scalar", lambda e: e.mul(sb_A[:], sb_A[:], -1.0), reads=("A",), writes=("A",))

    xn = [ctx.sb(f"l0xn{i}", [128, WS], BF16, side="right") for i in range(NDT)]
    xlo = [ctx.sb(f"l0lo{i}", [128, WS], BF16) for i in range(NDT)]
    xt = [ctx.sb(f"xs{i}", [128, WS], F32) for i in range(NDT)]
    for i in range(NDT):
        ctx.dma("sync", xt[i][:], x_in[i * 128:(i + 1) * 128, :], writes=(f"x{i}",))
    xkeys = [f"x{i}" for i in range(NDT)]
    xnkeys = [f"l0xn{i}" for i in range(NDT)]
    hkeys = [f"l0hm{i}" for i in range(NDT)]
    lokeys = [f"l0lo{i}" for i in range(NDT)]
    fm_norm(ctx, P, xt, xkeys, xn, xnkeys, 1e-5, True, None, None,
            lo=xlo, lokeys=lokeys)
    ctx.free([f"xs{i}" for i in range(NDT)])
    meq = ctx.sb("meq", [128, 16 * 8], F32, side="right")
    sWrL = ctx.sb("sWrL", [128, 64], BF16)
    WrTf = ctx.sb("WrTf", [128, 64], F32)
    for k in range(8):
        ctx.dma("sync", WrTf[:, k * 8:(k + 1) * 8], WrT[k * 128:(k + 1) * 128, :],
                writes=("WrTf",))
    vop(ctx, "tensor_sub", sWrL[:], WrTf[:], sb_WrT[:], reads=("WrTf", "sWrT"),
        writes=("sWrL",))
    lg = ctx.sb("lgt", [128, 8], F32)
    mx8 = ctx.sb("mx8", [128, 8], F32)
    for tt in range(16):
        parts = []
        for k in range(8):
            xh = xn[k][:, PAD + tt * 128:PAD + (tt + 1) * 128]
            xl = xlo[k][:, PAD + tt * 128:PAD + (tt + 1) * 128]
            parts.append((xh, sb_WrT[:, k * 8:(k + 1) * 8]))
            parts.append((xh, sWrL[:, k * 8:(k + 1) * 8]))
            parts.append((xl, sb_WrT[:, k * 8:(k + 1) * 8]))
        mm(ctx, P["psB"][:, 0:8], parts,
           reads=tuple(xnkeys) + tuple(lokeys) + ("sWrT", "sWrL"), writes=("psB0",))
        vop(ctx, "tensor_add", lg[:], P["psB"][0:128, 0:8], sb_br[:],
            reads=("psB0", "br"), writes=("lgt",))
        vop(ctx, "max", mx8[:], lg[:], reads=("lgt",), writes=("mx8",))
        vop(ctx, "tensor_scalar", meq[:, tt * 8:(tt + 1) * 8], lg[:], mx8[:, 0:1],
            None, ALU.is_equal, reads=("lgt", "mx8"), writes=("meq",))
    ctx.free(["lgt", "mx8", "sWrL", "WrTf"])
    ctx.free([f"l0lo{i}" for i in range(NDT)])
    hmx = [ctx.sb(f"l0hm{i}", [128, WS], BF16, side="right") for i in range(NDT)]
    fm_norm(ctx, P, xn, xnkeys, hmx, hkeys, 1e-6, False,
            [sb_rmsw[:, i:i + 1] for i in range(NDT)], "rmsw")

    if "xn" in dbg:
        o = ctx.dram_t("dbg_xn", [NDT * 128, WS], BF16, kind="ExternalOutput").ap()
        for i in range(NDT):
            ctx.dma("sync", o[i * 128:(i + 1) * 128, :], xn[i][:], reads=(xnkeys[i],))

    wbuf = [ctx.sb(f"wb{p}", [128, 8 * 1024], BF16) for p in range(2)]
    evac = [ctx.sb(f"ev{p}", [128, 512], F32) for p in range(2)]
    gbf = [ctx.sb(f"gbf{p}", [128, 512], BF16) for p in range(2)]
    CH = chunks(WS, 512)
    for mb in range(4):
        wk = f"wb{mb % 2}"
        for k in range(8):
            ctx.dma("gpsimd", wbuf[mb % 2][:, k * 1024:(k + 1) * 1024],
                    inW[k * 128:(k + 1) * 128, mb * 1024:(mb + 1) * 1024], writes=(wk,))
        for mi in range(8):
            m_ = mb * 8 + mi
            for ci, (c0, w) in enumerate(CH):
                pp = (mi * len(CH) + ci) % 2
                pk = ["psA0", "psA1"][pp]
                pv = [P["psA"][:, 0:512], P["psA"][:, 512:1024]][pp]
                ek = f"ev{pp}"
                mm(ctx, pv[:, 0:w],
                   [(wbuf[mb % 2][:, k * 1024 + mi * 128:k * 1024 + (mi + 1) * 128],
                     hmx[k][:, c0:c0 + w]) for k in range(8)],
                   reads=(wk,) + tuple(hkeys), writes=(pk,))
                if m_ < 16:
                    ctx.op("scalar", lambda e, pv=pv, pp=pp, w=w, m_=m_:
                           e.add(evac[pp][:, 0:w], pv[:, 0:w], sb_inb[:, m_:m_ + 1]),
                           reads=(pk, "inb"), writes=(ek,))
                    ctx.dma("sync", x1_dr[m_, :, c0:c0 + w], evac[pp][:, 0:w],
                            reads=(ek,), writes=("x1_dr",))
                else:
                    sact(ctx, evac[pp][:, 0:w], pv[:, 0:w], AF.Silu,
                         bias=sb_inb[:, m_:m_ + 1], reads=(pk, "inb"), writes=(ek,))
                    gk = f"gbf{pp}"
                    vop(ctx, "tensor_copy", gbf[pp][:, 0:w], evac[pp][:, 0:w],
                        reads=(ek,), writes=(gk,))
                    lo = max(c0, PAD)
                    if lo < c0 + w:
                        ctx.dma("sync", gs_dr[m_ - 16, :, lo - PAD:c0 + w - PAD],
                                gbf[pp][:, lo - c0:w], reads=(gk,), writes=("gs_dr",))
    ctx.free(["wb0", "wb1", "ev0", "ev1", "gbf0", "gbf1"])
    ctx.free([f"l0hm{i}" for i in range(NDT)])

    cx = ctx.sb("cx", [128, WS], F32)
    cacc = ctx.sb("cacc", [128, TLOC], F32)
    cs = ctx.sb("cs", [128, TLOC], F32)
    csb = ctx.sb("csb", [128, TLOC], BF16)
    for i in range(16):
        ctx.dma("sync", cx[:], x1_dr[i, :, :], reads=("x1_dr",), writes=("cx",))
        vop(ctx, "tensor_scalar", cacc[:], cx[:, 5:5 + TLOC],
            sb_convw[:, i * 4:i * 4 + 1], None, ALU.mult,
            reads=("cx", "convw"), writes=("cacc",))
        for kk in range(1, 4):
            vop(ctx, "scalar_tensor_tensor", cacc[:], cx[:, 5 + kk:5 + kk + TLOC],
                sb_convw[:, i * 4 + kk:i * 4 + kk + 1], cacc[:], ALU.mult, ALU.add,
                reads=("cx", "convw", "cacc"), writes=("cacc",))
        sact(ctx, cs[:], cacc[:], AF.Silu, bias=sb_convb[:, i:i + 1],
             reads=("cacc", "convb"), writes=("cs",))
        vop(ctx, "tensor_copy", csb[:], cs[:], reads=("cs",), writes=("csb",),
            eng="gpsimd")
        ctx.dma("sync", x1s_dr[i, :, :], cs[:], reads=("cs",), writes=("x1s_dr",))
        ctx.dma("sync", x1sb_dr[i, :, :], csb[:], reads=("csb",), writes=("x1sb_dr",))
    ctx.free(["cx", "cacc", "cs", "csb"])

    sb_dbcW = ctx.sb("dbcWs", [128, 16 * 160], BF16)
    for k in range(16):
        ctx.dma("gpsimd", sb_dbcW[:, k * 160:(k + 1) * 160],
                dbcW[k * 128:(k + 1) * 128, :], writes=("dbcWs",))
    Bsb = ctx.sb("Bsb", [16, TLOC], BF16, side="right")
    Csb = ctx.sb("Csb", [16, TLOC], BF16, side="right")
    dpre = ctx.sb("dpre", [128, TLOC], BF16, side="right")
    xall = ctx.sb("xall", [128, 16 * 512], BF16)
    for ci, (c0, w) in enumerate(chunks(TLOC, 512)):
        for k in range(16):
            ctx.dma("sync", xall[:, k * 512:k * 512 + w], x1sb_dr[k, :, c0:c0 + w],
                    reads=("x1sb_dr",), writes=("xall",))
        mm(ctx, P["psA"][:, 0:w],
           [(sb_dbcW[:, k * 160:k * 160 + 128], xall[:, k * 512:k * 512 + w])
            for k in range(16)], reads=("dbcWs", "xall"), writes=("psA0",))
        ctx.op("scalar", lambda e, w=w, c0=c0:
               e.add(dpre[:, c0:c0 + w], P["psA"][:, 0:w], sb_dbcb[:, 0:1]),
               reads=("psA0", "dbcb"), writes=("dpre",))
        mm(ctx, P["psB"][0:16, 0:w],
           [(sb_dbcW[:, k * 160 + 128:k * 160 + 144], xall[:, k * 512:k * 512 + w])
            for k in range(16)], reads=("dbcWs", "xall"), writes=("psB0",))
        ctx.op("scalar", lambda e, w=w, c0=c0:
               e.add(Bsb[:, c0:c0 + w], P["psB"][0:16, 0:w], sb_bcbB[:, 0:1]),
               reads=("psB0", "bcbB"), writes=("Bsb",))
        mm(ctx, P["psB"][0:16, 512:512 + w],
           [(sb_dbcW[:, k * 160 + 144:k * 160 + 160], xall[:, k * 512:k * 512 + w])
            for k in range(16)], reads=("dbcWs", "xall"), writes=("psB1",))
        ctx.op("scalar", lambda e, w=w, c0=c0:
               e.add(Csb[:, c0:c0 + w], P["psB"][0:16, 512:512 + w], sb_bcbC[:, 0:1]),
               reads=("psB1", "bcbC"), writes=("Csb",))
    ctx.free(["dbcWs", "xall"])

    sb_dupW = ctx.sb("dupWs", [128, 2048], BF16)
    ctx.dma("gpsimd", sb_dupW[:], dupW[:], writes=("dupWs",))
    tt_ = ctx.sb("spt", [128, 512], F32)
    ab_ = ctx.sb("spa", [128, 512], F32)
    ee_ = ctx.sb("spe", [128, 512], F32)
    rr_ = ctx.sb("spr", [128, 512], F32)
    dd_ = ctx.sb("spd", [128, 512], F32)
    for m_ in range(16):
        for ci, (c0, w) in enumerate(chunks(TLOC, 512)):
            mm(ctx, P["psA"][:, 0:w],
               [(sb_dupW[:, m_ * 128:(m_ + 1) * 128], dpre[:, c0:c0 + w])],
               reads=("dupWs", "dpre"), writes=("psA0",))
            vop(ctx, "tensor_scalar_add", tt_[:, 0:w], P["psA"][:, 0:w],
                sb_dupb[:, m_:m_ + 1], reads=("psA0", "dupb"), writes=("spt",))
            sact(ctx, ab_[:, 0:w], tt_[:, 0:w], AF.Abs,
                 reads=("spt",), writes=("spa",))
            sact(ctx, ee_[:, 0:w], ab_[:, 0:w], AF.Exp, scale=-1.0,
                 reads=("spa",), writes=("spe",))
            sact(ctx, ee_[:, 0:w], ee_[:, 0:w], AF.Ln, bias=1.0,
                 reads=("spe",), writes=("spe",))
            sact(ctx, rr_[:, 0:w], tt_[:, 0:w], AF.Relu,
                 reads=("spt",), writes=("spr",))
            vop(ctx, "tensor_add", dd_[:, 0:w], ee_[:, 0:w], rr_[:, 0:w],
                eng="gpsimd", reads=("spe", "spr"), writes=("spd",))
            ctx.dma("sync", delta_dr[m_, :, c0:c0 + w], dd_[:, 0:w],
                    reads=("spd",), writes=("delta_dr",))
    ctx.free(["dupWs", "spt", "spa", "spe", "spr", "spd"])
    ctx.free(["dpre"])

    HW = 1024
    bcB = ctx.sb("bcB", [128, 16 * HW], BF16)
    bcC = ctx.sb("bcC", [128, 16 * HW], BF16)
    sstate = ctx.sb("sstate", [128, 256], F32)
    cstate = ctx.sb("cstate", [128, 16], F32)
    sdh = ctx.sb("sdh", [128, 2], F32)
    delt = ctx.sb("delt", [128, HW], F32)
    x1sf = ctx.sb("x1sf", [128, HW], F32)
    dxb = ctx.sb("dxb", [128, HW], BF16)
    a_s = ctx.sb("a_s", [128, HW], F32)
    b_s = ctx.sb("b_s", [128, HW], BF16)
    hs = ctx.sb("hs", [128, HW], F32)
    ya = ctx.sb("ya", [128, HW], F32)
    ymt = ctx.sb("ymt", [128, HW], F32)
    fin = ctx.sb("fin", [128, 16], F32)
    cpk = ctx.sb("cpk", [128, 16], F32)

    def build_bc(half, which_b, which_c):
        for n in range(16):
            for sub in range(2):
                c0 = half * HW + sub * 512
                if which_b:
                    mm(ctx, P["psA"][:, sub * 512:sub * 512 + 512],
                       [(sb_seln[:, n * 128:(n + 1) * 128], Bsb[:, c0:c0 + 512])],
                       reads=("c_seln", "Bsb"), writes=(f"psA{sub}",))
                if which_c:
                    mm(ctx, P["psA"][:, 1024 + sub * 512:1024 + sub * 512 + 512],
                       [(sb_seln[:, n * 128:(n + 1) * 128], Csb[:, c0:c0 + 512])],
                       reads=("c_seln", "Csb"), writes=(f"psA{2 + sub}",))
            if which_b:
                vop(ctx, "tensor_copy", bcB[:, n * HW:(n + 1) * HW],
                    P["psA"][:, 0:HW], reads=("psA0", "psA1"), writes=("bcB",))
            if which_c:
                vop(ctx, "tensor_copy", bcC[:, n * HW:(n + 1) * HW],
                    P["psA"][:, 1024:1024 + HW], reads=("psA2", "psA3"), writes=("bcC",))

    for half in range(2):
        build_bc(half, True, True)
        for dt in range(16):
            ctx.dma("sync", delt[:], delta_dr[dt, :, half * HW:(half + 1) * HW],
                    reads=("delta_dr",), writes=("delt",))
            ctx.dma("sync", x1sf[:], x1s_dr[dt, :, half * HW:(half + 1) * HW],
                    reads=("x1s_dr",), writes=("x1sf",))
            vop(ctx, "tensor_mul", dxb[:], delt[:], x1sf[:],
                reads=("delt", "x1sf"), writes=("dxb",))
            vop(ctx, "reduce_sum", sdh[:, half:half + 1], delt[:], AX.X,
                reads=("delt",), writes=("sdh",))
            for n in range(16):
                sact(ctx, a_s[:], delt[:], AF.Exp,
                     scale=sb_A[:, dt * 16 + n:dt * 16 + n + 1],
                     reads=("delt", "A"), writes=("a_s",))
                vop(ctx, "tensor_mul", b_s[:], dxb[:], bcB[:, n * HW:(n + 1) * HW],
                    eng="gpsimd", reads=("dxb", "bcB"), writes=("b_s",))
                ini = 0.0 if half == 0 else sstate[:, dt * 16 + n:dt * 16 + n + 1]
                rd = ("a_s", "b_s") if half == 0 else ("a_s", "b_s", "sstate")
                vop(ctx, "tensor_tensor_scan", hs[:], a_s[:], b_s[:], ini,
                    ALU.mult, ALU.add, reads=rd, writes=("hs",))
                ctx.op("scalar", lambda e, dt=dt, n=n:
                       e.copy(sstate[:, dt * 16 + n:dt * 16 + n + 1],
                              hs[:, HW - 1:HW]),
                       reads=("hs",), writes=("sstate",))
                vop(ctx, "tensor_mul", ymt[:], hs[:], bcC[:, n * HW:(n + 1) * HW],
                    reads=("hs", "bcC"), writes=("ymt",))
                if n == 0:
                    vop(ctx, "tensor_copy", ya[:], ymt[:], eng="gpsimd",
                        reads=("ymt",), writes=("ya",))
                else:
                    vop(ctx, "tensor_add", ya[:], ya[:], ymt[:], eng="gpsimd",
                        reads=("ymt", "ya"), writes=("ya",))
            ctx.dma("sync", y1_dr[dt, :, half * HW:(half + 1) * HW], ya[:],
                    reads=("ya",), writes=("y1_dr",))
            if half == 1:
                vop(ctx, "tensor_add", sdh[:, 0:1], sdh[:, 0:1], sdh[:, 1:2],
                    reads=("sdh",), writes=("sdh",))
                for n in range(16):
                    sact(ctx, cpk[:, n:n + 1], sdh[:, 0:1], AF.Exp,
                         scale=sb_A[:, dt * 16 + n:dt * 16 + n + 1],
                         reads=("sdh", "A"), writes=("cpk",))
                vop(ctx, "tensor_copy", fin[:],
                    sstate[:, dt * 16:(dt + 1) * 16], reads=("sstate",),
                    writes=("fin",))
                ctx.dma("sync", cc_in[:, dt * 32:dt * 32 + 16], fin[:],
                        reads=("fin",), writes=("cc_in",))
                ctx.dma("sync", cc_in[:, dt * 32 + 16:dt * 32 + 32], cpk[:],
                        reads=("cpk",), writes=("cc_in",))

    ctx.collective("AllGather", ALU.bypass, GRP, cc_in, cc_out,
                   reads=("cc_in",), writes=("cc_out",))
    h0 = ctx.sb("h0", [128, 16 * 16], F32)
    s1_ = ctx.sb("cs1", [128, 16], F32)
    s2_ = ctx.sb("cs2", [128, 16], F32)
    s3_ = ctx.sb("cs3", [128, 16], F32)
    fj = [ctx.sb(f"fj{j}", [128, 16], F32) for j in range(3)]
    pj = [ctx.sb(f"pj{j}", [128, 16], F32) for j in range(3)]
    for dt in range(16):
        for j in range(3):
            ctx.dma("sync", fj[j][:], cc_out[j * 128:(j + 1) * 128, dt * 32:dt * 32 + 16],
                    reads=("cc_out",), writes=(f"fj{j}",))
            ctx.dma("sync", pj[j][:],
                    cc_out[j * 128:(j + 1) * 128, dt * 32 + 16:dt * 32 + 32],
                    reads=("cc_out",), writes=(f"pj{j}",))
        vop(ctx, "tensor_copy", s1_[:], fj[0][:], reads=("fj0",), writes=("cs1",))
        vop(ctx, "tensor_mul", s2_[:], pj[1][:], s1_[:], reads=("pj1", "cs1"),
            writes=("cs2",))
        vop(ctx, "tensor_add", s2_[:], s2_[:], fj[1][:], reads=("cs2", "fj1"),
            writes=("cs2",))
        vop(ctx, "tensor_mul", s3_[:], pj[2][:], s2_[:], reads=("pj2", "cs2"),
            writes=("cs3",))
        vop(ctx, "tensor_add", s3_[:], s3_[:], fj[2][:], reads=("cs3", "fj2"),
            writes=("cs3",))
        hsl = h0[:, dt * 16:(dt + 1) * 16]
        vop(ctx, "tensor_scalar", hsl, s1_[:], sb_ssel[:, 0:1], None, ALU.mult,
            reads=("cs1", "ssel"), writes=("h0",))
        vop(ctx, "scalar_tensor_tensor", hsl, s2_[:], sb_ssel[:, 1:2], hsl,
            ALU.mult, ALU.add, reads=("cs2", "ssel", "h0"), writes=("h0",))
        vop(ctx, "scalar_tensor_tensor", hsl, s3_[:], sb_ssel[:, 2:3], hsl,
            ALU.mult, ALU.add, reads=("cs3", "ssel", "h0"), writes=("h0",))
    ctx.free(["cs1", "cs2", "cs3", "fj0", "fj1", "fj2", "pj0", "pj1", "pj2"])

    cd = ctx.sb("cdt", [128, HW], F32)
    ec = ctx.sb("ec", [128, HW], F32)
    wcb = ctx.sb("wcb", [128, HW], F32)
    y2 = ctx.sb("y2", [128, HW], F32)
    zz = ctx.sb("zz", [128, HW], F32)
    vop(ctx, "memset", zz[:], 0.0, writes=("zz",))
    for half in range(2):
        build_bc(half, False, True)
        for dt in range(16):
            ctx.dma("sync", delt[:], delta_dr[dt, :, half * HW:(half + 1) * HW],
                    reads=("delta_dr",), writes=("delt",))
            ini = 0.0 if half == 0 else cstate[:, dt:dt + 1]
            rd = ("delt", "zz") if half == 0 else ("delt", "zz", "cstate")
            vop(ctx, "tensor_tensor_scan", cd[:], delt[:], zz[:], ini,
                ALU.add, ALU.add, reads=rd, writes=("cdt",))
            ctx.op("scalar", lambda e, dt=dt: e.copy(cstate[:, dt:dt + 1],
                                                     cd[:, HW - 1:HW]),
                   reads=("cdt",), writes=("cstate",))
            for n in range(16):
                sact(ctx, ec[:], cd[:], AF.Exp,
                     scale=sb_A[:, dt * 16 + n:dt * 16 + n + 1],
                     reads=("cdt", "A"), writes=("ec",))
                vop(ctx, "tensor_mul", wcb[:], ec[:], bcC[:, n * HW:(n + 1) * HW],
                    eng="gpsimd", reads=("ec", "bcC"), writes=("wcb",))
                if n == 0:
                    vop(ctx, "tensor_scalar", y2[:], wcb[:],
                        h0[:, dt * 16 + n:dt * 16 + n + 1], None, ALU.mult,
                        reads=("wcb", "h0"), writes=("y2",))
                else:
                    vop(ctx, "scalar_tensor_tensor", y2[:], wcb[:],
                        h0[:, dt * 16 + n:dt * 16 + n + 1], y2[:], ALU.mult, ALU.add,
                        reads=("wcb", "h0", "y2"), writes=("y2",))
            ctx.dma("sync", y2_dr[dt, :, half * HW:(half + 1) * HW], y2[:],
                    reads=("y2",), writes=("y2_dr",))
    ctx.free(["bcB", "bcC", "sstate", "cstate", "sdh", "delt", "x1sf", "dxb",
              "a_s", "b_s", "hs", "ya", "ymt", "fin", "cpk", "h0", "cdt", "ec",
              "wcb", "y2", "zz"])
    ctx.free(["dpre", "Csb", "Bsb"])

    sb_outW = ctx.sb("outWs", [128, 16 * 1024], BF16)
    for k in range(16):
        ctx.dma("gpsimd", sb_outW[:, k * 1024:(k + 1) * 1024],
                outW[k * 128:(k + 1) * 128, :], writes=("outWs",))
    ygb = ctx.sb("ygb", [128, 16 * 512], BF16)
    y1t = ctx.sb("y1t", [128, 512], F32)
    y2t = ctx.sb("y2t", [128, 512], F32)
    gst = ctx.sb("gst", [128, 512], BF16)
    ysum = ctx.sb("ysum", [128, 512], F32)
    mxev = [ctx.sb(f"mxev{p}", [128, 512], F32) for p in range(2)]
    for ci, (c0, w) in enumerate(chunks(TLOC, 512)):
        for k in range(16):
            ctx.dma("sync", y1t[:, 0:w], y1_dr[k, :, c0:c0 + w], reads=("y1_dr",),
                    writes=("y1t",))
            ctx.dma("sync", y2t[:, 0:w], y2_dr[k, :, c0:c0 + w], reads=("y2_dr",),
                    writes=("y2t",))
            ctx.dma("sync", gst[:, 0:w], gs_dr[k, :, c0:c0 + w], reads=("gs_dr",),
                    writes=("gst",))
            vop(ctx, "tensor_add", ysum[:, 0:w], y1t[:, 0:w], y2t[:, 0:w],
                reads=("y1t", "y2t"), writes=("ysum",))
            vop(ctx, "tensor_mul", ygb[:, k * 512:k * 512 + w], ysum[:, 0:w],
                gst[:, 0:w], reads=("ysum", "gst"), writes=("ygb",))
        for m_ in range(8):
            pp = m_ % 2
            pv = [P["psA"][:, 0:512], P["psA"][:, 512:1024]][pp]
            pk = ["psA0", "psA1"][pp]
            ek = f"mxev{pp}"
            mm(ctx, pv[:, 0:w],
               [(sb_outW[:, k * 1024 + m_ * 128:k * 1024 + (m_ + 1) * 128],
                 ygb[:, k * 512:k * 512 + w]) for k in range(16)],
               reads=("outWs", "ygb"), writes=(pk,))
            ctx.op("scalar", lambda e, pv=pv, pp=pp, w=w, m_=m_:
                   e.add(mxev[pp][:, 0:w], pv[:, 0:w], sb_outb[:, m_:m_ + 1]),
                   reads=(pk, "outb"), writes=(ek,))
            ctx.dma("sync", mix_dr[m_, :, c0:c0 + w], mxev[pp][:, 0:w],
                    reads=(ek,), writes=("mix_dr",))
    ctx.free(["outWs", "ygb", "y1t", "y2t", "gst", "ysum", "mxev0", "mxev1"])


    onesr = ctx.sb("onesr", [1, 512], BF16)
    moe = ctx.sb("moe", [128, 16 * 1024], BF16, side="right")
    yac = ctx.sb("yac", [128, 16 * 1024], BF16, side="right")
    b1row = ctx.sb("b1row", [1, 4096], BF16)
    b2row = ctx.sb("b2row", [1, 1024], BF16)

    for e_ in range(8):
        ctx.dma("gpsimd", b1row[:], b1_in[e_:e_ + 1, :], writes=("b1row",))
        ctx.dma("gpsimd", b2row[:], b2_in[e_:e_ + 1, :], writes=("b2row",))
        wbuf = [ctx.sb(f"wb{p}", [128, 8 * 1024], BF16) for p in range(2)]
        hst = [ctx.sb(f"hst{p}", [128, 512], BF16) for p in range(2)]
        for mb in range(4):
            wk = f"wb{mb % 2}"
            for k in range(8):
                ctx.dma("gpsimd", wbuf[mb % 2][:, k * 1024:(k + 1) * 1024],
                        W1T[e_, k * 128:(k + 1) * 128, mb * 1024:(mb + 1) * 1024],
                        writes=(wk,))
            for mi in range(8):
                m_ = mb * 8 + mi
                for ci, (c0, w) in enumerate(chunks(TLOC, 512)):
                    pp = (mi * 4 + ci) % 2
                    pk = ["psA0", "psA1"][pp]
                    pv = [P["psA"][:, 0:512], P["psA"][:, 512:1024]][pp]
                    parts = [(wbuf[mb % 2][:, k * 1024 + mi * 128:k * 1024 + (mi + 1) * 128],
                              xn[k][:, PAD + c0:PAD + c0 + w]) for k in range(8)]
                    parts.append((b1row[:, m_ * 128:(m_ + 1) * 128], onesr[:, 0:w]))
                    mm(ctx, pv[:, 0:w], parts,
                       reads=(wk,) + tuple(xnkeys) + ("onesr", "b1row"), writes=(pk,))
                    hk = f"hst{pp}"
                    sact(ctx, hst[pp][:, 0:w], pv[:, 0:w], AF.Relu,
                         reads=(pk,), writes=(hk,))
                    ctx.dma("sync", h_dr[m_, :, c0:c0 + w], hst[pp][:, 0:w],
                            reads=(hk,), writes=("h_dr",))
        ctx.free(["wb0", "wb1", "hst0", "hst1"])
        hgb = ctx.sb("hgb", [128, 8 * TLOC], BF16)
        w2b = ctx.sb("w2b", [128, 8 * 1024], BF16)
        for g_ in range(4):
            for j in range(8):
                ctx.dma("gpsimd", w2b[:, j * 1024:(j + 1) * 1024],
                        W2T[e_, (g_ * 8 + j) * 128:(g_ * 8 + j + 1) * 128, :],
                        writes=("w2b",))
                ctx.dma("sync", hgb[:, j * TLOC:(j + 1) * TLOC],
                        h_dr[g_ * 8 + j, :, :], reads=("h_dr",), writes=("hgb",))
            for tt in range(16):
                for hf in range(2):
                    pp = (tt * 2 + hf) % 2
                    pk = ["psA0", "psA1"][pp]
                    pv = [P["psA"][:, 0:512], P["psA"][:, 512:1024]][pp]
                    parts = [(hgb[:, j * TLOC + tt * 128:j * TLOC + (tt + 1) * 128],
                              w2b[:, j * 1024 + hf * 512:j * 1024 + (hf + 1) * 512])
                             for j in range(8)]
                    if g_ == 3:
                        parts.append((onesr[:, 0:128],
                                      b2row[:, hf * 512:(hf + 1) * 512]))
                    mm(ctx, pv[:, 0:512], parts,
                       reads=("w2b", "hgb", "onesr", "b2row"), writes=(pk,))
                    ys = yac[:, tt * 1024 + hf * 512:tt * 1024 + (hf + 1) * 512]
                    if g_ == 0:
                        vop(ctx, "tensor_copy", ys, pv[:, 0:512],
                            reads=(pk,), writes=("yac",))
                    else:
                        vop(ctx, "tensor_add", ys, ys, pv[:, 0:512],
                            reads=(pk, "yac"), writes=("yac",))
        ctx.free(["hgb", "w2b"])
        for tt in range(16):
            msl = moe[:, tt * 1024:(tt + 1) * 1024]
            ysl = yac[:, tt * 1024:(tt + 1) * 1024]
            if e_ == 0:
                vop(ctx, "tensor_scalar", msl, ysl, meq[:, tt * 8 + e_:tt * 8 + e_ + 1],
                    None, ALU.mult, reads=("yac", "meq"), writes=("moe",))
            else:
                vop(ctx, "scalar_tensor_tensor", msl, ysl,
                    meq[:, tt * 8 + e_:tt * 8 + e_ + 1], msl, ALU.mult, ALU.add,
                    reads=("yac", "meq", "moe"), writes=("moe",))
    if "dbg2" not in dbg:
        ctx.free(["b1row", "b2row"])
        ctx.free(["yac"])

    if "dbg2" in dbg:
        o1 = ctx.dram_t("dbg_mix", [8, 128, TLOC], F32, kind="ExternalOutput").ap()
        ctx.dma("sync", o1[:, :, :], mix_dr[:, :, :], reads=("mix_dr",))
        o2 = ctx.dram_t("dbg_y1", [16, 128, TLOC], F32, kind="ExternalOutput").ap()
        ctx.dma("sync", o2[:, :, :], y1_dr[:, :, :], reads=("y1_dr",))
        o3 = ctx.dram_t("dbg_y2", [16, 128, TLOC], F32, kind="ExternalOutput").ap()
        ctx.dma("sync", o3[:, :, :], y2_dr[:, :, :], reads=("y2_dr",))
        o4 = ctx.dram_t("dbg_delta", [16, 128, TLOC], F32, kind="ExternalOutput").ap()
        ctx.dma("sync", o4[:, :, :], delta_dr[:, :, :], reads=("delta_dr",))
        o5 = ctx.dram_t("dbg_x1s", [16, 128, TLOC], F32, kind="ExternalOutput").ap()
        ctx.dma("sync", o5[:, :, :], x1s_dr[:, :, :], reads=("x1s_dr",))
        o6 = ctx.dram_t("dbg_gs", [16, 128, TLOC], BF16, kind="ExternalOutput").ap()
        ctx.dma("sync", o6[:, :, :], gs_dr[:, :, :], reads=("gs_dr",))
        o7 = ctx.dram_t("dbg_moe", [128, 16 * 1024], BF16, kind="ExternalOutput").ap()
        ctx.dma("sync", o7[:, :], moe[:], reads=("moe",))
        o8 = ctx.dram_t("dbg_meq", [128, 128], F32, kind="ExternalOutput").ap()
        ctx.dma("sync", o8[:, :], meq[:], reads=("meq",))
        o9 = ctx.dram_t("dbg_yac", [128, 16 * 1024], BF16, kind="ExternalOutput").ap()
        ctx.dma("sync", o9[:, :], yac[:], reads=("yac",))
        o10 = ctx.dram_t("dbg_hdr", [32, 128, TLOC], BF16, kind="ExternalOutput").ap()
        ctx.dma("sync", o10[:, :, :], h_dr[:, :, :], reads=("h_dr",))
    xo = [ctx.sb(f"xo{i}", [128, TLOC], F32) for i in range(NDT)]
    mixt = ctx.sb("mixt", [128, TLOC], F32)
    for dm in range(NDT):
        ctx.dma("sync", xo[dm][:], x_in[dm * 128:(dm + 1) * 128, PAD:WS],
                writes=(f"xo{dm}",))
        ctx.dma("sync", mixt[:], mix_dr[dm, :, :], reads=("mix_dr",), writes=("mixt",))
        vop(ctx, "tensor_add", xo[dm][:], xo[dm][:], mixt[:],
            reads=("mixt", f"xo{dm}"), writes=(f"xo{dm}",))
        vop(ctx, "tensor_add", xo[dm][:], xo[dm][:], xn[dm][:, PAD:WS],
            reads=(f"l0xn{dm}", f"xo{dm}"), writes=(f"xo{dm}",))
        for tt in range(16):
            ctx.op("tensor", lambda t, tt=tt, dm=dm:
                   t.transpose(P["psT"][:, 0:128],
                               moe[:, tt * 1024 + dm * 128:tt * 1024 + (dm + 1) * 128],
                               sb_ident[:]),
                   reads=("moe", "c_ident"), writes=("psT",))
            vop(ctx, "tensor_add", xo[dm][:, tt * 128:(tt + 1) * 128],
                xo[dm][:, tt * 128:(tt + 1) * 128], P["psT"][:, 0:128],
                reads=("psT", f"xo{dm}"), writes=(f"xo{dm}",))
    if "dbg2" not in dbg:
        ctx.free(["moe", "meq"] + [f"l0xn{i}" for i in range(NDT)])
    else:
        ctx.free(["yac", "moe", "meq"] + [f"l0xn{i}" for i in range(NDT)])
    return xo



def build_block_program(block_id, dbg=(), params=None):
    params = params or {}
    nc = bass.Bass("TRN2", target_bir_lowering=False)
    ctx = Ctx(nc)
    P = {
        "psA": ctx.psum_t("psA", [128, 2048], F32),
        "psB": ctx.psum_t("psB", [128, 1536], F32),
        "psT": ctx.psum_t("psT", [128, 512], BF16),
    }
    win = WS if block_id == 0 else TLOC
    x_in = ctx.dram_t("x_fm", [D, win], F32, kind="ExternalInput").ap()
    x_out = ctx.dram_t("x_out", [D, TLOC], F32, kind="ExternalOutput").ap()
    with nc.Block() as block:
        ones_col = ctx.sb("ones_col", [128, 1], BF16)
        ones_colf = ctx.sb("ones_colf", [128, 1], F32)
        ones1 = ctx.sb("ones1", [1, 128], BF16)
        onesr = ctx.sb("onesr", [1, 512], BF16)
        vop(ctx, "memset", ones_col[:], 1.0, writes=("ones_col",))
        vop(ctx, "memset", ones_colf[:], 1.0, writes=("ones_colf",))
        vop(ctx, "memset", ones1[:], 1.0, writes=("ones1",))
        vop(ctx, "memset", onesr[:], 1.0, writes=("onesr",))
        if block_id == 0:
            xo = emit_l0(ctx, P, x_in, dbg=dbg)
            okeys = [f"xo{i}" for i in range(NDT)]
        else:
            xt = [ctx.sb(f"xs{i}", [128, TLOC], F32) for i in range(NDT)]
            if block_id not in (32,):
                for i in range(NDT):
                    ctx.dma("sync", xt[i][:], x_in[i * 128:(i + 1) * 128, :],
                            writes=(f"x{i}",))
            if block_id == 1:
                xo = emit_l1(ctx, P, xt, params.get("l1_R", 8))
            elif block_id == 2:
                xo = emit_l2(ctx, P, xt)
            elif block_id == 3:
                xo = emit_l3(ctx, P, xt, dbg=dbg)
            elif block_id == 31:
                xo = emit_l3(ctx, P, xt, parts=("attn",), dbg=dbg)
            elif block_id == 32:
                xacc = ctx.dram_t("xacc_fm", [D, TLOC], F32,
                                  kind="ExternalInput").ap()
                for i in range(NDT):
                    ctx.dma("sync", xt[i][:], x_in[i * 128:(i + 1) * 128, :],
                            writes=(f"x{i}",))
                xo = emit_l3(ctx, P, xt, parts=("ffn",), reload=xacc, dbg=dbg)
            okeys = [f"x{i}" for i in range(NDT)]
        for i in range(NDT):
            ctx.dma("sync", x_out[i * 128:(i + 1) * 128, :], xo[i][:, 0:TLOC],
                    reads=(okeys[i],), writes=("x_out",))
        ctx.drain_all()
        ctx.finalize(block)
    ctx.es.close()
    return nc


def run_block(block_id, per_core, shared, dbg=(), params=None):
    nc = build_block_program(block_id, dbg=dbg, params=params)
    names = {a.memorylocations[0].name for a in nc.m.functions[0].allocations
             if isinstance(a, mybir.MemoryLocationSet)
             and a.kind == "ExternalInput"}
    names.discard("partition_id")
    in_maps = []
    for c in range(NCORES):
        m = {}
        for k in names:
            if k in per_core[c]:
                m[k] = per_core[c][k]
            elif k in shared:
                m[k] = shared[k]
            else:
                raise KeyError(f"missing input {k}")
        in_maps.append(m)
    res = run_bass_kernel_spmd(nc, in_maps, core_ids=list(range(NCORES)))
    return res.results


def kernel(**inputs):
    per_core, shared = host_prep(inputs)
    results = run_block(0, per_core, shared)
    for blk in (1, 2, 3):
        for c in range(NCORES):
            per_core[c]["x_fm"] = np.ascontiguousarray(results[c]["x_out"])
        params = {"l1_R": per_core[0]["l1_rix"].shape[0]} if blk == 1 else None
        results = run_block(blk, per_core, shared, params=params)
    out = np.zeros((B, T, D), np.float32)
    for c in range(NCORES):
        b, r = c // 4, c % 4
        out[b, r * TLOC:(r + 1) * TLOC, :] = results[c]["x_out"].T
    return out



def emit_l2(ctx, P, xt):
    AI = lambda name, shape, dt=F32: ctx.dram_t(name, shape, dt, kind="ExternalInput").ap()
    lnw = AI("l2_lnw", [D, 1])
    lnb = AI("l2_lnb", [D, 1])
    Wq = AI("l2_Wq_my", [1024, 256])
    Wk = AI("l2_Wk_my", [1024, 256])
    Wv = AI("l2_Wv_my", [1024, 512])
    Wg = AI("l2_Wg_my", [1024, 512])
    Wo = AI("l2_Wo_my", [512, 1024])
    gnw = AI("l2_gnw", [1, 512])
    DmT = AI("l2_DmaskT", [64, 64])
    crT = AI("l2_crossT", [128, 64])
    kde = AI("l2_kdec", [64, 1])
    gC = AI("l2_gC", [128, 1])
    cosT = AI("l2_cosT", [128, T])
    sinT = AI("l2_sinT", [128, T])
    w1 = AI("l2_w1", [4096, 1024])
    w1T = AI("l2_w1T", [1024, 4096])
    w2 = AI("l2_w2", [1024, 4096])
    w2T = AI("l2_w2T", [4096, 1024])
    b1r = AI("l2_b1r", [1, 4096])
    b2 = AI("l2_b2", [D, 1])
    idin = ctx.dram["ident"].ap() if "ident" in ctx.dram else AI("ident", [128, 128])

    ccx_in = ctx.dram_t("l2_ccx_in", [1024, TLOC], BF16).ap()
    ccx_out = ctx.dram_t("l2_ccx_out", [4096, TLOC], BF16).ap()
    q_dr = ctx.dram_t("l2_q_dr", [256, T], BF16).ap()
    k_dr = ctx.dram_t("l2_k_dr", [256, T], BF16).ap()
    v_dr = ctx.dram_t("l2_v_dr", [T, 512], BF16).ap()
    g_dr = ctx.dram_t("l2_g_dr", [T, 512], BF16).ap()
    wop_in = ctx.dram_t("l2_wop_in", [T, 1024], BF16).ap()
    wop_out = ctx.dram_t("l2_wop_out", [TLOC, 1024], BF16).ap()
    bw1_dr = ctx.dram_t("l2_bw1_dr", [1024, 4096], BF16).ap()
    bw2_dr = ctx.dram_t("l2_bw2_dr", [4096, 1024], BF16).ap()
    h2_dr = ctx.dram_t("l2_h2_dr", [32, 128, TLOC], BF16).ap()

    def ldc(name, rows, cols, src_slices, dt=F32, qn="sync"):
        tl = ctx.sb(name, [rows, cols], dt)
        for (dst_sl, src_ap) in src_slices:
            ctx.dma(qn, tl[dst_sl] if dst_sl else tl[:], src_ap, writes=(name,))
        return tl

    def col_slices(src, n, per=1):
        return [((slice(None), slice(i * per, (i + 1) * per)),
                 src[i * 128:(i + 1) * 128, :]) for i in range(n)]

    sb_lnw = ldc("l2lnw", 128, NDT, col_slices(lnw, NDT))
    sb_lnb = ldc("l2lnb", 128, NDT, col_slices(lnb, NDT))
    sb_DmT = ldc("l2DmT", 64, 64, [(None, DmT[:])], BF16, qn="gpsimd")
    sb_crT = ldc("l2crT", 128, 64, [(None, crT[:])], BF16, qn="gpsimd")
    sb_kde = ldc("l2kde", 64, 1, [(None, kde[:])])
    sb_gC = ldc("l2gC", 128, 1, [(None, gC[:])])
    sb_gnwr = ldc("l2gnwr", 1, 512, [(None, gnw[:])], BF16, qn="gpsimd")
    sb_b2 = ldc("l2b2", 128, 8, col_slices(b2, 8))
    sb_id2 = ldc("l2id", 128, 128, [(None, idin[:])], BF16, qn="gpsimd")
    ones1 = ctx.sb("ones1", [1, 128], BF16)
    onesr = ctx.sb("onesr", [1, 512], BF16)

    gnw_bc = ctx.sb("l2gnbc", [128, 512], BF16)
    mm(ctx, P["psB"][:, 0:512], [(ones1[:], sb_gnwr[:, 0:512])],
       reads=("l2gnwr", "ones1"), writes=("psB0",))
    vop(ctx, "tensor_copy", gnw_bc[:], P["psB"][:, 0:512], reads=("psB0",),
        writes=("l2gnbc",))

    xn2 = [ctx.sb(f"l2xn{i}", [128, TLOC], BF16, side="right") for i in range(NDT)]
    xn2k = [f"l2xn{i}" for i in range(NDT)]
    fm_norm(ctx, P, xt, [f"x{i}" for i in range(NDT)], xn2, xn2k, 1e-5, True,
            None, None, width=TLOC)
    nrm = [ctx.sb(f"l2nr{i}", [128, TLOC], BF16) for i in range(NDT)]
    nrmk = [f"l2nr{i}" for i in range(NDT)]
    fm_norm(ctx, P, xn2, xn2k, nrm, nrmk, 1e-5, True, None, None, width=TLOC,
            aff=([sb_lnw[:, i:i + 1] for i in range(NDT)],
                 [sb_lnb[:, i:i + 1] for i in range(NDT)]), affkey="l2lnw",
            src_f32=False)
    for i in range(NDT):
        ctx.dma("sync", ccx_in[i * 128:(i + 1) * 128, :], nrm[i][:],
                reads=(nrmk[i],), writes=("ccx_in",))
    ctx.collective("AllGather", ALU.bypass, GRP, ccx_in, ccx_out,
                   reads=("ccx_in",), writes=("ccx_out",))
    ctx.free(nrmk)

    sb_wq = ldc("l2wq", 128, 8 * 256, [((slice(None), slice(k * 256, (k + 1) * 256)),
                Wq[k * 128:(k + 1) * 128, :]) for k in range(8)], BF16, qn="gpsimd")
    sb_wk = ldc("l2wk", 128, 8 * 256, [((slice(None), slice(k * 256, (k + 1) * 256)),
                Wk[k * 128:(k + 1) * 128, :]) for k in range(8)], BF16, qn="gpsimd")
    sb_wv = ldc("l2wv", 128, 8 * 512, [((slice(None), slice(k * 512, (k + 1) * 512)),
                Wv[k * 128:(k + 1) * 128, :]) for k in range(8)], BF16, qn="gpsimd")
    sb_wg = ldc("l2wg", 128, 8 * 512, [((slice(None), slice(k * 512, (k + 1) * 512)),
                Wg[k * 128:(k + 1) * 128, :]) for k in range(8)], BF16, qn="gpsimd")
    nrm_sb = ctx.sb("l2nrmsb", [128, 8 * 512], BF16)
    cosb = ctx.sb("l2cos", [128, 512], BF16)
    sinb = ctx.sb("l2sin", [128, 512], BF16)
    qa = ctx.sb("l2qa", [128, 512], F32)
    qb = ctx.sb("l2qb", [128, 512], F32)
    t1 = ctx.sb("l2t1", [128, 512], F32)
    t2 = ctx.sb("l2t2", [128, 512], F32)
    ro = ctx.sb("l2ro", [128, 512], BF16)
    vev = ctx.sb("l2vev", [128, 512], BF16)
    for tb in range(4):
        for ci in range(4):
            c0 = ci * 512
            gcol = tb * TLOC + c0
            for k in range(8):
                ctx.dma("sync", nrm_sb[:, k * 512:(k + 1) * 512],
                        ccx_out[tb * 1024 + k * 128:tb * 1024 + (k + 1) * 128, c0:c0 + 512],
                        reads=("ccx_out",), writes=("l2nrmsb",))
            ctx.dma("gpsimd", cosb[:], cosT[:, gcol:gcol + 512], writes=("l2cos",))
            ctx.dma("gpsimd", sinb[:], sinT[:, gcol:gcol + 512], writes=("l2sin",))
            for (wsb, wk_, dst, dk_, scl) in ((sb_wq, "l2wq", q_dr, "q_dr", 0.0625),
                                              (sb_wk, "l2wk", k_dr, "k_dr", 1.0)):
                for mh in range(2):
                    pv = [P["psA"][:, 0:512], P["psA"][:, 512:1024]][mh]
                    pk = ["psA0", "psA1"][mh]
                    mm(ctx, pv, [(wsb[:, k * 256 + mh * 128:k * 256 + (mh + 1) * 128],
                                  nrm_sb[:, k * 512:(k + 1) * 512]) for k in range(8)],
                       reads=(wk_, "l2nrmsb"), writes=(pk,))
                ctx.op("scalar", lambda e, scl=scl: e.mul(qa[:], P["psA"][:, 0:512], scl),
                       reads=("psA0",), writes=("l2qa",))
                ctx.op("scalar", lambda e, scl=scl: e.mul(qb[:], P["psA"][:, 512:1024], scl),
                       reads=("psA1",), writes=("l2qb",))
                vop(ctx, "tensor_mul", t1[:], qa[:], cosb[:], reads=("l2qa", "l2cos"),
                    writes=("l2t1",))
                vop(ctx, "tensor_mul", t2[:], qb[:], sinb[:], reads=("l2qb", "l2sin"),
                    writes=("l2t2",))
                vop(ctx, "tensor_sub", ro[:], t1[:], t2[:], reads=("l2t1", "l2t2"),
                    writes=("l2ro",))
                ctx.dma("sync", dst[0:128, gcol:gcol + 512], ro[:], reads=("l2ro",),
                        writes=(dk_,))
                vop(ctx, "tensor_mul", t1[:], qb[:], cosb[:], reads=("l2qb", "l2cos"),
                    writes=("l2t1",))
                vop(ctx, "tensor_mul", t2[:], qa[:], sinb[:], reads=("l2qa", "l2sin"),
                    writes=("l2t2",))
                vop(ctx, "tensor_add", ro[:], t1[:], t2[:], reads=("l2t1", "l2t2"),
                    writes=("l2ro",))
                ctx.dma("sync", dst[128:256, gcol:gcol + 512], ro[:], reads=("l2ro",),
                        writes=(dk_,))
            for (wsb, wk_, dst, dk_) in ((sb_wv, "l2wv", v_dr, "v_dr"),
                                         (sb_wg, "l2wg", g_dr, "g_dr")):
                for ts in range(4):
                    mm(ctx, P["psA"][:, 0:512],
                       [(nrm_sb[:, k * 512 + ts * 128:k * 512 + (ts + 1) * 128],
                         wsb[:, k * 512:(k + 1) * 512]) for k in range(8)],
                       reads=(wk_, "l2nrmsb"), writes=("psA0",))
                    vop(ctx, "tensor_copy", vev[:], P["psA"][:, 0:512],
                        reads=("psA0",), writes=("l2vev",))
                    ctx.dma("sync", dst[gcol + ts * 128:gcol + (ts + 1) * 128, :],
                            vev[:], reads=("l2vev",), writes=(dk_,))
    ctx.free(["l2wq", "l2wk", "l2wv", "l2wg", "l2nrmsb", "l2cos", "l2sin",
              "l2qa", "l2qb", "l2t1", "l2t2", "l2ro", "l2vev"])

    wop_sb = ctx.sb("l2wop", [128, 4 * 1024], BF16)
    for k in range(4):
        ctx.dma("gpsimd", wop_sb[:, k * 1024:(k + 1) * 1024],
                Wo[k * 128:(k + 1) * 128, :], writes=("l2wop",))
    S_sb = [ctx.sb(f"l2S{i}", [128, 512], BF16) for i in range(2)]
    vop(ctx, "memset", S_sb[0][:], 0.0, writes=("l2S0",))
    vop(ctx, "memset", S_sb[1][:], 0.0, writes=("l2S1",))
    qc = ctx.sb("l2qc", [128, 128], BF16)
    kc = ctx.sb("l2kc", [128, 128], BF16)
    qcc = ctx.sb("l2qcc", [128, 128], BF16)
    adt = ctx.sb("l2adt", [64, 64], BF16)
    kdt = ctx.sb("l2kdt", [64, 256], BF16)
    vc = ctx.sb("l2vc", [64, 512], BF16)
    o2 = ctx.sb("l2o2", [128, 512], F32)
    gp = ctx.sb("l2gp", [128, 512], BF16)
    gss = ctx.sb("l2gss", [128, 512], BF16)
    sq2 = ctx.sb("l2sq2", [128, 512], F32)
    ms2 = ctx.sb("l2ms2", [128, 1], F32)
    onrm = ctx.sb("l2onrm", [128, 512], F32)
    ofb = ctx.sb("l2ofb", [128, 512], BF16)
    wpe = ctx.sb("l2wpe", [128, 1024], BF16)
    for pair in range(64):
        for sub in range(2):
            n = pair * 2 + sub
            c0 = n * 64
            ctx.dma("sync", qc[:, 0:64], q_dr[0:128, c0:c0 + 64], reads=("q_dr",), writes=("l2qc",))
            ctx.dma("sync", qc[:, 64:128], q_dr[128:256, c0:c0 + 64], reads=("q_dr",), writes=("l2qc",))
            ctx.dma("sync", kc[:, 0:64], k_dr[0:128, c0:c0 + 64], reads=("k_dr",), writes=("l2kc",))
            ctx.dma("sync", kc[:, 64:128], k_dr[128:256, c0:c0 + 64], reads=("k_dr",), writes=("l2kc",))
            ctx.dma("sync", vc[:], v_dr[c0:c0 + 64, :], reads=("v_dr",), writes=("l2vc",))
            mm(ctx, P["psB"][0:64, 1024:1088],
               [(kc[:, 0:64], qc[:, 0:64]), (kc[:, 64:128], qc[:, 64:128])],
               reads=("l2kc", "l2qc"), writes=("psB2",))
            vop(ctx, "tensor_mul", adt[:], P["psB"][0:64, 1024:1088], sb_DmT[:],
                reads=("psB2", "l2DmT"), writes=("l2adt",))
            vop(ctx, "tensor_mul", qcc[:, 0:64], qc[:, 0:64], sb_crT[:, 0:64],
                reads=("l2qc", "l2crT"), writes=("l2qcc",))
            vop(ctx, "tensor_mul", qcc[:, 64:128], qc[:, 64:128], sb_crT[:, 0:64],
                reads=("l2qc", "l2crT"), writes=("l2qcc",))
            mm(ctx, P["psB"][0:64, 0:512],
               [(adt[:], vc[:]),
                (qcc[:, 0:64], S_sb[0][:]), (qcc[:, 64:128], S_sb[1][:])],
               reads=("l2adt", "l2vc", "l2qcc", "l2S0", "l2S1"), writes=("psB0",))
            ctx.op("tensor", lambda t: t.transpose(P["psT"][0:64, 0:128], kc[:, 0:64], sb_id2[:]),
                   reads=("l2kc", "l2id"), writes=("psT",))
            vop(ctx, "tensor_copy", kdt[:, 0:128], P["psT"][0:64, 0:128],
                reads=("psT",), writes=("l2kdt",))
            ctx.op("tensor", lambda t: t.transpose(P["psT"][0:64, 0:128], kc[:, 64:128], sb_id2[:]),
                   reads=("l2kc", "l2id"), writes=("psT",))
            vop(ctx, "tensor_copy", kdt[:, 128:256], P["psT"][0:64, 0:128],
                reads=("psT",), writes=("l2kdt",))
            vop(ctx, "tensor_scalar", kdt[:], kdt[:], sb_kde[:, 0:1], None, ALU.mult,
                reads=("l2kdt", "l2kde"), writes=("l2kdt",))
            mm(ctx, P["psA"][:, 0:512], [(kdt[:, 0:128], vc[:])],
               reads=("l2kdt", "l2vc"), writes=("psA0",))
            mm(ctx, P["psA"][:, 512:1024], [(kdt[:, 128:256], vc[:])],
               reads=("l2kdt", "l2vc"), writes=("psA1",))
            vop(ctx, "scalar_tensor_tensor", S_sb[0][:], S_sb[0][:], sb_gC[:, 0:1],
                P["psA"][:, 0:512], ALU.mult, ALU.add,
                reads=("l2S0", "l2gC", "psA0"), writes=("l2S0",))
            vop(ctx, "scalar_tensor_tensor", S_sb[1][:], S_sb[1][:], sb_gC[:, 0:1],
                P["psA"][:, 512:1024], ALU.mult, ALU.add,
                reads=("l2S1", "l2gC", "psA1"), writes=("l2S1",))
            vop(ctx, "tensor_copy", o2[sub * 64:(sub + 1) * 64, :],
                P["psB"][0:64, 0:512], reads=("psB0",), writes=("l2o2",))
        sact(ctx, sq2[:], o2[:], AF.Square, reads=("l2o2",), writes=("l2sq2",))
        vop(ctx, "reduce_sum", ms2[:], sq2[:], AX.X, reads=("l2sq2",), writes=("l2ms2",))
        vop(ctx, "tensor_scalar", ms2[:], ms2[:], 1.0 / 512, 1e-5, ALU.mult, ALU.add,
            reads=("l2ms2",), writes=("l2ms2",))
        sact(ctx, ms2[:], ms2[:], AF.Ln, reads=("l2ms2",), writes=("l2ms2",))
        sact(ctx, ms2[:], ms2[:], AF.Exp, scale=-0.5, reads=("l2ms2",), writes=("l2ms2",))
        vop(ctx, "tensor_scalar", onrm[:], o2[:], ms2[:, 0:1], None, ALU.mult,
            reads=("l2o2", "l2ms2"), writes=("l2onrm",))
        vop(ctx, "tensor_mul", onrm[:], onrm[:], gnw_bc[:], reads=("l2onrm", "l2gnbc"),
            writes=("l2onrm",))
        ctx.dma("sync", gp[:], g_dr[pair * 128:(pair + 1) * 128, :], reads=("g_dr",),
                writes=("l2gp",))
        sact(ctx, gss[:], gp[:], AF.Silu, reads=("l2gp",), writes=("l2gss",))
        vop(ctx, "tensor_mul", ofb[:], onrm[:], gss[:], reads=("l2onrm", "l2gss"),
            writes=("l2ofb",))
        for hh in range(2):
            pv = [P["psA"][:, 0:512], P["psA"][:, 512:1024]][hh]
            pk = ["psA0", "psA1"][hh]
            parts = []
            for db in range(4):
                pass
        offm = ctx.sb("l2offm", [128, 4 * 128], BF16)
        for db in range(4):
            ctx.op("tensor", lambda t, db=db: t.transpose(
                P["psT"][:, 0:128], ofb[:, db * 128:(db + 1) * 128], sb_id2[:]),
                reads=("l2ofb", "l2id"), writes=("psT",))
            vop(ctx, "tensor_copy", offm[:, db * 128:(db + 1) * 128], P["psT"][:, 0:128],
                reads=("psT",), writes=("l2offm",))
        for hh in range(2):
            pv = [P["psA"][:, 0:512], P["psA"][:, 512:1024]][hh]
            pk = ["psA0", "psA1"][hh]
            mm(ctx, pv, [(offm[:, db * 128:(db + 1) * 128],
                          wop_sb[:, db * 1024 + hh * 512:db * 1024 + (hh + 1) * 512])
                         for db in range(4)],
               reads=("l2offm", "l2wop"), writes=(pk,))
            vop(ctx, "tensor_copy", wpe[:, hh * 512:(hh + 1) * 512], pv,
                reads=(pk,), writes=("l2wpe",))
        ctx.dma("sync", wop_in[pair * 128:(pair + 1) * 128, :], wpe[:],
                reads=("l2wpe",), writes=("wop_in",))
    ctx.free(["l2wop", "l2S0", "l2S1", "l2qc", "l2kc", "l2qcc", "l2adt", "l2kdt",
              "l2vc", "l2o2", "l2gp", "l2gss", "l2sq2", "l2ms2", "l2onrm",
              "l2ofb", "l2wpe", "l2offm"])

    ctx.collective("ReduceScatter", ALU.add, GRP, wop_in, wop_out,
                   reads=("wop_in",), writes=("wop_out",))
    wrow = ctx.sb("l2wrow", [128, 1024], BF16)
    for i in range(NDT):
        vop(ctx, "tensor_add", xt[i][:, 0:TLOC], xt[i][:, 0:TLOC], xn2[i][:],
            reads=(xn2k[i], f"x{i}"), writes=(f"x{i}",))
    for tt in range(16):
        ctx.dma("sync", wrow[:], wop_out[tt * 128:(tt + 1) * 128, :],
                reads=("wop_out",), writes=("l2wrow",))
        for dm in range(8):
            ctx.op("tensor", lambda t, dm=dm: t.transpose(
                P["psT"][:, 0:128], wrow[:, dm * 128:(dm + 1) * 128], sb_id2[:]),
                reads=("l2wrow", "l2id"), writes=("psT",))
            vop(ctx, "tensor_add", xt[dm][:, tt * 128:(tt + 1) * 128],
                xt[dm][:, tt * 128:(tt + 1) * 128], P["psT"][:, 0:128],
                reads=("psT", f"x{dm}"), writes=(f"x{dm}",))
    ctx.free(["l2wrow"])

    wrow_f = ctx.sb("l2wf", [128, 1024], F32)
    alc = ctx.sb("l2alc", [128, 1], F32)
    alr = ctx.sb("l2alr", [1, 4096], BF16)
    alf = ctx.sb("l2alf", [1, 4096], F32)
    al_dr = ctx.dram_t("l2_al_dr", [4096], F32).ap()
    for r in range(32):
        ctx.dma("sync", wrow_f[:], w1[r * 128:(r + 1) * 128, :], writes=("l2wf",))
        vop(ctx, "reduce_sum", alc[:], wrow_f[:], AX.X, apply_absolute_value=True,
            reads=("l2wf",), writes=("l2alc",))
        vop(ctx, "tensor_scalar_mul", alc[:], alc[:], 1.0 / 1024,
            reads=("l2alc",), writes=("l2alc",))
        ctx.dma("sync", al_dr[r * 128:(r + 1) * 128], alc[:, 0:1], reads=("l2alc",),
                writes=("al_dr",))
    ctx.dma("sync", alf[:], al_dr[:], reads=("al_dr",), writes=("l2alf",))
    vop(ctx, "tensor_copy", alr[:], alf[:], reads=("l2alf",), writes=("l2alr",))
    sgn = ctx.sb("l2sgn", [128, 4096], BF16)
    wTf = ctx.sb("l2wTf", [128, 4096], F32)
    bwt = ctx.sb("l2bwt", [128, 4096], BF16)
    for k in range(8):
        ctx.dma("sync", wTf[:], w1T[k * 128:(k + 1) * 128, :], writes=("l2wTf",))
        ctx.op("scalar", lambda e: e.sign(sgn[:], wTf[:]), reads=("l2wTf",),
               writes=("l2sgn",))
        for sb_ in range(8):
            mm(ctx, P["psB"][:, 0:512], [(ones1[:], alr[:, sb_ * 512:(sb_ + 1) * 512])],
               reads=("l2alr", "ones1"), writes=("psB0",))
            vop(ctx, "tensor_mul", bwt[:, sb_ * 512:(sb_ + 1) * 512],
                sgn[:, sb_ * 512:(sb_ + 1) * 512], P["psB"][:, 0:512],
                reads=("l2sgn", "psB0"), writes=("l2bwt",))
        ctx.dma("sync", bw1_dr[k * 128:(k + 1) * 128, :], bwt[:], reads=("l2bwt",),
                writes=("bw1_dr",))
    for r in range(8):
        ctx.dma("sync", wrow_f[:, 0:1024], w2[r * 128:(r + 1) * 128, 0:1024],
                writes=("l2wf",))
        ctx.dma("sync", wrow_f[:, 0:1024], w2[r * 128:(r + 1) * 128, 1024:2048],
                writes=("l2wf",)) if False else None
        vop(ctx, "reduce_sum", alc[:], wrow_f[:, 0:1024], AX.X,
            apply_absolute_value=True, reads=("l2wf",), writes=("l2alc",))
        acc2 = ctx.sb("l2acc2", [128, 1], F32)
        vop(ctx, "tensor_copy", acc2[:], alc[:], reads=("l2alc",), writes=("l2acc2",))
        for hseg in range(1, 4):
            ctx.dma("sync", wrow_f[:, 0:1024],
                    w2[r * 128:(r + 1) * 128, hseg * 1024:(hseg + 1) * 1024],
                    writes=("l2wf",))
            vop(ctx, "reduce_sum", alc[:], wrow_f[:, 0:1024], AX.X,
                apply_absolute_value=True, reads=("l2wf",), writes=("l2alc",))
            vop(ctx, "tensor_add", acc2[:], acc2[:], alc[:], reads=("l2alc", "l2acc2"),
                writes=("l2acc2",))
        vop(ctx, "tensor_scalar_mul", acc2[:], acc2[:], 1.0 / 4096,
            reads=("l2acc2",), writes=("l2acc2",))
        ctx.dma("sync", al_dr[r * 128:(r + 1) * 128], acc2[:, 0:1], reads=("l2acc2",),
                writes=("al_dr",))
    ctx.dma("sync", alf[:, 0:1024], al_dr[0:1024], reads=("al_dr",), writes=("l2alf",))
    vop(ctx, "tensor_copy", alr[:, 0:1024], alf[:, 0:1024], reads=("l2alf",),
        writes=("l2alr",))
    for k in range(32):
        ctx.dma("sync", wTf[:, 0:1024], w2T[k * 128:(k + 1) * 128, :], writes=("l2wTf",))
        ctx.op("scalar", lambda e: e.sign(sgn[:, 0:1024], wTf[:, 0:1024]),
               reads=("l2wTf",), writes=("l2sgn",))
        for sb_ in range(2):
            mm(ctx, P["psB"][:, 0:512], [(ones1[:], alr[:, sb_ * 512:(sb_ + 1) * 512])],
               reads=("l2alr", "ones1"), writes=("psB0",))
            vop(ctx, "tensor_mul", bwt[:, sb_ * 512:(sb_ + 1) * 512],
                sgn[:, sb_ * 512:(sb_ + 1) * 512], P["psB"][:, 0:512],
                reads=("l2sgn", "psB0"), writes=("l2bwt",))
        ctx.dma("sync", bw2_dr[k * 128:(k + 1) * 128, :], bwt[:, 0:1024],
                reads=("l2bwt",), writes=("bw2_dr",))
    ctx.free(["l2wf", "l2alc", "l2alr", "l2alf", "l2sgn", "l2wTf", "l2bwt", "l2acc2"])

    b1row = ctx.sb("l2b1row", [1, 4096], BF16)
    ctx.dma("gpsimd", b1row[:], b1r[:], writes=("l2b1row",))
    wbuf = [ctx.sb(f"l2wb{p}", [128, 8 * 1024], BF16) for p in range(2)]
    hst = [ctx.sb(f"l2hst{p}", [128, 512], BF16) for p in range(2)]
    for mb in range(4):
        wk_ = f"l2wb{mb % 2}"
        for k in range(8):
            ctx.dma("sync", wbuf[mb % 2][:, k * 1024:(k + 1) * 1024],
                    bw1_dr[k * 128:(k + 1) * 128, mb * 1024:(mb + 1) * 1024],
                    reads=("bw1_dr",), writes=(wk_,))
        for mi in range(8):
            m_ = mb * 8 + mi
            for ci, (c0, w) in enumerate(chunks(TLOC, 512)):
                pp = (mi * 4 + ci) % 2
                pk = ["psA0", "psA1"][pp]
                pv = [P["psA"][:, 0:512], P["psA"][:, 512:1024]][pp]
                parts = [(wbuf[mb % 2][:, k * 1024 + mi * 128:k * 1024 + (mi + 1) * 128],
                          xn2[k][:, c0:c0 + w]) for k in range(8)]
                parts.append((b1row[:, m_ * 128:(m_ + 1) * 128], onesr[:, 0:w]))
                mm(ctx, pv[:, 0:w], parts,
                   reads=(wk_,) + tuple(xn2k) + ("onesr", "l2b1row"), writes=(pk,))
                hk = f"l2hst{pp}"
                sact(ctx, hst[pp][:, 0:w], pv[:, 0:w], AF.Gelu, reads=(pk,), writes=(hk,))
                ctx.dma("sync", h2_dr[m_, :, c0:c0 + w], hst[pp][:, 0:w],
                        reads=(hk,), writes=("h2_dr",))
    ctx.free(["l2wb0", "l2wb1", "l2hst0", "l2hst1"])
    yac2 = [ctx.sb(f"l2ya{i}", [128, TLOC], BF16) for i in range(NDT)]
    hgb = ctx.sb("l2hgb", [128, 8 * TLOC], BF16)
    w2bf = ctx.sb("l2w2b", [128, 8 * 1024], BF16)
    for g_ in range(4):
        for j in range(8):
            ctx.dma("sync", w2bf[:, j * 1024:(j + 1) * 1024],
                    bw2_dr[(g_ * 8 + j) * 128:(g_ * 8 + j + 1) * 128, :],
                    reads=("bw2_dr",), writes=("l2w2b",))
            ctx.dma("sync", hgb[:, j * TLOC:(j + 1) * TLOC], h2_dr[g_ * 8 + j, :, :],
                    reads=("h2_dr",), writes=("l2hgb",))
        for dm in range(8):
            for ci, (c0, w) in enumerate(chunks(TLOC, 512)):
                pp = (dm * 4 + ci) % 2
                pk = ["psA0", "psA1"][pp]
                pv = [P["psA"][:, 0:512], P["psA"][:, 512:1024]][pp]
                mm(ctx, pv[:, 0:w],
                   [(w2bf[:, j * 1024 + dm * 128:j * 1024 + (dm + 1) * 128],
                     hgb[:, j * TLOC + c0:j * TLOC + c0 + w]) for j in range(8)],
                   reads=("l2w2b", "l2hgb"), writes=(pk,))
                ys = yac2[dm][:, c0:c0 + w]
                if g_ == 0:
                    vop(ctx, "tensor_copy", ys, pv[:, 0:w], reads=(pk,),
                        writes=(f"l2ya{dm}",))
                else:
                    vop(ctx, "tensor_add", ys, ys, pv[:, 0:w],
                        reads=(pk, f"l2ya{dm}"), writes=(f"l2ya{dm}",))
    for dm in range(8):
        vop(ctx, "scalar_tensor_tensor", xt[dm][:, 0:TLOC], yac2[dm][:],
            sb_b2[:, dm:dm + 1], xt[dm][:, 0:TLOC], ALU.add, ALU.add,
            reads=(f"l2ya{dm}", "l2b2", f"x{dm}"), writes=(f"x{dm}",))
    if False:
        for nm, ap_, shp, dt in (("dbg2_q", q_dr, [256, T], BF16),
                                 ("dbg2_k", k_dr, [256, T], BF16),
                                 ("dbg2_v", v_dr, [T, 512], BF16),
                                 ("dbg2_g", g_dr, [T, 512], BF16),
                                 ("dbg2_wop", wop_in, [T, 1024], BF16),
                                 ("dbg2_wout", wop_out, [TLOC, 1024], BF16),
                                 ("dbg2_bw1", bw1_dr, [1024, 4096], BF16),
                                 ("dbg2_h", h2_dr, [32, 128, TLOC], BF16)):
            o = ctx.dram_t(nm, shp, dt, kind="ExternalOutput").ap()
            ctx.dma("sync", o, ap_, reads=())
    ctx.free(["l2hgb", "l2w2b", "l2b1row"] + [f"l2ya{i}" for i in range(NDT)])
    ctx.free([f"l2xn{i}" for i in range(NDT)])
    return xt



def emit_l1(ctx, P, xt, R1):
    AI = lambda name, shape, dt=F32: ctx.dram_t(name, shape, dt, kind="ExternalInput").ap()
    WqT = AI("l1_WqT_my", [1024, 256])
    WkT = AI("l1_WkT_my", [1024, 256])
    WvT = AI("l1_WvT_my", [1024, 256])
    WoT = AI("l1_WoT_my", [256, 1024])
    bqm = AI("l1_bq_my", [128, 2])
    bkm = AI("l1_bk_my", [128, 2])
    bvm = AI("l1_bv_my", [1, 256])
    bo = AI("l1_bo", [D, 1])
    RT = AI("l1_RT", [128, 128])
    cosT = AI("l1_cosT", [128, T])
    sinT = AI("l1_sinT", [128, T])
    cmask = AI("l1_cmask", [128, 2048])
    W1m = AI("l1_W1_my", [1024, 4096])
    W2m = AI("l1_W2_my", [4096, 1024])
    b1m = AI("l1_b1_my", [1, 4096])
    b2m = AI("l1_b2_my", [1, 1024])
    gidx = ctx.dram_t("l1_gidx", [16, 128], I32, kind="ExternalInput").ap()
    rix = ctx.dram_t("l1_rix", [R1, 16, 128], I32, kind="ExternalInput").ap()
    cwf = AI("l1_cw2", [T, 8])
    esel = AI("l1_esel", [128, 8])
    idin = ctx.dram["ident"].ap() if "ident" in ctx.dram else AI("ident", [128, 128])

    ccx_in = ctx.dram_t("l1_ccx_in", [1024, TLOC], BF16).ap()
    ccx_out = ctx.dram_t("l1_ccx_out", [4096, TLOC], BF16).ap()
    xtm_in = ctx.dram_t("l1_xtm_in", [TLOC, 1024], BF16).ap()
    xtm_out = ctx.dram_t("l1_xtm_out", [NCORES * TLOC, 1024], BF16).ap()
    q_dr = ctx.dram_t("l1_q_dr", [256, T], BF16).ap()
    k_dr = ctx.dram_t("l1_k_dr", [256, T], BF16).ap()
    v_dr = ctx.dram_t("l1_v_dr", [T, 256], BF16).ap()
    o_dr = ctx.dram_t("l1_o_dr", [256, T], BF16).ap()
    wop_in = ctx.dram_t("l1_wop_in", [T, 1024], BF16).ap()
    wop_out = ctx.dram_t("l1_wop_out", [TLOC, 1024], BF16).ap()
    wout_in = ctx.dram_t("l1_wout_in", [TLOC, 1024], BF16).ap()
    wout_out = ctx.dram_t("l1_wout_out", [NCORES * TLOC + 128, 1024], BF16).ap()
    h1_dr = ctx.dram_t("l1_h1_dr", [32, 128, TLOC], BF16).ap()

    def ldc(name, rows, cols, src_slices, dt=F32, qn="sync"):
        tl = ctx.sb(name, [rows, cols], dt)
        for (dst_sl, src_ap) in src_slices:
            ctx.dma(qn, tl[dst_sl] if dst_sl else tl[:], src_ap, writes=(name,))
        return tl

    def col_slices(src, n, per=1):
        return [((slice(None), slice(i * per, (i + 1) * per)),
                 src[i * 128:(i + 1) * 128, :]) for i in range(n)]

    sb_id = ldc("l1id", 128, 128, [(None, idin[:])], BF16, qn="gpsimd")
    sb_RT = ldc("l1RT", 128, 128, [(None, RT[:])], BF16, qn="gpsimd")
    sb_bq = ldc("l1bq", 128, 2, [(None, bqm[:])])
    sb_bk = ldc("l1bk", 128, 2, [(None, bkm[:])])
    sb_bvr = ldc("l1bvr", 1, 256, [(None, bvm[:])], BF16, qn="gpsimd")
    sb_bo = ldc("l1bo", 128, 8, col_slices(bo, 8))
    sb_cm = ldc("l1cm", 128, 2048, [(None, cmask[:])], BF16, qn="gpsimd")
    ones1 = ctx.sb("ones1", [1, 128], BF16)
    onesr = ctx.sb("onesr", [1, 512], BF16)
    ones_col = ctx.sb("ones_col", [128, 1], BF16)

    xn1 = [ctx.sb(f"l1xn{i}", [128, TLOC], BF16, side="right") for i in range(NDT)]
    xn1k = [f"l1xn{i}" for i in range(NDT)]
    fm_norm(ctx, P, xt, [f"x{i}" for i in range(NDT)], xn1, xn1k, 1e-5, True,
            None, None, width=TLOC)
    for i in range(NDT):
        ctx.dma("sync", ccx_in[i * 128:(i + 1) * 128, :], xn1[i][:],
                reads=(xn1k[i],), writes=("ccx_in",))
    ctx.collective("AllGather", ALU.bypass, GRP, ccx_in, ccx_out,
                   reads=("ccx_in",), writes=("ccx_out",))
    xtt = ctx.sb("l1xtt", [128, 128], BF16)
    for tti in range(16):
        for dmi in range(8):
            ctx.op("tensor", lambda t, tti=tti, dmi=dmi: t.transpose(
                P["psT"][:, 0:128], xn1[dmi][:, tti * 128:(tti + 1) * 128], sb_id[:]),
                reads=(xn1k[dmi], "l1id"), writes=("psT",))
            vop(ctx, "tensor_copy", xtt[:], P["psT"][:, 0:128], reads=("psT",),
                writes=("l1xtt",))
            ctx.dma("sync", xtm_in[tti * 128:(tti + 1) * 128, dmi * 128:(dmi + 1) * 128],
                    xtt[:], reads=("l1xtt",), writes=("xtm_in",))
    ctx.collective("AllGather", ALU.bypass, GRP8, xtm_in, xtm_out,
                   reads=("xtm_in",), writes=("xtm_out",))
    ctx.free(["l1xtt"])

    sb_wq = ldc("l1wq", 128, 8 * 256, [((slice(None), slice(k * 256, (k + 1) * 256)),
                WqT[k * 128:(k + 1) * 128, :]) for k in range(8)], BF16, qn="gpsimd")
    sb_wk = ldc("l1wk", 128, 8 * 256, [((slice(None), slice(k * 256, (k + 1) * 256)),
                WkT[k * 128:(k + 1) * 128, :]) for k in range(8)], BF16, qn="gpsimd")
    sb_wv = ldc("l1wv", 128, 8 * 256, [((slice(None), slice(k * 256, (k + 1) * 256)),
                WvT[k * 128:(k + 1) * 128, :]) for k in range(8)], BF16, qn="gpsimd")
    nrm_sb = ctx.sb("l1nrm", [128, 8 * 512], BF16)
    cosb = ctx.sb("l1cos", [128, 512], BF16)
    sinb = ctx.sb("l1sin", [128, 512], BF16)
    qtmp = ctx.sb("l1qt", [128, 512], BF16)
    rq = ctx.sb("l1rq", [128, 512], F32)
    t1b = ctx.sb("l1t1", [128, 512], F32)
    rob = ctx.sb("l1ro", [128, 512], BF16)
    vev = ctx.sb("l1vev", [128, 256], BF16)
    for tb in range(4):
        for ci in range(4):
            c0 = ci * 512
            gcol = tb * TLOC + c0
            for k in range(8):
                ctx.dma("sync", nrm_sb[:, k * 512:(k + 1) * 512],
                        ccx_out[tb * 1024 + k * 128:tb * 1024 + (k + 1) * 128, c0:c0 + 512],
                        reads=("ccx_out",), writes=("l1nrm",))
            ctx.dma("gpsimd", cosb[:], cosT[:, gcol:gcol + 512], writes=("l1cos",))
            ctx.dma("gpsimd", sinb[:], sinT[:, gcol:gcol + 512], writes=("l1sin",))
            for (wsb, wkk, bb, bk_, dst, dk_) in (
                    (sb_wq, "l1wq", sb_bq, "l1bq", q_dr, "q_dr"),
                    (sb_wk, "l1wk", sb_bk, "l1bk", k_dr, "k_dr")):
                for pk_ in range(2):
                    mm(ctx, P["psA"][:, 0:512],
                       [(wsb[:, k * 256 + pk_ * 128:k * 256 + (pk_ + 1) * 128],
                         nrm_sb[:, k * 512:(k + 1) * 512]) for k in range(8)],
                       reads=(wkk, "l1nrm"), writes=("psA0",))
                    ctx.op("scalar", lambda e, pk_=pk_, bb=bb: e.add(
                        qtmp[:], P["psA"][:, 0:512], bb[:, pk_:pk_ + 1]),
                        reads=("psA0", bk_), writes=("l1qt",))
                    mm(ctx, P["psA"][:, 512:1024], [(sb_RT[:], qtmp[:])],
                       reads=("l1RT", "l1qt"), writes=("psA1",))
                    vop(ctx, "tensor_mul", t1b[:], qtmp[:], cosb[:],
                        reads=("l1qt", "l1cos"), writes=("l1t1",))
                    vop(ctx, "tensor_mul", rq[:], P["psA"][:, 512:1024], sinb[:],
                        reads=("psA1", "l1sin"), writes=("l1rq",))
                    vop(ctx, "tensor_add", rob[:], t1b[:], rq[:],
                        reads=("l1t1", "l1rq"), writes=("l1ro",))
                    ctx.dma("sync", dst[pk_ * 128:(pk_ + 1) * 128, gcol:gcol + 512],
                            rob[:], reads=("l1ro",), writes=(dk_,))
            for ts in range(4):
                parts = [(nrm_sb[:, k * 512 + ts * 128:k * 512 + (ts + 1) * 128],
                          sb_wv[:, k * 256:(k + 1) * 256]) for k in range(8)]
                parts.append((onesr[:, 0:128], sb_bvr[:, 0:256]))
                mm(ctx, P["psA"][:, 0:256], parts,
                   reads=("l1wv", "l1nrm", "onesr", "l1bvr"), writes=("psA0",))
                vop(ctx, "tensor_copy", vev[:], P["psA"][:, 0:256],
                    reads=("psA0",), writes=("l1vev",))
                ctx.dma("sync", v_dr[gcol + ts * 128:gcol + (ts + 1) * 128, :],
                        vev[:], reads=("l1vev",), writes=("v_dr",))
    ctx.free(["l1wq", "l1wk", "l1wv", "l1nrm", "l1cos", "l1sin", "l1qt", "l1rq",
              "l1t1", "l1ro", "l1vev"])

    qch = ctx.sb("l1qch", [64, 512], BF16)
    kti = ctx.sb("l1kti", [64, 2 * 128], BF16)
    vti = ctx.sb("l1vti", [128, 2 * 64], BF16)
    pt = ctx.sb("l1pt", [128, 1024], BF16)
    rl = ctx.sb("l1rl", [1, 512], F32)
    rlb = ctx.sb("l1rlb", [1, 512], BF16)
    rlbc = ctx.sb("l1rlbc", [64, 512], BF16)
    ofm = ctx.sb("l1ofm", [64, 512], BF16)
    for h in range(4):
        qrow = (h // 2) * 128 + (h % 2) * 64
        for tq in range(16):
            ctx.dma("sync", qch[:], q_dr[qrow:qrow + 64, tq * 512:(tq + 1) * 512],
                    reads=("q_dr",), writes=("l1qch",))
            ntk = (tq + 1) * 4
            for tk2 in range((ntk + 1) // 2):
                npair = min(2, ntk - tk2 * 2)
                for j in range(npair):
                    tk = tk2 * 2 + j
                    ctx.dma("sync", kti[:, j * 128:(j + 1) * 128],
                            k_dr[qrow:qrow + 64, tk * 128:(tk + 1) * 128],
                            reads=("k_dr",), writes=("l1kti",))
                    ctx.dma("sync", vti[:, j * 64:(j + 1) * 64],
                            v_dr[tk * 128:(tk + 1) * 128, h * 64:(h + 1) * 64],
                            reads=("v_dr",), writes=("l1vti",))
                    mm(ctx, P["psA"][:, j * 512:(j + 1) * 512],
                       [(kti[:, j * 128:(j + 1) * 128], qch[:])],
                       reads=("l1kti", "l1qch"), writes=(f"psA{j}",))
                sact(ctx, pt[:, 0:npair * 512], P["psA"][:, 0:npair * 512], AF.Exp,
                     scale=0.125, reads=("psA0", "psA1"), writes=("l1pt",))
                for j in range(npair):
                    tk = tk2 * 2 + j
                    moff = tk - tq * 4
                    if 0 <= moff <= 3:
                        vop(ctx, "tensor_mul", pt[:, j * 512:(j + 1) * 512],
                            pt[:, j * 512:(j + 1) * 512],
                            sb_cm[:, moff * 512:(moff + 1) * 512],
                            reads=("l1pt", "l1cm"), writes=("l1pt",))
                for j in range(npair):
                    tk = tk2 * 2 + j
                    first = (tk == 0)
                    mm(ctx, P["psB"][0:64, 0:512],
                       [(vti[:, j * 64:(j + 1) * 64], pt[:, j * 512:(j + 1) * 512])],
                       reads=("l1vti", "l1pt"), writes=("psB0",),
                       start=first, stop=(tk == ntk - 1))
                    mm(ctx, P["psB"][64:65, 0:512],
                       [(ones_col[:], pt[:, j * 512:(j + 1) * 512])],
                       reads=("ones_col", "l1pt"), writes=("psB1",),
                       start=first, stop=(tk == ntk - 1))
            vop(ctx, "reciprocal", rl[:], P["psB"][64:65, 0:512],
                reads=("psB1",), writes=("l1rl",))
            vop(ctx, "tensor_copy", rlb[:], rl[:], reads=("l1rl",), writes=("l1rlb",))
            mm(ctx, P["psA"][0:64, 0:512], [(ones1[:, 0:64], rlb[:])],
               reads=("ones1", "l1rlb"), writes=("psA0",))
            vop(ctx, "tensor_copy", rlbc[:], P["psA"][0:64, 0:512],
                reads=("psA0",), writes=("l1rlbc",))
            vop(ctx, "tensor_mul", ofm[:], P["psB"][0:64, 0:512], rlbc[:],
                reads=("psB0", "l1rlbc"), writes=("l1ofm",))
            ctx.dma("sync", o_dr[qrow:qrow + 64, tq * 512:(tq + 1) * 512],
                    ofm[:], reads=("l1ofm",), writes=("o_dr",))
    ctx.free(["l1qch", "l1kti", "l1vti", "l1pt", "l1rl", "l1rlb", "l1rlbc", "l1ofm"])

    sb_wo = ctx.sb("l1wo", [128, 2 * 1024], BF16)
    for k in range(2):
        ctx.dma("gpsimd", sb_wo[:, k * 1024:(k + 1) * 1024],
                WoT[k * 128:(k + 1) * 128, :], writes=("l1wo",))
    oblk = ctx.sb("l1oblk", [128, 2 * 128], BF16)
    wpe = ctx.sb("l1wpe", [128, 1024], BF16)
    for tt in range(64):
        for k in range(2):
            ctx.dma("sync", oblk[:, k * 128:(k + 1) * 128],
                    o_dr[k * 128:(k + 1) * 128, tt * 128:(tt + 1) * 128],
                    reads=("o_dr",), writes=("l1oblk",))
        for hf in range(2):
            pv = [P["psA"][:, 0:512], P["psA"][:, 512:1024]][hf]
            pk = ["psA0", "psA1"][hf]
            mm(ctx, pv, [(oblk[:, k * 128:(k + 1) * 128],
                          sb_wo[:, k * 1024 + hf * 512:k * 1024 + (hf + 1) * 512])
                         for k in range(2)],
               reads=("l1oblk", "l1wo"), writes=(pk,))
            vop(ctx, "tensor_copy", wpe[:, hf * 512:(hf + 1) * 512], pv,
                reads=(pk,), writes=("l1wpe",))
        ctx.dma("sync", wop_in[tt * 128:(tt + 1) * 128, :], wpe[:],
                reads=("l1wpe",), writes=("wop_in",))
    ctx.collective("ReduceScatter", ALU.add, GRP, wop_in, wop_out,
                   reads=("wop_in",), writes=("wop_out",))
    wrow = ctx.sb("l1wrow", [128, 1024], BF16)
    for tt in range(16):
        ctx.dma("sync", wrow[:], wop_out[tt * 128:(tt + 1) * 128, :],
                reads=("wop_out",), writes=("l1wrow",))
        for dm in range(8):
            ctx.op("tensor", lambda t, dm=dm: t.transpose(
                P["psT"][:, 0:128], wrow[:, dm * 128:(dm + 1) * 128], sb_id[:]),
                reads=("l1wrow", "l1id"), writes=("psT",))
            vop(ctx, "scalar_tensor_tensor", xt[dm][:, tt * 128:(tt + 1) * 128],
                P["psT"][:, 0:128], sb_bo[:, dm:dm + 1],
                xt[dm][:, tt * 128:(tt + 1) * 128], ALU.add, ALU.add,
                reads=("psT", "l1bo", f"x{dm}"), writes=(f"x{dm}",))
    ctx.free(["l1wo", "l1oblk", "l1wpe", "l1wrow"])

    sb_esel = ldc("l1esel", 128, 8, [(None, esel[:])])
    wvals = ctx.sb("l1wv2", [128, 16], F32)
    wst = ctx.sb("l1wst", [128, 8], F32)
    wtmp = ctx.sb("l1wtmp", [128, 8], F32)
    gix = ctx.sb("l1gix", [128, 1], I32)
    xg_tm = ctx.sb("l1xgt", [128, 1024], BF16)
    xgf = [ctx.sb(f"l1xg{i}", [128, TLOC], BF16, side="right") for i in range(NDT)]
    xgfk = [f"l1xg{i}" for i in range(NDT)]
    for gt in range(16):
        ctx.dma("sync", gix[:], gidx[gt:gt + 1, :], writes=("l1gix",))
        ctx.idma(xg_tm[:], xtm_out[:], gix[:, 0:1],
                 reads=("l1gix", "xtm_out"), writes=("l1xgt",))
        for dmi in range(8):
            ctx.op("tensor", lambda t, dmi=dmi: t.transpose(
                P["psT"][:, 0:128], xg_tm[:, dmi * 128:(dmi + 1) * 128], sb_id[:]),
                reads=("l1xgt", "l1id"), writes=("psT",))
            vop(ctx, "tensor_copy", xgf[dmi][:, gt * 128:(gt + 1) * 128],
                P["psT"][:, 0:128], reads=("psT",), writes=(xgfk[dmi],))
    for gt in range(16):
        ctx.dma("sync", gix[:], gidx[gt:gt + 1, :], writes=("l1gix",))
        ctx.idma(wst[:], cwf[:], gix[:, 0:1], reads=("l1gix",), writes=("l1wst",))
        vop(ctx, "tensor_mul", wtmp[:], wst[:], sb_esel[:], reads=("l1wst", "l1esel"),
            writes=("l1wtmp",))
        vop(ctx, "reduce_sum", wvals[:, gt:gt + 1], wtmp[:], AX.X,
            reads=("l1wtmp",), writes=("l1wv2",))
    ctx.free(["l1gix", "l1xgt", "l1wtmp", "l1wst"])

    b1row = ctx.sb("l1b1r", [1, 4096], BF16)
    ctx.dma("gpsimd", b1row[:], b1m[:], writes=("l1b1r",))
    b2row = ctx.sb("l1b2r", [1, 1024], BF16)
    ctx.dma("gpsimd", b2row[:], b2m[:], writes=("l1b2r",))
    wbuf = [ctx.sb(f"l1wb{p}", [128, 8 * 1024], BF16) for p in range(2)]
    sgel = ctx.sb("l1sg", [128, 512], F32)
    tgel = ctx.sb("l1tg", [128, 512], F32)
    ugel = ctx.sb("l1ug", [128, 512], F32)
    hst = [ctx.sb(f"l1hs{p}", [128, 512], BF16) for p in range(2)]
    for mb in range(4):
        wkk = f"l1wb{mb % 2}"
        for k in range(8):
            ctx.dma("gpsimd", wbuf[mb % 2][:, k * 1024:(k + 1) * 1024],
                    W1m[k * 128:(k + 1) * 128, mb * 1024:(mb + 1) * 1024], writes=(wkk,))
        for mi in range(8):
            m_ = mb * 8 + mi
            for ci, (c0, w) in enumerate(chunks(TLOC, 512)):
                pp = (mi * 4 + ci) % 2
                pk = ["psA0", "psA1"][pp]
                pv = [P["psA"][:, 0:512], P["psA"][:, 512:1024]][pp]
                parts = [(wbuf[mb % 2][:, k * 1024 + mi * 128:k * 1024 + (mi + 1) * 128],
                          xgf[k][:, c0:c0 + w]) for k in range(8)]
                parts.append((b1row[:, m_ * 128:(m_ + 1) * 128], onesr[:, 0:w]))
                mm(ctx, pv[:, 0:w], parts,
                   reads=(wkk,) + tuple(xgfk) + ("onesr", "l1b1r"), writes=(pk,))
                sact(ctx, sgel[:, 0:w], pv[:, 0:w], AF.Square, reads=(pk,), writes=("l1sg",))
                vop(ctx, "tensor_scalar", tgel[:, 0:w], sgel[:, 0:w], 0.044715, 1.0,
                    ALU.mult, ALU.add, reads=("l1sg",), writes=("l1tg",))
                vop(ctx, "scalar_tensor_tensor", ugel[:, 0:w], tgel[:, 0:w],
                    0.7978845608028654, pv[:, 0:w], ALU.mult, ALU.mult,
                    reads=("l1tg", pk), writes=("l1ug",))
                sact(ctx, ugel[:, 0:w], ugel[:, 0:w], AF.Tanh, reads=("l1ug",),
                     writes=("l1ug",))
                vop(ctx, "tensor_scalar", tgel[:, 0:w], ugel[:, 0:w], 0.5, 0.5,
                    ALU.mult, ALU.add, reads=("l1ug",), writes=("l1tg",))
                hk = f"l1hs{pp}"
                vop(ctx, "tensor_mul", hst[pp][:, 0:w], tgel[:, 0:w], pv[:, 0:w],
                    reads=("l1tg", pk), writes=(hk,))
                ctx.dma("sync", h1_dr[m_, :, c0:c0 + w], hst[pp][:, 0:w],
                        reads=(hk,), writes=("h1_dr",))
    ctx.free(["l1wb0", "l1wb1", "l1sg", "l1tg", "l1ug", "l1hs0", "l1hs1"])
    ctx.free(xgfk)
    yac = ctx.sb("l1yac", [128, 16 * 1024], BF16, side="right")
    hgb = ctx.sb("l1hgb", [128, 8 * TLOC], BF16)
    w2b = ctx.sb("l1w2b", [128, 8 * 1024], BF16)
    for g_ in range(4):
        for j in range(8):
            ctx.dma("gpsimd", w2b[:, j * 1024:(j + 1) * 1024],
                    W2m[(g_ * 8 + j) * 128:(g_ * 8 + j + 1) * 128, :], writes=("l1w2b",))
            ctx.dma("sync", hgb[:, j * TLOC:(j + 1) * TLOC], h1_dr[g_ * 8 + j, :, :],
                    reads=("h1_dr",), writes=("l1hgb",))
        for tt in range(16):
            for hf in range(2):
                pp = (tt * 2 + hf) % 2
                pk = ["psA0", "psA1"][pp]
                pv = [P["psA"][:, 0:512], P["psA"][:, 512:1024]][pp]
                parts = [(hgb[:, j * TLOC + tt * 128:j * TLOC + (tt + 1) * 128],
                          w2b[:, j * 1024 + hf * 512:j * 1024 + (hf + 1) * 512])
                         for j in range(8)]
                if g_ == 3:
                    parts.append((onesr[:, 0:128], b2row[:, hf * 512:(hf + 1) * 512]))
                mm(ctx, pv[:, 0:512], parts,
                   reads=("l1w2b", "l1hgb", "onesr", "l1b2r"), writes=(pk,))
                ys = yac[:, tt * 1024 + hf * 512:tt * 1024 + (hf + 1) * 512]
                if g_ == 0:
                    vop(ctx, "tensor_copy", ys, pv[:, 0:512], reads=(pk,),
                        writes=("l1yac",))
                else:
                    vop(ctx, "tensor_add", ys, ys, pv[:, 0:512],
                        reads=(pk, "l1yac"), writes=("l1yac",))
    wosc = ctx.sb("l1wos", [128, 1024], BF16)
    for tt in range(16):
        vop(ctx, "tensor_scalar", wosc[:], yac[:, tt * 1024:(tt + 1) * 1024],
            wvals[:, tt:tt + 1], None, ALU.mult, reads=("l1yac", "l1wv2"),
            writes=("l1wos",))
        ctx.dma("sync", wout_in[tt * 128:(tt + 1) * 128, :], wosc[:],
                reads=("l1wos",), writes=("wout_in",))
    ctx.free(["l1hgb", "l1w2b", "l1wos", "l1b2r", "l1b1r"])
    ctx.collective("AllGather", ALU.bypass, GRP8, wout_in, wout_out[0:NCORES * TLOC, :],
                   reads=("wout_in",), writes=("wout_out",))
    zpad = ctx.sb("l1zp", [128, 1024], BF16)
    vop(ctx, "memset", zpad[:], 0.0, writes=("l1zp",))
    ctx.dma("sync", wout_out[NCORES * TLOC:NCORES * TLOC + 128, :], zpad[:],
            reads=("l1zp",), writes=("wout_out",))

    acc = ctx.sb("l1acc", [128, 1024], BF16)
    rixt = ctx.sb("l1rixt", [128, 1], I32)
    for tt in range(16):
        vop(ctx, "memset", acc[:], 0.0, writes=("l1acc",), eng="gpsimd")
        for r_ in range(R1):
            ctx.dma("sync", rixt[:], rix[r_, tt:tt + 1, :], writes=("l1rixt",))
            ctx.idma(acc[:], wout_out[:], rixt[:, 0:1],
                     reads=("l1rixt", "wout_out", "l1acc"), writes=("l1acc",),
                     compute_op=ALU.add)
        for dm in range(8):
            ctx.op("tensor", lambda t, dm=dm: t.transpose(
                P["psT"][:, 0:128], acc[:, dm * 128:(dm + 1) * 128], sb_id[:]),
                reads=("l1acc", "l1id"), writes=("psT",))
            vop(ctx, "tensor_add", xt[dm][:, tt * 128:(tt + 1) * 128],
                xt[dm][:, tt * 128:(tt + 1) * 128], P["psT"][:, 0:128],
                reads=("psT", f"x{dm}"), writes=(f"x{dm}",))
    if False:
        for nm, ap_, shp, dt in (("dbg1_q", q_dr, [256, T], BF16),
                                 ("dbg1_k", k_dr, [256, T], BF16),
                                 ("dbg1_v", v_dr, [T, 256], BF16),
                                 ("dbg1_o", o_dr, [256, T], BF16),
                                 ("dbg1_wopi", wop_in, [T, 1024], BF16),
                                 ("dbg1_wopo", wop_out, [TLOC, 1024], BF16),
                                 ("dbg1_wouto", wout_out, [NCORES * TLOC + 128, 1024], BF16)):
            o_ = ctx.dram_t(nm, shp, dt, kind="ExternalOutput").ap()
            ctx.dma("sync", o_, ap_, reads=())
    ctx.free(["l1acc", "l1rixt", "l1zp", "l1yac"])
    ctx.free(xn1k)
    return xt



def emit_l3(ctx, P, xt, xsrc=None, parts=("attn", "ffn"), reload=None, dbg=()):
    AI = lambda name, shape, dt=F32: ctx.dram_t(name, shape, dt, kind="ExternalInput").ap()
    do_a = "attn" in parts
    do_f = "ffn" in parts
    if xsrc is None:
        xsrc = xt
    xsrck = [f"x{i}" for i in range(NDT)] if xsrc is xt else [f"xsrc{i}" for i in range(NDT)]
    idin = ctx.dram["ident"].ap() if "ident" in ctx.dram else AI("ident", [128, 128])
    sb_id3 = ctx.sb("l3id", [128, 128], BF16)
    ctx.dma("gpsimd", sb_id3[:], idin[:], writes=("l3id",))
    ones1 = ctx.sb("ones1", [1, 128], BF16)

    def ldc(name, rows, cols, src_slices, dt=F32, qn="sync"):
        tl = ctx.sb(name, [rows, cols], dt)
        for (dst_sl, src_ap) in src_slices:
            ctx.dma(qn, tl[dst_sl] if dst_sl else tl[:], src_ap, writes=(name,))
        return tl

    xn3 = [ctx.sb(f"l3xn{i}", [128, TLOC], BF16, side="right") for i in range(NDT)]
    xn3k = [f"l3xn{i}" for i in range(NDT)]
    if do_f:
        we_sb = ctx.sb("l3we", [128, 128], F32)
        xlo = [ctx.sb(f"l3xl{i}", [128, TLOC], BF16) for i in range(NDT)]
        xlok = [f"l3xl{i}" for i in range(NDT)]
        fm_norm(ctx, P, xsrc, xsrck, xn3, xn3k, 1e-5, True, None, None,
                width=TLOC, lo=xlo, lokeys=xlok)
    else:
        fm_norm(ctx, P, xsrc, xsrck, xn3, xn3k, 1e-5, True, None, None, width=TLOC)

    if do_f:
        WrT3 = AI("l3f_WrT", [1024, 8])
        wrtf = ldc("l3wrtf", 128, 64, [((slice(None), slice(kb * 8, (kb + 1) * 8)),
                   WrT3[kb * 128:(kb + 1) * 128, :]) for kb in range(8)])
        wrtb = ctx.sb("l3wrtb", [128, 64], BF16)
        wrtl = ctx.sb("l3wrtl", [128, 64], BF16)
        vop(ctx, "tensor_copy", wrtb[:], wrtf[:], reads=("l3wrtf",), writes=("l3wrtb",))
        vop(ctx, "tensor_sub", wrtl[:], wrtf[:], wrtb[:], reads=("l3wrtf", "l3wrtb"),
            writes=("l3wrtl",))
        lgs = ctx.sb("l3lgs", [128, 8], F32)
        mxs = ctx.sb("l3mxs", [128, 8], F32)
        dcol = ctx.sb("l3dcol", [128, 1], F32)
        w1c = ctx.sb("l3w1c", [128, 1], F32)
        w2c = ctx.sb("l3w2c", [128, 1], F32)
        me1 = ctx.sb("l3me1", [128, 8], F32)
        me2 = ctx.sb("l3me2", [128, 8], F32)
        for tt in range(16):
            ts_ = slice(tt * 128, (tt + 1) * 128)
            pp = []
            for kb in range(8):
                pp.append((xn3[kb][:, ts_], wrtb[:, kb * 8:(kb + 1) * 8]))
                pp.append((xlo[kb][:, ts_], wrtb[:, kb * 8:(kb + 1) * 8]))
                pp.append((xn3[kb][:, ts_], wrtl[:, kb * 8:(kb + 1) * 8]))
            mm(ctx, P["psB"][:, 0:8], pp,
               reads=tuple(xn3k) + tuple(xlok) + ("l3wrtb", "l3wrtl"), writes=("psB0",))
            vop(ctx, "tensor_copy", lgs[:], P["psB"][:, 0:8], reads=("psB0",),
                writes=("l3lgs",))
            vop(ctx, "max", mxs[:], lgs[:], reads=("l3lgs",), writes=("l3mxs",))
            vop(ctx, "tensor_sub", dcol[:], mxs[:, 0:1], mxs[:, 1:2], reads=("l3mxs",),
                writes=("l3dcol",))
            sact(ctx, w1c[:], dcol[:], AF.Sigmoid, reads=("l3dcol",), writes=("l3w1c",))
            vop(ctx, "tensor_scalar", w2c[:], w1c[:], -1.0, 1.0, ALU.mult, ALU.add,
                reads=("l3w1c",), writes=("l3w2c",))
            vop(ctx, "tensor_scalar", me1[:], lgs[:], mxs[:, 0:1], None, ALU.is_equal,
                reads=("l3lgs", "l3mxs"), writes=("l3me1",))
            vop(ctx, "tensor_scalar", me1[:], me1[:], w1c[:, 0:1], None, ALU.mult,
                reads=("l3me1", "l3w1c"), writes=("l3me1",))
            vop(ctx, "tensor_scalar", me2[:], lgs[:], mxs[:, 1:2], None, ALU.is_equal,
                reads=("l3lgs", "l3mxs"), writes=("l3me2",))
            vop(ctx, "tensor_scalar", me2[:], me2[:], w2c[:, 0:1], None, ALU.mult,
                reads=("l3me2", "l3w2c"), writes=("l3me2",))
            vop(ctx, "tensor_add", we_sb[:, tt * 8:(tt + 1) * 8], me1[:], me2[:],
                reads=("l3me1", "l3me2"), writes=("l3we",))
        ctx.free(["l3wrtf", "l3wrtb", "l3wrtl", "l3lgs", "l3mxs", "l3dcol",
                  "l3w1c", "l3w2c", "l3me1", "l3me2"])
        ctx.free(xlok)

    ccx3_in = ctx.dram_t("l3_ccx_in", [1024, TLOC], BF16).ap()
    if do_a:
        ccx3_out = ctx.dram_t("l3_ccx_out", [4096, TLOC], BF16).ap()
    for i in range(NDT):
        ctx.dma("sync", ccx3_in[i * 128:(i + 1) * 128, :], xn3[i][:],
                reads=(xn3k[i],), writes=("ccx3_in",))
    if do_a:
        ctx.collective("AllGather", ALU.bypass, GRP, ccx3_in, ccx3_out,
                       reads=("ccx3_in",), writes=("ccx3_out",))
        ctx.free(xn3k)
        emit_l3_attn(ctx, P, xt, ccx3_out, sb_id3, ones1, AI, ldc, dbg=dbg)
    if do_f:
        if reload is not None:
            for i in range(NDT):
                ctx.dma("sync", xt[i][:], reload[i * 128:(i + 1) * 128, :],
                        writes=(f"x{i}",))
        if do_a:
            xn3f = [ctx.sb(f"l3xf{i}", [128, TLOC], BF16) for i in range(NDT)]
            xn3fk = [f"l3xf{i}" for i in range(NDT)]
            for i in range(NDT):
                ctx.dma("sync", xn3f[i][:], ccx3_in[i * 128:(i + 1) * 128, :],
                        reads=("ccx3_in",), writes=(xn3fk[i],))
        else:
            xn3f, xn3fk = xn3, xn3k
        emit_l3_ffn(ctx, P, xt, xn3f, xn3fk, we_sb, sb_id3, AI)
    return xt


def emit_l3_attn(ctx, P, xt, ccx, sb_id3, ones1, AI, ldc, dbg=()):
    def col_slices(src, n, per=1):
        return [((slice(None), slice(i * per, (i + 1) * per)),
                 src[i * 128:(i + 1) * 128, :]) for i in range(n)]
    xm = {}
    for nm in ("x_r", "x_w", "x_k", "x_v", "x_a", "x_g"):
        ap = AI(f"l3_{nm}", [D, 1])
        t_ = ldc(f"l3m{nm}", 128, 8, col_slices(ap, 8))
        s_ = ctx.sb(f"l3s{nm}", [128, 8], F32)
        vop(ctx, "tensor_scalar", s_[:], t_[:], -1.0, 1.0, ALU.mult, ALU.add,
            reads=(f"l3m{nm}",), writes=(f"l3s{nm}",))
        xm[nm] = (t_, s_, f"l3m{nm}", f"l3s{nm}")
    wtmp = ctx.sb("l3wt", [128, 256], F32)
    folded = {}
    for (nm, wname, cols, mixnm) in (("r", "l3_Wr_my", 256, "x_r"),
                                     ("k", "l3_Wk_my", 256, "x_k"),
                                     ("v", "l3_Wv_my", 256, "x_v"),
                                     ("w", "l3_w1", 64, "x_w"),
                                     ("a", "l3_a1", 32, "x_a"),
                                     ("g", "l3_g1", 128, "x_g")):
        wap = AI(wname, [1024, cols])
        ft = ctx.sb(f"l3W{nm}", [128, 8 * 2 * cols], BF16)
        fk = f"l3W{nm}"
        mt, st, mk, sk = xm[mixnm]
        for kb in range(8):
            ctx.dma("sync", wtmp[:, 0:cols], wap[kb * 128:(kb + 1) * 128, :],
                    writes=("l3wt",))
            base = kb * 2 * cols
            vop(ctx, "tensor_scalar", ft[:, base:base + cols], wtmp[:, 0:cols],
                st[:, kb:kb + 1], None, ALU.mult, reads=("l3wt", sk), writes=(fk,))
            vop(ctx, "tensor_scalar", ft[:, base + cols:base + 2 * cols],
                wtmp[:, 0:cols], mt[:, kb:kb + 1], None, ALU.mult,
                reads=("l3wt", mk), writes=(fk,))
        folded[nm] = ft
    w2s = ldc("l3w2s", 64, 256, [(None, AI("l3_w2_my", [64, 256])[:])], BF16, qn="gpsimd")
    a2s = ldc("l3a2s", 32, 256, [(None, AI("l3_a2_my", [32, 256])[:])], BF16, qn="gpsimd")
    g2s = ldc("l3g2s", 128, 256, [(None, AI("l3_g2_my", [128, 256])[:])], BF16, qn="gpsimd")
    w0c = ldc("l3w0c", 128, 2, [(None, AI("l3_w0_my", [128, 2])[:])])
    a0c = ldc("l3a0c", 128, 2, [(None, AI("l3_a0_my", [128, 2])[:])])
    kkcol = ldc("l3kkc", 128, 2, [(None, AI("l3_kk_my", [128, 2])[:])])
    kacol = ldc("l3kac", 128, 2, [(None, AI("l3_ka_my", [128, 2])[:])])
    rkcol = ldc("l3rkc", 128, 2, [(None, AI("l3_rk_my", [128, 2])[:])])
    negw0 = ctx.sb("l3nw0", [128, 2], F32)
    omka = ctx.sb("l3omka", [128, 2], F32)
    vop(ctx, "tensor_scalar_mul", negw0[:], w0c[:], -1.0, reads=("l3w0c",),
        writes=("l3nw0",))
    vop(ctx, "tensor_scalar", omka[:], kacol[:], -1.0, 1.0, ALU.mult, ALU.add,
        reads=("l3kac",), writes=("l3omka",))
    hselC = ldc("l3hsC", 128, 2, [(None, AI("l3_hselC", [128, 2])[:])], BF16, qn="gpsimd")
    hselR = ldc("l3hsR", 2, 128, [(None, AI("l3_hsel", [2, 128])[:])], BF16, qn="gpsimd")

    ew_dr = ctx.dram_t("l3_ew_dr", [256, T], F32).ap()
    r_dr = ctx.dram_t("l3_r_dr", [256, T], BF16).ap()
    k2_dr = ctx.dram_t("l3_k2_dr", [256, T], BF16).ap()
    kkn_dr = ctx.dram_t("l3_kkn_dr", [256, T], BF16).ap()
    ag_dr = ctx.dram_t("l3_ag_dr", [256, T], BF16).ap()
    v_dr = ctx.dram_t("l3_v_dr", [256, T], BF16).ap()
    g_dr = ctx.dram_t("l3_g_dr", [T, 256], BF16).ap()
    vtm_dr = ctx.dram_t("l3_vtm_dr", [T, 256], BF16).ap()
    y_dr = ctx.dram_t("l3_y_dr", [T, 256], BF16).ap()
    s_dr = ctx.dram_t("l3_s_dr", [2, 2, T], BF16).ap()
    wop3_in = ctx.dram_t("l3_wop_in", [T, 1024], BF16).ap()
    wop3_out = ctx.dram_t("l3_wop_out", [TLOC, 1024], BF16).ap()

    nsb = ctx.sb("l3nsb", [128, 8 * 513], BF16)
    rev = ctx.sb("l3rev", [128, 1024], BF16)
    kev = ctx.sb("l3kev", [128, 1024], BF16)
    vev = ctx.sb("l3vev", [128, 1024], BF16)
    agt = ctx.sb("l3agt", [128, 1024], BF16)
    t1 = ctx.sb("l3t1", [64, 512], BF16)
    sa1 = ctx.sb("l3sa1", [32, 512], BF16)
    sg1 = ctx.sb("l3sg1", [128, 512], BF16)
    au = ctx.sb("l3au", [128, 512], F32)
    eu = ctx.sb("l3eu", [128, 512], F32)
    lp = ctx.sb("l3lp", [128, 512], F32)
    mn = ctx.sb("l3mn", [128, 512], F32)
    dd = ctx.sb("l3dd", [128, 512], F32)
    ewt = ctx.sb("l3ewt", [128, 512], F32)
    kkt = ctx.sb("l3kkt", [128, 512], BF16)
    sqk = ctx.sb("l3sqk", [128, 512], BF16)
    nrm2 = ctx.sb("l3nr2", [2, 512], F32)
    nrb = ctx.sb("l3nrb", [2, 512], BF16)
    kknt = ctx.sb("l3kkn", [128, 512], BF16)
    k2t = ctx.sb("l3k2t", [128, 512], BF16)
    tt2 = ctx.sb("l3tt2", [128, 512], BF16)
    gtile = ctx.sb("l3gti", [128, 256], BF16)
    srow = ctx.sb("l3srw", [2, 512], BF16)
    stash = ctx.sb("l3stash", [128, 8], BF16)
    prA = ("l3Wr", "l3Wk", "l3Wv", "l3Ww", "l3Wa", "l3Wg", "l3nsb")
    for tb in (() if "nopre" in dbg else range(4)):
        for ci in range(4):
            gcol = tb * TLOC + ci * 512
            if ci > 0:
                for kb in range(8):
                    ctx.dma("sync", nsb[:, kb * 513:kb * 513 + 513],
                            ccx[tb * 1024 + kb * 128:tb * 1024 + (kb + 1) * 128,
                                ci * 512 - 1:ci * 512 + 512],
                            reads=("ccx3_out",), writes=("l3nsb",))
            else:
                for kb in range(8):
                    ctx.dma("sync", nsb[:, kb * 513 + 1:kb * 513 + 513],
                            ccx[tb * 1024 + kb * 128:tb * 1024 + (kb + 1) * 128,
                                0:512],
                            reads=("ccx3_out",), writes=("l3nsb",))
                if gcol == 0:
                    for kb in range(8):
                        vop(ctx, "memset", nsb[:, kb * 513:kb * 513 + 1], 0.0,
                            writes=("l3nsb",))
                else:
                    for kb in range(8):
                        vop(ctx, "tensor_copy", nsb[:, kb * 513:kb * 513 + 1],
                            stash[:, kb:kb + 1], reads=("l3stash",),
                            writes=("l3nsb",))

            if ci == 3:
                for kb in range(8):
                    vop(ctx, "tensor_copy", stash[:, kb:kb + 1],
                        nsb[:, kb * 513 + 512:kb * 513 + 513], reads=("l3nsb",),
                        writes=("l3stash",))

            def pparts(nm, cols, m):
                ft = folded[nm]
                ps = []
                for kb in range(8):
                    base = kb * 2 * cols
                    ps.append((ft[:, base + m * 128:base + m * 128 + min(128, cols)],
                               nsb[:, kb * 513 + 1:kb * 513 + 513]))
                    ps.append((ft[:, base + cols + m * 128:
                                  base + cols + m * 128 + min(128, cols)],
                               nsb[:, kb * 513:kb * 513 + 512]))
                return ps
            mm(ctx, P["psB"][0:32, 1024:1536], pparts("a", 32, 0), reads=prA,
               writes=("psB2",))
            vop(ctx, "tensor_copy", sa1[:], P["psB"][0:32, 1024:1536],
                reads=("psB2",), writes=("l3sa1",))
            for m in range(2):
                mm(ctx, P["psA"][:, 0:512], [(a2s[:, m * 128:(m + 1) * 128], sa1[:])],
                   reads=("l3a2s", "l3sa1"), writes=("psA0",))
                sact(ctx, agt[:, m * 512:(m + 1) * 512], P["psA"][:, 0:512],
                     AF.Sigmoid, bias=a0c[:, m:m + 1],
                     reads=("psA0", "l3a0c"), writes=("l3agt",))
                ctx.dma("sync", ag_dr[m * 128:(m + 1) * 128, gcol:gcol + 512],
                        agt[:, m * 512:(m + 1) * 512], reads=("l3agt",),
                        writes=("ag_dr",))
            for (nm, dst, dk_, sbuf, sk) in (("r", r_dr, "r_dr", rev, "l3rev"),
                                             ("k", None, None, kev, "l3kev"),
                                             ("v", v_dr, "v_dr", vev, "l3vev")):
                for m in range(2):
                    pv = [P["psA"][:, 0:512], P["psA"][:, 512:1024]][m]
                    pk = ["psA0", "psA1"][m]
                    mm(ctx, pv, pparts(nm, 256, m), reads=prA, writes=(pk,))
                    vop(ctx, "tensor_copy", sbuf[:, m * 512:(m + 1) * 512], pv,
                        reads=(pk,), writes=(sk,))
                    if dst is not None:
                        ctx.dma("sync", dst[m * 128:(m + 1) * 128, gcol:gcol + 512],
                                sbuf[:, m * 512:(m + 1) * 512], reads=(sk,),
                                writes=(dk_,))
            mm(ctx, P["psB"][0:64, 1024:1536], pparts("w", 64, 0), reads=prA,
               writes=("psB2",))
            sact(ctx, t1[:], P["psB"][0:64, 1024:1536], AF.Tanh, reads=("psB2",),
                 writes=("l3t1",))
            for m in range(2):
                mm(ctx, P["psA"][:, 0:512], [(w2s[:, m * 128:(m + 1) * 128], t1[:])],
                   reads=("l3w2s", "l3t1"), writes=("psA0",))
                vop(ctx, "tensor_scalar", au[:], P["psA"][:, 0:512],
                    w0c[:, m:m + 1], None, ALU.add, reads=("psA0", "l3w0c"),
                    writes=("l3au",))
                vop(ctx, "tensor_scalar", mn[:], au[:], 0.0, None, ALU.min,
                    reads=("l3au",), writes=("l3mn",))
                sact(ctx, eu[:], au[:], AF.Abs, reads=("l3au",), writes=("l3eu",))
                sact(ctx, eu[:], eu[:], AF.Exp, scale=-1.0, reads=("l3eu",),
                     writes=("l3eu",))
                sact(ctx, lp[:], eu[:], AF.Ln, bias=1.0, reads=("l3eu",),
                     writes=("l3lp",))
                vop(ctx, "scalar_tensor_tensor", dd[:], mn[:], -0.5, lp[:],
                    ALU.add, ALU.subtract, reads=("l3mn", "l3lp"), writes=("l3dd",))
                sact(ctx, ewt[:], dd[:], AF.Exp, reads=("l3dd",), writes=("l3ewt",))
                ctx.dma("sync", ew_dr[m * 128:(m + 1) * 128, gcol:gcol + 512],
                        ewt[:], reads=("l3ewt",), writes=("ew_dr",))
            mm(ctx, P["psB"][:, 1024:1536], pparts("g", 128, 0), reads=prA,
               writes=("psB2",))
            sact(ctx, sg1[:], P["psB"][:, 1024:1536], AF.Sigmoid, reads=("psB2",),
                 writes=("l3sg1",))
            for ts in range(4):
                mm(ctx, P["psA"][:, 512:768], [(sg1[:, ts * 128:(ts + 1) * 128], g2s[:])],
                   reads=("l3sg1", "l3g2s"), writes=("psA1",))
                vop(ctx, "tensor_copy", gtile[:], P["psA"][:, 512:768],
                    reads=("psA1",), writes=("l3gti",))
                ctx.dma("sync", g_dr[gcol + ts * 128:gcol + (ts + 1) * 128, :],
                        gtile[:], reads=("l3gti",), writes=("g_dr",))
            for m in range(2):
                ksl = kev[:, m * 512:(m + 1) * 512]
                vop(ctx, "tensor_scalar", kkt[:], ksl, kkcol[:, m:m + 1], None,
                    ALU.mult, reads=("l3kev", "l3kkc"), writes=("l3kkt",))
                sact(ctx, sqk[:], kkt[:], AF.Square, reads=("l3kkt",), writes=("l3sqk",))
                mm(ctx, P["psB"][0:2, 1024:1536], [(hselC[:], sqk[:])],
                   reads=("l3hsC", "l3sqk"), writes=("psB2",))
                vop(ctx, "tensor_copy", nrm2[:], P["psB"][0:2, 1024:1536],
                    reads=("psB2",), writes=("l3nr2",))
                sact(ctx, nrm2[:], nrm2[:], AF.Ln, reads=("l3nr2",), writes=("l3nr2",))
                sact(ctx, nrm2[:], nrm2[:], AF.Exp, scale=-0.5, reads=("l3nr2",),
                     writes=("l3nr2",))
                vop(ctx, "tensor_scalar", nrm2[:], nrm2[:], 1e12, None, ALU.min,
                    reads=("l3nr2",), writes=("l3nr2",))
                vop(ctx, "tensor_copy", nrb[:], nrm2[:], reads=("l3nr2",),
                    writes=("l3nrb",))
                mm(ctx, P["psA"][:, 1024:1536], [(hselR[:], nrb[:])],
                   reads=("l3hsR", "l3nrb"), writes=("psA2",))
                vop(ctx, "tensor_mul", kknt[:], kkt[:], P["psA"][:, 1024:1536],
                    reads=("l3kkt", "psA2"), writes=("l3kkn",))
                ctx.dma("sync", kkn_dr[m * 128:(m + 1) * 128, gcol:gcol + 512],
                        kknt[:], reads=("l3kkn",), writes=("kkn_dr",))
                vop(ctx, "tensor_scalar", tt2[:], agt[:, m * 512:(m + 1) * 512],
                    kacol[:, m:m + 1], None, ALU.mult,
                    reads=("l3agt", "l3kac"), writes=("l3tt2",))
                vop(ctx, "tensor_scalar", tt2[:], tt2[:], omka[:, m:m + 1], None,
                    ALU.add, reads=("l3tt2", "l3omka"), writes=("l3tt2",))
                vop(ctx, "tensor_mul", k2t[:], ksl, tt2[:], reads=("l3kev", "l3tt2"),
                    writes=("l3k2t",))
                ctx.dma("sync", k2_dr[m * 128:(m + 1) * 128, gcol:gcol + 512],
                        k2t[:], reads=("l3k2t",), writes=("k2_dr",))
                vop(ctx, "tensor_mul", tt2[:], rev[:, m * 512:(m + 1) * 512], k2t[:],
                    reads=("l3rev", "l3k2t"), writes=("l3tt2",))
                vop(ctx, "tensor_scalar", tt2[:], tt2[:], rkcol[:, m:m + 1], None,
                    ALU.mult, reads=("l3tt2", "l3rkc"), writes=("l3tt2",))
                mm(ctx, P["psB"][0:2, 1024:1536], [(hselC[:], tt2[:])],
                   reads=("l3hsC", "l3tt2"), writes=("psB2",))
                vop(ctx, "tensor_copy", srow[:], P["psB"][0:2, 1024:1536],
                    reads=("psB2",), writes=("l3srw",))
                ctx.dma("sync", s_dr[m, :, gcol:gcol + 512], srow[:],
                        reads=("l3srw",), writes=("s_dr",))
    ctx.free(["l3nsb", "l3rev", "l3kev", "l3vev", "l3agt", "l3t1", "l3sa1",
              "l3sg1", "l3au", "l3eu", "l3lp", "l3mn", "l3dd", "l3ewt", "l3kkt",
              "l3sqk", "l3nr2", "l3nrb", "l3kkn", "l3k2t", "l3tt2", "l3gti",
              "l3srw", "l3stash"])
    ctx.free(["l3wt"] + [f"l3W{n}" for n in ("r", "k", "v", "w", "a", "g")] +
             ["l3w2s", "l3a2s", "l3g2s", "l3w0c", "l3a0c", "l3kkc", "l3kac",
              "l3rkc", "l3nw0", "l3omka", "l3hsC", "l3hsR"])
    ctx.free([f"l3m{nm}" for nm in ("x_r", "x_w", "x_k", "x_v", "x_a", "x_g")] +
             [f"l3s{nm}" for nm in ("x_r", "x_w", "x_k", "x_v", "x_a", "x_g")])

    mskL = ldc("l3mkL", 128, 128, [(None, AI("l3_mskL", [128, 128])[:])], BF16, qn="gpsimd")
    mskU = ldc("l3mkU", 128, 128, [(None, AI("l3_mskU", [128, 128])[:])], BF16, qn="gpsimd")
    mskUI = ldc("l3mkI", 128, 128, [(None, AI("l3_mskUI", [128, 128])[:])], BF16, qn="gpsimd")
    zz2k = ctx.sb("l3zz", [128, 2048], F32)
    vop(ctx, "memset", zz2k[:], 0.0, writes=("l3zz",))
    ewin = ctx.sb("l3ewi", [128, T], F32)
    ewc = ctx.sb("l3ewc", [128, T + 1], F32)
    ST = ctx.sb("l3ST", [128, 64], F32)
    STb = ctx.sb("l3STb", [128, 64], BF16)
    kkc = ctx.sb("l3c_kk", [128, 128], BF16)
    agc = ctx.sb("l3c_ag", [128, 128], BF16)
    kc2 = ctx.sb("l3c_k2", [128, 128], BF16)
    rc = ctx.sb("l3c_r", [128, 128], BF16)
    vcf = ctx.sb("l3c_v", [128, 128], BF16)
    P1 = ctx.sb("l3P1", [128, 128], F32)
    Pp1 = ctx.sb("l3Pp1", [128, 128], F32)
    Pi1 = ctx.sb("l3Pi1", [128, 128], F32)
    nPp1 = ctx.sb("l3nP1", [128, 128], F32)
    P2h = ctx.sb("l3P2", [128, 64], F32)
    Pp2h = ctx.sb("l3Pp2", [128, 64], F32)
    Pi2h = ctx.sb("l3Pi2", [128, 64], F32)
    nPp2 = ctx.sb("l3nP2", [128, 64], F32)
    noff = ctx.sb("l3nof", [128, 2], F32)
    tmpa = ctx.sb("l3tma", [128, 128], F32)
    tmpb = ctx.sb("l3tmb", [128, 128], F32)
    tm2a = ctx.sb("l3t2a", [128, 64], F32)
    tm2b = ctx.sb("l3t2b", [128, 64], F32)
    ah1 = ctx.sb("l3ah1", [128, 128], BF16)
    bh1 = ctx.sb("l3bh1", [128, 128], BF16)
    kh1 = ctx.sb("l3kh1", [128, 128], BF16)
    rh1 = ctx.sb("l3rh1", [128, 128], BF16)
    bb_ = ctx.sb("l3bb", [128, 128], BF16)
    ah2 = ctx.sb("l3ah2", [128, 64], BF16)
    bh2 = ctx.sb("l3bh2", [128, 64], BF16)
    kh2 = ctx.sb("l3kh2", [128, 64], BF16)
    rh2 = ctx.sb("l3rh2", [128, 64], BF16)
    bbar = ctx.sb("l3bbr", [128, 128], BF16)
    kbar = ctx.sb("l3kbr", [128, 128], BF16)
    vtm = ctx.sb("l3vtm", [128, 128], BF16)
    btm = ctx.sb("l3btm", [128, 128], BF16)
    ktm = ctx.sb("l3ktm", [128, 128], BF16)
    Ms = ctx.sb("l3Ms", [128, 128], BF16)
    MTs = ctx.sb("l3MTs", [128, 128], BF16)
    NTs = ctx.sb("l3NTs", [128, 128], BF16)
    MrTs = ctx.sb("l3MrT", [128, 128], BF16)
    NrTs = ctx.sb("l3NrT", [128, 128], BF16)
    M2s = ctx.sb("l3M2s", [128, 128], BF16)
    MT2s = ctx.sb("l3MT2", [128, 128], BF16)
    IMp = ctx.sb("l3IMp", [128, 128], BF16)
    xsb = ctx.sb("l3xsb", [128, 64], BF16)
    Hsb = ctx.sb("l3Hsb", [128, 64], BF16)
    ysb = ctx.sb("l3ysb", [128, 64], BF16)
    for p in (() if "nowkv" in dbg else range(2)):
        ctx.dma("sync", ewin[:], ew_dr[p * 128:(p + 1) * 128, :], reads=("ew_dr",),
                writes=("l3ewi",))
        vop(ctx, "memset", ewc[:, 0:1], 0.0, writes=("l3ewc",))
        for sc in range(4):
            ini = 0.0 if sc == 0 else ewc[:, sc * 2048:sc * 2048 + 1]
            vop(ctx, "tensor_tensor_scan", ewc[:, 1 + sc * 2048:1 + (sc + 1) * 2048],
                ewin[:, sc * 2048:(sc + 1) * 2048], zz2k[:], ini, ALU.add, ALU.add,
                reads=("l3ewi", "l3zz", "l3ewc"), writes=("l3ewc",))
        vop(ctx, "memset", ST[:], 0.0, writes=("l3ST",))
        vop(ctx, "memset", STb[:], 0.0, writes=("l3STb",))
        for c in range(64):
            tok0 = c * 128
            pr = slice(p * 128, (p + 1) * 128)
            ctx.dma("sync", kkc[:], kkn_dr[pr, tok0:tok0 + 128], reads=("kkn_dr",),
                    writes=("l3c_kk",))
            ctx.dma("gpsimd", agc[:], ag_dr[pr, tok0:tok0 + 128], reads=("ag_dr",),
                    writes=("l3c_ag",))
            ctx.dma("sync", kc2[:], k2_dr[pr, tok0:tok0 + 128], reads=("k2_dr",),
                    writes=("l3c_k2",))
            ctx.dma("gpsimd", rc[:], r_dr[pr, tok0:tok0 + 128], reads=("r_dr",),
                    writes=("l3c_r",))
            ctx.dma("sync", vcf[:], v_dr[pr, tok0:tok0 + 128], reads=("v_dr",),
                    writes=("l3c_v",))
            off = ewc[:, tok0:tok0 + 1]
            off2 = ewc[:, tok0 + 64:tok0 + 65]
            vop(ctx, "tensor_scalar", tmpa[:], ewc[:, tok0 + 1:tok0 + 129], off,
                None, ALU.subtract, reads=("l3ewc",), writes=("l3tma",))
            vop(ctx, "tensor_scalar", tmpb[:], ewc[:, tok0:tok0 + 128], off,
                None, ALU.subtract, reads=("l3ewc",), writes=("l3tmb",))
            vop(ctx, "tensor_scalar", tm2a[:], ewc[:, tok0 + 65:tok0 + 129], off2,
                None, ALU.subtract, reads=("l3ewc",), writes=("l3t2a",))
            vop(ctx, "tensor_scalar", tm2b[:], ewc[:, tok0 + 64:tok0 + 128], off2,
                None, ALU.subtract, reads=("l3ewc",), writes=("l3t2b",))
            sact(ctx, P1[:], tmpa[:], AF.Exp, scale=-1.0, reads=("l3tma",),
                 writes=("l3P1",))
            sact(ctx, Pi1[:], tmpa[:], AF.Exp, reads=("l3tma",), writes=("l3Pi1",))
            sact(ctx, Pp1[:], tmpb[:], AF.Exp, scale=-1.0, reads=("l3tmb",),
                 writes=("l3Pp1",))
            sact(ctx, P2h[:], tm2a[:], AF.Exp, scale=-1.0, reads=("l3t2a",),
                 writes=("l3P2",))
            sact(ctx, Pi2h[:], tm2a[:], AF.Exp, reads=("l3t2a",), writes=("l3Pi2",))
            sact(ctx, Pp2h[:], tm2b[:], AF.Exp, scale=-1.0, reads=("l3t2b",),
                 writes=("l3Pp2",))
            vop(ctx, "tensor_scalar_mul", nPp1[:], Pp1[:], -1.0, eng="gpsimd",
                reads=("l3Pp1",), writes=("l3nP1",))
            vop(ctx, "tensor_scalar_mul", nPp2[:], Pp2h[:], -1.0, eng="gpsimd",
                reads=("l3Pp2",), writes=("l3nP2",))
            vop(ctx, "tensor_mul", ah1[:], kkc[:], nPp1[:], eng="gpsimd",
                reads=("l3c_kk", "l3nP1"), writes=("l3ah1",))
            vop(ctx, "tensor_mul", bb_[:], kkc[:], agc[:], eng="gpsimd",
                reads=("l3c_kk", "l3c_ag"), writes=("l3bb",))
            vop(ctx, "tensor_mul", bh1[:], bb_[:], Pi1[:], eng="gpsimd",
                reads=("l3bb", "l3Pi1"), writes=("l3bh1",))
            vop(ctx, "tensor_mul", kh1[:], kc2[:], Pi1[:], eng="gpsimd",
                reads=("l3c_k2", "l3Pi1"), writes=("l3kh1",))
            vop(ctx, "tensor_mul", rh1[:], rc[:], P1[:], eng="gpsimd",
                reads=("l3c_r", "l3P1"), writes=("l3rh1",))
            vop(ctx, "tensor_mul", ah2[:], kkc[:, 64:128], nPp2[:], eng="gpsimd",
                reads=("l3c_kk", "l3nP2"), writes=("l3ah2",))
            vop(ctx, "tensor_mul", bh2[:], bb_[:, 64:128], Pi2h[:], eng="gpsimd",
                reads=("l3bb", "l3Pi2"), writes=("l3bh2",))
            vop(ctx, "tensor_mul", kh2[:], kc2[:, 64:128], Pi2h[:], eng="gpsimd",
                reads=("l3c_k2", "l3Pi2"), writes=("l3kh2",))
            vop(ctx, "tensor_mul", rh2[:], rc[:, 64:128], P2h[:], eng="gpsimd",
                reads=("l3c_r", "l3P2"), writes=("l3rh2",))
            PL = P1[:, 127:128]
            vop(ctx, "tensor_scalar", bbar[:], bh1[:], PL, None, ALU.mult,
                reads=("l3bh1", "l3P1"), writes=("l3bbr",))
            vop(ctx, "tensor_scalar", kbar[:], kh1[:], PL, None, ALU.mult,
                reads=("l3kh1", "l3P1"), writes=("l3kbr",))
            for (src_, sk_, dst_, dk_) in ((vcf, "l3c_v", vtm, "l3vtm"),
                                           (bbar, "l3bbr", btm, "l3btm"),
                                           (kbar, "l3kbr", ktm, "l3ktm")):
                ctx.op("tensor", lambda t, src_=src_: t.transpose(
                    P["psT"][:, 0:128], src_[:], sb_id3[:]),
                    reads=(sk_, "l3id"), writes=("psT",))
                vop(ctx, "tensor_copy", dst_[:], P["psT"][:, 0:128],
                    reads=("psT",), writes=(dk_,))
            ctx.dma("sync", vtm_dr[tok0:tok0 + 128, p * 128:(p + 1) * 128], vtm[:],
                    reads=("l3vtm",), writes=("vtm_dr",))
            for h in range(2):
                hs = slice(h * 64, (h + 1) * 64)
                hc = slice(h * 64, (h + 1) * 64)
                pM = P["psA"][:, 0:128]
                pMT = P["psA"][:, 512:640]
                pNT = P["psA"][:, 1024:1152]
                pMrT = P["psA"][:, 1536:1664]
                pNrT = P["psB"][:, 0:128]
                for (pv, pk, A1, A2, B1, B2, msk, mk, dst_, dk_, du) in (
                    (pM, "psA0", ah1, ah2, bh1, bh2, mskL, "l3mkL", Ms, "l3Ms", 0),
                    (pMT, "psA1", bh1, bh2, ah1, ah2, mskU, "l3mkU", MTs, "l3MTs", 1),
                    (pNT, "psA2", kh1, kh2, ah1, ah2, mskU, "l3mkU", NTs, "l3NTs", 1),
                    (pMrT, "psA3", bh1, bh2, rh1, rh2, mskUI, "l3mkI", MrTs, "l3MrT", 1),
                    (pNrT, "psB0", kh1, kh2, rh1, rh2, mskUI, "l3mkI", NrTs, "l3NrT", 1),
                ):
                    rds = ("l3ah1", "l3ah2", "l3bh1", "l3bh2", "l3kh1", "l3kh2",
                           "l3rh1", "l3rh2")
                    mm(ctx, pv[0:64, 0:64], [(A1[hs, 0:64], B1[hs, 0:64])],
                       reads=rds, writes=(pk,))
                    mm(ctx, pv[64:128, 64:128], [(A2[hs, :], B2[hs, :])],
                       reads=rds, writes=(pk,))
                    if du == 0:
                        mm(ctx, pv[64:128, 0:64], [(A1[hs, 64:128], B1[hs, 0:64])],
                           reads=rds, writes=(pk,))
                    else:
                        mm(ctx, pv[0:64, 64:128], [(A1[hs, 0:64], B1[hs, 64:128])],
                           reads=rds, writes=(pk,))
                    vop(ctx, "tensor_mul", dst_[:], pv[0:128, 0:128], msk[:],
                        reads=(pk, mk), writes=(dk_,))
                pCa = P["psB"][:, 512:576]
                pCb = P["psB"][:, 1024:1088]
                mm(ctx, pCa, [(ah1[hs, :], STb[hs, :]), (NTs[:], vtm[:, hc])],
                   reads=("l3ah1", "l3STb", "l3NTs", "l3vtm"), writes=("psB1",))
                cur, oth = (pCa, "psB1"), (pCb, "psB2")
                curM, curMT, nxtM, nxtMT = (Ms, "l3Ms"), (MTs, "l3MTs"), \
                    (M2s, "l3M2s"), (MT2s, "l3MT2")
                for lev in range(7):
                    vop(ctx, "tensor_copy", xsb[:], cur[0], reads=(cur[1],),
                        writes=("l3xsb",))
                    vop(ctx, "tensor_add", IMp[:], curMT[0][:], sb_id3[:],
                        eng="gpsimd", reads=(curMT[1], "l3id"), writes=("l3IMp",))
                    mm(ctx, oth[0], [(IMp[:], xsb[:])],
                       reads=("l3IMp", "l3xsb"), writes=(oth[1],))
                    cur, oth = oth, cur
                    if lev < 6:
                        mm(ctx, pM, [(curMT[0][:], curM[0][:])],
                           reads=(curMT[1], curM[1]), writes=("psA0",))
                        vop(ctx, "tensor_copy", nxtM[0][:], pM[0:128, 0:128],
                            reads=("psA0",), writes=(nxtM[1],))
                        mm(ctx, pMT, [(curM[0][:], curMT[0][:])],
                           reads=(curM[1], curMT[1]), writes=("psA1",))
                        vop(ctx, "tensor_copy", nxtMT[0][:], pMT[0:128, 0:128],
                            reads=("psA1",), writes=(nxtMT[1],))
                        curM, curMT, nxtM, nxtMT = nxtM, nxtMT, curM, curMT
                vop(ctx, "tensor_copy", Hsb[:], cur[0], reads=(cur[1],),
                    writes=("l3Hsb",))
                pY = P["psA"][:, 1024:1088]
                mm(ctx, pY, [(rh1[hs, :], STb[hs, :]), (MrTs[:], Hsb[:]),
                             (NrTs[:], vtm[:, hc])],
                   reads=("l3rh1", "l3STb", "l3MrT", "l3NrT", "l3Hsb", "l3vtm"),
                   writes=("psA2",))
                vop(ctx, "tensor_copy", ysb[:], pY, reads=("psA2",), writes=("l3ysb",))
                ctx.dma("sync", y_dr[tok0:tok0 + 128, (2 * p + h) * 64:
                                     (2 * p + h + 1) * 64], ysb[:],
                        reads=("l3ysb",), writes=("y_dr",))
                pS = P["psA"][h * 64:(h + 1) * 64, 1536:1600]
                mm(ctx, pS, [(btm[:, hc], Hsb[:]), (ktm[:, hc], vtm[:, hc])],
                   reads=("l3btm", "l3ktm", "l3Hsb", "l3vtm"), writes=("psA3",))
                vop(ctx, "scalar_tensor_tensor", ST[hs, :], ST[hs, :],
                    P1[hs, 127:128], pS, ALU.mult, ALU.add,
                    reads=("l3ST", "l3P1", "psA3"), writes=("l3ST",))
            vop(ctx, "tensor_copy", STb[:], ST[:], reads=("l3ST",), writes=("l3STb",))
    ctx.free(["l3zz", "l3ewi", "l3ewc", "l3ST", "l3STb", "l3c_kk", "l3c_ag",
              "l3c_k2", "l3c_r", "l3c_v", "l3P1", "l3Pp1", "l3Pi1", "l3nP1",
              "l3P2", "l3Pp2", "l3Pi2", "l3nP2", "l3nof", "l3tma", "l3tmb", "l3t2a", "l3t2b", "l3ah1", "l3bh1",
              "l3kh1", "l3rh1", "l3bb", "l3ah2", "l3bh2", "l3kh2", "l3rh2",
              "l3bbr", "l3kbr", "l3vtm", "l3btm", "l3ktm", "l3Ms", "l3MTs",
              "l3NTs", "l3MrT", "l3NrT", "l3M2s", "l3MT2", "l3IMp", "l3xsb",
              "l3Hsb", "l3ysb"])

    EPS3 = 0.00064
    lnwr = AI("l3_lnw_my", [1, 256])
    lnbr = AI("l3_lnb_my", [1, 256])
    lnwrow = ldc("l3lwr", 1, 256, [(None, lnwr[:])], BF16, qn="gpsimd")
    lnbrow = ldc("l3lbr", 1, 256, [(None, lnbr[:])], BF16, qn="gpsimd")
    lnw_bc = ctx.sb("l3lwb", [128, 256], BF16)
    lnb_bc = ctx.sb("l3lbb", [128, 256], BF16)
    for (row, rk_, bc, bk_) in ((lnwrow, "l3lwr", lnw_bc, "l3lwb"),
                                (lnbrow, "l3lbr", lnb_bc, "l3lbb")):
        mm(ctx, P["psB"][:, 0:256], [(ones1[:], row[:])],
           reads=(rk_, "ones1"), writes=("psB0",))
        vop(ctx, "tensor_copy", bc[:], P["psB"][:, 0:256], reads=("psB0",),
            writes=(bk_,))
    WoT3 = AI("l3_WoT_my", [256, 1024])
    wosb = ldc("l3wos", 128, 2048, [((slice(None), slice(cb * 1024, (cb + 1) * 1024)),
               WoT3[cb * 128:(cb + 1) * 128, :]) for cb in range(2)], BF16, qn="gpsimd")
    ysb2 = ctx.sb("l3py", [128, 256], BF16)
    vtt = ctx.sb("l3pv", [128, 256], BF16)
    gtt = ctx.sb("l3pg", [128, 256], BF16)
    s0t = ctx.sb("l3ps0", [2, 128], BF16)
    s_tm = ctx.sb("l3psm", [128, 4], F32)
    sqt = ctx.sb("l3psq", [128, 256], F32)
    mcol = ctx.sb("l3pm1", [128, 4], F32)
    m2col = ctx.sb("l3pm2", [128, 4], F32)
    musq = ctx.sb("l3pmu", [128, 4], F32)
    varc = ctx.sb("l3pva", [128, 4], F32)
    ynt = ctx.sb("l3pyn", [128, 256], F32)
    ygb = ctx.sb("l3pyg", [128, 256], BF16)
    ygf = ctx.sb("l3pyf", [128, 256], BF16)
    wpe3 = ctx.sb("l3pwp", [128, 1024], BF16)
    for tt in (() if "nopost" in dbg else range(64)):
        tok0 = tt * 128
        ctx.dma("sync", ysb2[:], y_dr[tok0:tok0 + 128, :], reads=("y_dr",),
                writes=("l3py",))
        ctx.dma("gpsimd", vtt[:], vtm_dr[tok0:tok0 + 128, :], reads=("vtm_dr",),
                writes=("l3pv",))
        ctx.dma("gpsimd", gtt[:], g_dr[tok0:tok0 + 128, :], reads=("g_dr",),
                writes=("l3pg",))
        for m in range(2):
            ctx.dma("sync", s0t[:, 0:128] if m == 0 else s0t[:, 0:128],
                    s_dr[m, :, tok0:tok0 + 128], reads=("s_dr",), writes=("l3ps0",))
            ctx.op("tensor", lambda t: t.transpose(P["psT"][:, 0:2], s0t[:],
                                                   sb_id3[0:2, 0:2]),
                   reads=("l3ps0", "l3id"), writes=("psT",))
            vop(ctx, "tensor_copy", s_tm[:, m * 2:(m + 1) * 2], P["psT"][:, 0:2],
                reads=("psT",), writes=("l3psm",))
        sact(ctx, sqt[:], ysb2[:], AF.Square, reads=("l3py",), writes=("l3psq",))
        for h4 in range(4):
            seg = slice(h4 * 64, (h4 + 1) * 64)
            vop(ctx, "reduce_sum", mcol[:, h4:h4 + 1], ysb2[:, seg], AX.X,
                reads=("l3py",), writes=("l3pm1",))
            vop(ctx, "reduce_sum", m2col[:, h4:h4 + 1], sqt[:, seg], AX.X,
                reads=("l3psq",), writes=("l3pm2",))
        vop(ctx, "tensor_scalar_mul", mcol[:], mcol[:], 1.0 / 64,
            reads=("l3pm1",), writes=("l3pm1",))
        vop(ctx, "tensor_scalar_mul", m2col[:], m2col[:], 1.0 / 64,
            reads=("l3pm2",), writes=("l3pm2",))
        vop(ctx, "tensor_mul", musq[:], mcol[:], mcol[:], reads=("l3pm1",),
            writes=("l3pmu",))
        vop(ctx, "tensor_sub", varc[:], m2col[:], musq[:],
            reads=("l3pm2", "l3pmu"), writes=("l3pva",))
        vop(ctx, "tensor_scalar_add", varc[:], varc[:], EPS3, reads=("l3pva",),
            writes=("l3pva",))
        sact(ctx, varc[:], varc[:], AF.Ln, reads=("l3pva",), writes=("l3pva",))
        sact(ctx, varc[:], varc[:], AF.Exp, scale=-0.5, reads=("l3pva",),
             writes=("l3pva",))
        for h4 in range(4):
            seg = slice(h4 * 64, (h4 + 1) * 64)
            vop(ctx, "tensor_scalar", ynt[:, seg], ysb2[:, seg],
                mcol[:, h4:h4 + 1], None, ALU.subtract,
                reads=("l3py", "l3pm1"), writes=("l3pyn",))
            vop(ctx, "tensor_scalar", ynt[:, seg], ynt[:, seg],
                varc[:, h4:h4 + 1], None, ALU.mult,
                reads=("l3pyn", "l3pva"), writes=("l3pyn",))
        vop(ctx, "tensor_mul", ynt[:], ynt[:], lnw_bc[:], reads=("l3pyn", "l3lwb"),
            writes=("l3pyn",))
        vop(ctx, "tensor_add", ynt[:], ynt[:], lnb_bc[:], reads=("l3pyn", "l3lbb"),
            writes=("l3pyn",))
        for h4 in range(4):
            seg = slice(h4 * 64, (h4 + 1) * 64)
            vop(ctx, "scalar_tensor_tensor", ynt[:, seg], vtt[:, seg],
                s_tm[:, h4:h4 + 1], ynt[:, seg], ALU.mult, ALU.add,
                reads=("l3pv", "l3psm", "l3pyn"), writes=("l3pyn",))
        vop(ctx, "tensor_mul", ygb[:], ynt[:], gtt[:], reads=("l3pyn", "l3pg"),
            writes=("l3pyg",))
        for half in range(2):
            ctx.op("tensor", lambda t, half=half: t.transpose(
                P["psT"][:, 0:128], ygb[:, half * 128:(half + 1) * 128], sb_id3[:]),
                reads=("l3pyg", "l3id"), writes=("psT",))
            vop(ctx, "tensor_copy", ygf[:, half * 128:(half + 1) * 128],
                P["psT"][:, 0:128], reads=("psT",), writes=("l3pyf",))
        for hh in range(2):
            pv = [P["psA"][:, 0:512], P["psA"][:, 512:1024]][hh]
            pk = ["psA0", "psA1"][hh]
            mm(ctx, pv, [(ygf[:, cb * 128:(cb + 1) * 128],
                          wosb[:, cb * 1024 + hh * 512:cb * 1024 + (hh + 1) * 512])
                         for cb in range(2)],
               reads=("l3pyf", "l3wos"), writes=(pk,))
            vop(ctx, "tensor_copy", wpe3[:, hh * 512:(hh + 1) * 512], pv,
                reads=(pk,), writes=("l3pwp",))
        ctx.dma("sync", wop3_in[tok0:tok0 + 128, :], wpe3[:], reads=("l3pwp",),
                writes=("wop3_in",))
    ctx.collective("ReduceScatter", ALU.add, GRP, wop3_in, wop3_out,
                   reads=("wop3_in",), writes=("wop3_out",))
    wrow3 = ctx.sb("l3wr3", [128, 1024], BF16)
    for tt in range(16):
        ctx.dma("sync", wrow3[:], wop3_out[tt * 128:(tt + 1) * 128, :],
                reads=("wop3_out",), writes=("l3wr3",))
        for dm in range(8):
            ctx.op("tensor", lambda t, dm=dm: t.transpose(
                P["psT"][:, 0:128], wrow3[:, dm * 128:(dm + 1) * 128], sb_id3[:]),
                reads=("l3wr3", "l3id"), writes=("psT",))
            vop(ctx, "tensor_add", xt[dm][:, tt * 128:(tt + 1) * 128],
                xt[dm][:, tt * 128:(tt + 1) * 128], P["psT"][:, 0:128],
                reads=("psT", f"x{dm}"), writes=(f"x{dm}",))
    if "dbg3" in dbg:
        for nm, ap_, shp in (("dbg3_ew", ew_dr, [256, T]),):
            o = ctx.dram_t(nm, shp, F32, kind="ExternalOutput").ap()
            ctx.dma("sync", o, ap_, reads=())
        for nm, ap_, shp in (("dbg3_r", r_dr, [256, T]), ("dbg3_k2", k2_dr, [256, T]),
                             ("dbg3_kkn", kkn_dr, [256, T]), ("dbg3_ag", ag_dr, [256, T]),
                             ("dbg3_v", v_dr, [256, T]), ("dbg3_g", g_dr, [T, 256]),
                             ("dbg3_y", y_dr, [T, 256]), ("dbg3_wop", wop3_in, [T, 1024]),
                             ("dbg3_wout", wop3_out, [TLOC, 1024])):
            o = ctx.dram_t(nm, shp, BF16, kind="ExternalOutput").ap()
            ctx.dma("sync", o, ap_, reads=())
    ctx.free(["l3mkL", "l3mkU", "l3mkI", "l3lwr", "l3lbr", "l3lwb", "l3lbb",
              "l3wos", "l3py", "l3pv", "l3pg", "l3ps0", "l3psm", "l3psq",
              "l3pm1", "l3pm2", "l3pmu", "l3pva", "l3pyn", "l3pyg", "l3pyf",
              "l3pwp", "l3wr3"])


def emit_l3_ffn(ctx, P, xt, xn3f, xn3fk, we_sb, sb_id3, AI):
    W1T3 = AI("l3f_W1T", [8, 1024, 4096])
    W2T3 = AI("l3f_W2T", [8, 2048, 1024])
    act_dr = ctx.dram_t("l3_act_dr", [16, 128, TLOC], BF16).ap()
    moe3 = ctx.sb("l3moe", [128, 16 * 1024], BF16)
    w1b = ctx.sb("l3w1b", [128, 8 * 2048], BF16)
    w2b = w1b
    hgb = ctx.sb("l3hgb", [128, 8 * TLOC], BF16)
    sgt = ctx.sb("l3sgt", [128, 512], BF16)
    acb = ctx.sb("l3acb", [128, 512], BF16)
    for e in range(8):
        for half in range(2):
            for kb in range(8):
                ctx.dma("gpsimd", w1b[:, kb * 2048:(kb + 1) * 2048],
                        W1T3[e, kb * 128:(kb + 1) * 128,
                             half * 2048:(half + 1) * 2048], writes=("l3w1b",))
            for mp in range(8):
                for ci in range(4):
                    c0 = ci * 512
                    mm(ctx, P["psA"][:, 0:512],
                       [(w1b[:, kb * 2048 + mp * 256:kb * 2048 + mp * 256 + 128],
                         xn3f[kb][:, c0:c0 + 512]) for kb in range(8)],
                       reads=("l3w1b",) + tuple(xn3fk), writes=("psA0",))
                    mm(ctx, P["psA"][:, 512:1024],
                       [(w1b[:, kb * 2048 + mp * 256 + 128:kb * 2048 + (mp + 1) * 256],
                         xn3f[kb][:, c0:c0 + 512]) for kb in range(8)],
                       reads=("l3w1b",) + tuple(xn3fk), writes=("psA1",))
                    sact(ctx, sgt[:], P["psA"][:, 512:1024], AF.Silu,
                         reads=("psA1",), writes=("l3sgt",))
                    vop(ctx, "tensor_mul", acb[:], P["psA"][:, 0:512], sgt[:],
                        reads=("psA0", "l3sgt"), writes=("l3acb",))
                    ctx.dma("sync", act_dr[half * 8 + mp, :, c0:c0 + 512], acb[:],
                            reads=("l3acb",), writes=("act_dr",))
        for g2 in range(2):
            for j in range(8):
                ctx.dma("gpsimd", w2b[:, j * 1024:(j + 1) * 1024],
                        W2T3[e, (g2 * 8 + j) * 128:(g2 * 8 + j + 1) * 128, :],
                        writes=("l3w1b",))
                ctx.dma("sync", hgb[:, j * TLOC:(j + 1) * TLOC],
                        act_dr[g2 * 8 + j, :, :], reads=("act_dr",), writes=("l3hgb",))
            for tt in range(16):
                we_ap = we_sb[:, tt * 8 + e:tt * 8 + e + 1]
                for hf in range(2):
                    pp = (tt * 2 + hf) % 2
                    pk = ["psA0", "psA1"][pp]
                    pv = [P["psA"][:, 0:512], P["psA"][:, 512:1024]][pp]
                    mm(ctx, pv,
                       [(hgb[:, j * TLOC + tt * 128:j * TLOC + (tt + 1) * 128],
                         w2b[:, j * 1024 + hf * 512:j * 1024 + (hf + 1) * 512])
                        for j in range(8)],
                       reads=("l3w1b", "l3hgb"), writes=(pk,))
                    msl = moe3[:, tt * 1024 + hf * 512:tt * 1024 + (hf + 1) * 512]
                    if e == 0 and g2 == 0:
                        vop(ctx, "tensor_scalar", msl, pv, we_ap, None, ALU.mult,
                            reads=(pk, "l3we"), writes=("l3moe",))
                    else:
                        vop(ctx, "scalar_tensor_tensor", msl, pv, we_ap, msl,
                            ALU.mult, ALU.add,
                            reads=(pk, "l3we", "l3moe"), writes=("l3moe",))
    for tt in range(16):
        for dm in range(8):
            ctx.op("tensor", lambda t, tt=tt, dm=dm: t.transpose(
                P["psT"][:, 0:128],
                moe3[:, tt * 1024 + dm * 128:tt * 1024 + (dm + 1) * 128], sb_id3[:]),
                reads=("l3moe", "l3id"), writes=("psT",))
            vop(ctx, "tensor_add", xt[dm][:, tt * 128:(tt + 1) * 128],
                xt[dm][:, tt * 128:(tt + 1) * 128], P["psT"][:, 0:128],
                reads=("psT", f"x{dm}"), writes=(f"x{dm}",))
    ctx.free(["l3moe", "l3w1b", "l3hgb", "l3sgt", "l3acb"])


# version 02
# speedup vs baseline: 1.0064x; 1.0064x over previous
import numpy as np
import ml_dtypes
from contextlib import ExitStack

import concourse.bass as bass
import concourse.mybir as mybir
from concourse.bass_utils import run_bass_kernel_spmd

F32 = mybir.dt.float32
BF16 = mybir.dt.bfloat16
I32 = mybir.dt.int32
AF = mybir.ActivationFunctionType
ALU = mybir.AluOpType
AX = mybir.AxisListType

NCORES = 8
B = 2
T = 8192
D = 1024
TLOC = 2048
PAD = 8
WS = TLOC + PAD
NDT = D // 128
GRP = [[0, 1, 2, 3], [4, 5, 6, 7]]
GRP8 = [[0, 1, 2, 3, 4, 5, 6, 7]]


def chunks(total, lim, start=0):
    out = []
    c0 = start
    while c0 < start + total:
        w = min(lim, start + total - c0)
        out.append((c0, w))
        c0 += w
    return out


SELF_SYNC = ("vector", "scalar", "gpsimd")


class Ctx:
    ENGINES = ("sync", "tensor", "vector", "scalar", "gpsimd")
    DMAQ = ("sync", "gpsimd", "scalar")

    def __init__(self, nc):
        self.nc = nc
        self.es = ExitStack()
        self.q = {e: [] for e in self.ENGINES}
        self.sems = {}
        self.cnt = {e: 0 for e in self.ENGINES}
        self.dmacnt = {q: 0 for q in self.DMAQ}
        self.sbuf = {}
        self.dram = {}
        self.track_w = {}
        self.track_r = {}
        self.pending = {}
        self.uid = 0
        for e in self.ENGINES:
            self.sems[e] = self.es.enter_context(nc.semaphore(f"ms_{e}"))
        for qn in self.DMAQ:
            self.sems["dma_" + qn] = self.es.enter_context(nc.semaphore(f"dq_{qn}"))
        self.sems["cc"] = self.es.enter_context(nc.semaphore("cc_sem"))
        self.cccnt = 0
        self.applied = {e: {} for e in self.ENGINES}

    def sb(self, name, shape, dtype, side="left"):
        if name not in self.sbuf:
            self.uid += 1
            cm = self.nc.sbuf_tensor(f"{name}__{self.uid}", list(shape), dtype,
                                     side=side)
            h = cm.__enter__()
            self.sbuf[name] = (cm, h, self.uid)
        return self.sbuf[name][1]

    def rename(self, old, new):
        self.sbuf[new] = self.sbuf.pop(old)
        if old in self.track_w:
            self.track_w[new] = self.track_w.pop(old)
        if old in self.track_r:
            self.track_r[new] = self.track_r.pop(old)

    def free(self, names):
        present = [(self.sbuf[n][2], n) for n in names if n in self.sbuf]
        for _, n in sorted(present, reverse=True):
            cm, h, _u = self.sbuf.pop(n)
            cm.__exit__(None, None, None)
            if n in self.track_w:
                src, v = self.track_w.pop(n)
                self.pending[src] = max(self.pending.get(src, 0), v)
            for src, v in self.track_r.pop(n, {}).items():
                self.pending[src] = max(self.pending.get(src, 0), v)

    def dram_t(self, name, shape, dtype, **kw):
        if name not in self.dram:
            self.dram[name] = self.nc.dram_tensor(name, list(shape), dtype, **kw)
        return self.dram[name]

    def psum_t(self, name, shape, dtype=F32):
        if name not in self.sbuf:
            self.uid += 1
            cm = self.nc.psum_tensor(name, list(shape), dtype)
            h = cm.__enter__()
            self.sbuf[name] = (cm, h, self.uid)
        return self.sbuf[name][1]

    def _wait_into(self, waits, src, v):
        waits[src] = max(waits.get(src, 0), v)

    def _emit_waits(self, eng, waits):
        ap = self.applied[eng]
        for src, v in waits.items():
            if src == eng and eng not in SELF_SYNC:
                continue
            if ap.get(src, -1) >= v:
                continue
            sem = self.sems[src]

            def f(e, sem=sem, v=v):
                e.wait_ge(sem, v)
            self.q[eng].append(f)
            ap[src] = v

    def _bump_dma(self, src, v):
        if src.startswith("dma_"):
            return self.dmacnt[src[4:]]
        return v

    def _pre(self, eng, reads, writes):
        waits = {}
        for src, v in self.pending.items():
            self._wait_into(waits, src, self._bump_dma(src, v))
        for b in reads:
            if b in self.track_w:
                src, v = self.track_w[b]
                self._wait_into(waits, src, self._bump_dma(src, v))
        for b in writes:
            if b in self.track_w:
                src, v = self.track_w[b]
                self._wait_into(waits, src, self._bump_dma(src, v))
            for src, v in self.track_r.get(b, {}).items():
                self._wait_into(waits, src, self._bump_dma(src, v))
        self._emit_waits(eng, waits)

    def _post(self, key, reads, writes, val):
        for b in reads:
            self.track_r.setdefault(b, {})[key] = val
        for b in writes:
            self.track_w[b] = (key, val)
            self.track_r[b] = {}

    def op(self, eng, fn, reads=(), writes=()):
        self._pre(eng, reads, writes)
        self.cnt[eng] += 1
        sem = self.sems[eng]

        def f(e, fn=fn, sem=sem):
            r = fn(e)
            ins = r[-1] if isinstance(r, (list, tuple)) else r
            ins.then_inc(sem, 1)
        self.q[eng].append(f)
        self._post(eng, reads, writes, self.cnt[eng])
        return self.cnt[eng]

    def dma(self, qn, out_ap, in_ap, reads=(), writes=(), **kw):
        self._pre(qn, reads, writes)
        self.dmacnt[qn] += 16
        sem = self.sems["dma_" + qn]
        tgt = self.dmacnt[qn]

        def f(e, out_ap=out_ap, in_ap=in_ap, sem=sem, kw=kw):
            e.dma_start(out=out_ap, in_=in_ap, **kw).then_inc(sem, 16)
        self.q[qn].append(f)
        self._post("dma_" + qn, reads, writes, tgt)
        return tgt

    def idma(self, out_ap, in_, in_off_ap, reads=(), writes=(), **kw):
        qn = "gpsimd"
        self._pre(qn, reads, writes)
        self.dmacnt[qn] += 16
        sem = self.sems["dma_" + qn]
        tgt = self.dmacnt[qn]

        def f(e, out_ap=out_ap, in_=in_, in_off_ap=in_off_ap, sem=sem, kw=kw):
            e.indirect_dma_start(
                out=out_ap, out_offset=None, in_=in_,
                in_offset=bass.IndirectOffsetOnAxis(ap=in_off_ap, axis=0),
                **kw).then_inc(sem, 16)
        self.q[qn].append(f)
        self._post("dma_" + qn, reads, writes, tgt)
        return tgt

    def collective(self, kind, alu, groups, in_ap, out_ap, reads=(), writes=()):
        self._pre("gpsimd", reads, writes)
        self.cccnt += 1
        sem = self.sems["cc"]
        tgt = self.cccnt

        def f(e, in_ap=in_ap, out_ap=out_ap, sem=sem):
            e.collective_compute(kind, alu, replica_groups=groups,
                                 ins=[in_ap], outs=[out_ap]).then_inc(sem, 1)
        self.q["gpsimd"].append(f)
        self._post("cc", reads, writes, tgt)
        return tgt

    def drain_all(self):
        waits = {}
        for qn in self.DMAQ:
            if self.dmacnt[qn]:
                self._wait_into(waits, "dma_" + qn, self.dmacnt[qn])
        if self.cccnt:
            self._wait_into(waits, "cc", self.cccnt)
        for e in ("tensor", "vector", "scalar", "gpsimd"):
            if self.cnt[e]:
                self._wait_into(waits, e, self.cnt[e])
        self._emit_waits("sync", waits)

    def finalize(self, block):
        for eng in self.ENGINES:
            ops = list(self.q[eng])

            def run(engine, ops=ops):
                for f in ops:
                    f(engine)
            getattr(block, eng)(run)


def vop(ctx, meth, out_ap, *args, eng="vector", reads=(), writes=(), **kw):
    return ctx.op(eng, lambda e: getattr(e, meth)(out_ap, *args, **kw),
                  reads=reads, writes=writes)


def sact(ctx, out_ap, in_ap, func, reads=(), writes=(), **kw):
    return ctx.op("scalar", lambda e: e.activation(out_ap, in_ap, func, **kw),
                  reads=reads, writes=writes)


def mm(ctx, psum_ap, parts, reads=(), writes=(), start=True, stop=True):
    n = len(parts)

    def f(t):
        insts = []
        for i, (l, r) in enumerate(parts):
            insts.append(t.matmul(psum_ap, l, r,
                                  start=(start and i == 0),
                                  stop=(stop and i == n - 1)))
        return insts
    return ctx.op("tensor", f, reads=reads, writes=writes)


def mmn(ctx, items, reads=(), writes=()):
    def f(t):
        return [t.matmul(p, l, r, start=True, stop=True) for (p, l, r) in items]
    return ctx.op("tensor", f, reads=reads, writes=writes)



def _t(a):
    return np.ascontiguousarray(np.asarray(a).T)


def _c(a):
    return np.ascontiguousarray(np.asarray(a))


def host_prep(inputs):
    per_core = [{} for _ in range(NCORES)]
    shared = {}
    x = np.asarray(inputs["L0_attn_x"])
    for c in range(NCORES):
        b, r = c // 4, c % 4
        t0 = r * TLOC
        xs = np.zeros((D, WS), np.float32)
        xs[:, PAD:] = x[b, t0:t0 + TLOC, :].T
        if r > 0:
            xs[:, PAD - 3:PAD] = x[b, t0 - 3:t0, :].T
        per_core[c]["x_fm"] = _c(xs)
        m = np.zeros((128, 3), np.float32)
        if r >= 1:
            m[:, r - 1] = 1.0
        per_core[c]["scan_sel"] = m
    shared["l0_inW"] = _c(inputs["L0_attn_in_W"])
    shared["l0_inb"] = _c(inputs["L0_attn_in_b"]).reshape(4096, 1)
    shared["l0_rmsw"] = _c(inputs["L0_attn_rms_w"]).reshape(D, 1)
    shared["l0_convw"] = _c(inputs["L0_attn_conv_w"])
    shared["l0_convb"] = _c(inputs["L0_attn_conv_b"]).reshape(2048, 1)
    shared["l0_Alog"] = _c(inputs["L0_attn_A_log"])
    shared["l0_dbcW"] = _c(inputs["L0_attn_dbc_W"])
    shared["l0_dbcb"] = _c(inputs["L0_attn_dbc_b"]).reshape(160, 1)
    shared["l0_dupW"] = _c(inputs["L0_attn_dup_W"])
    shared["l0_dupb"] = _c(inputs["L0_attn_dup_b"]).reshape(2048, 1)
    shared["l0_outW"] = _c(inputs["L0_attn_out_W"])
    shared["l0_outb"] = _c(inputs["L0_attn_out_b"]).reshape(D, 1)
    shared["l0_WrT"] = _t(inputs["L0_ffn_Wr"])
    shared["l0_br"] = _c(np.broadcast_to(
        np.asarray(inputs["L0_ffn_br"]).reshape(1, 8), (128, 8)))
    shared["l0_W1T"] = _c(np.transpose(np.asarray(inputs["L0_ffn_W1"]), (0, 2, 1)))
    shared["l0_W2T"] = _c(np.transpose(np.asarray(inputs["L0_ffn_W2"]), (0, 2, 1)))
    shared["l0_b1"] = _c(inputs["L0_ffn_b1"])
    shared["l0_b2"] = _c(inputs["L0_ffn_b2"])
    H1 = 16
    shared["l1_RT"] = None
    R64 = np.zeros((64, 64), np.float32)
    for j in range(32):
        R64[2 * j, 2 * j + 1] = -1.0
        R64[2 * j + 1, 2 * j] = 1.0
    Rp = np.zeros((128, 128), np.float32)
    Rp[0:64, 0:64] = R64
    Rp[64:128, 64:128] = R64
    shared["l1_RT"] = _c(Rp.T)
    half1 = 32
    invf1 = 1.0 / 10000.0 ** (np.arange(half1) / half1)
    rows = invf1[(np.arange(64) // 2)]
    ang1 = np.arange(T)[None, :] * np.concatenate([rows, rows])[:, None]
    shared["l1_cosT"] = _c(np.cos(ang1).astype(np.float32))
    shared["l1_sinT"] = _c(np.sin(ang1).astype(np.float32))
    cm = np.zeros((128, 4 * 512), np.float32)
    for m in range(4):
        for p in range(128):
            cm[p, m * 512:(m + 1) * 512] = (np.arange(512) >= m * 128 + p)
    shared["l1_cmask"] = _c(cm)
    shared["l1_bo"] = _c(inputs["L1_attn_bo"]).reshape(D, 1)
    shared["l1_cw2"] = _c(np.asarray(inputs["L1_ffn_combine_weights"], np.float32))
    top = np.asarray(inputs["L1_ffn_top_idx"]).astype(np.int64)
    Rmax = 1
    rounds_per_core = []
    for c in range(NCORES):
        b, r = c // 4, c % 4
        t0 = r * TLOC + b * T
        tbl = [[] for _ in range(TLOC)]
        if b == 0:
            for e in range(8):
                col = top[:, e]
                for slot, tokg in enumerate(col):
                    if t0 <= tokg < t0 + TLOC:
                        tbl[tokg - t0].append(e * TLOC + slot)
        Rmax = max(Rmax, max((len(x) for x in tbl), default=1))
        rounds_per_core.append(tbl)
    OOB = NCORES * TLOC
    for c in range(NCORES):
        tbl = rounds_per_core[c]
        rix = np.full((Rmax, TLOC), OOB, np.int64)
        for t_loc, lst in enumerate(tbl):
            for ri, s in enumerate(lst):
                rix[ri, t_loc] = s
        per_core[c]["l1_rix"] = _c(rix.reshape(Rmax, 16, 128).astype(np.int32))
        per_core[c]["l1_gidx"] = _c(top[:, c].astype(np.int32).reshape(16, 128))
        es_ = np.zeros((128, 8), np.float32)
        es_[:, c] = 1.0
        per_core[c]["l1_esel"] = es_
        per_core[c]["l1_W1_my"] = _c(np.asarray(inputs["L1_ffn_W1"])[c])
        per_core[c]["l1_W2_my"] = _c(np.asarray(inputs["L1_ffn_W2"])[c])
        per_core[c]["l1_b1_my"] = _c(np.asarray(inputs["L1_ffn_b1"])[c]).reshape(1, 4096)
        per_core[c]["l1_b2_my"] = _c(np.asarray(inputs["L1_ffn_b2"])[c]).reshape(1, 1024)
        r_ = c % 4
        per_core[c]["l1_WqT_my"] = _c(_t(inputs["L1_attn_Wq"])[:, r_ * 256:(r_ + 1) * 256])
        per_core[c]["l1_WkT_my"] = _c(_t(inputs["L1_attn_Wk"])[:, r_ * 256:(r_ + 1) * 256])
        per_core[c]["l1_WvT_my"] = _c(_t(inputs["L1_attn_Wv"])[:, r_ * 256:(r_ + 1) * 256])
        per_core[c]["l1_WoT_my"] = _c(_t(inputs["L1_attn_Wo"])[r_ * 256:(r_ + 1) * 256, :])
        per_core[c]["l1_bq_my"] = _c(np.asarray(inputs["L1_attn_bq"])[r_ * 256:(r_ + 1) * 256].reshape(2, 128).T)
        per_core[c]["l1_bk_my"] = _c(np.asarray(inputs["L1_attn_bk"])[r_ * 256:(r_ + 1) * 256].reshape(2, 128).T)
        per_core[c]["l1_bv_my"] = _c(np.asarray(inputs["L1_attn_bv"])[r_ * 256:(r_ + 1) * 256].reshape(1, 256))
    shared["l2_lnw"] = _c(inputs["L2_attn_ln_w"]).reshape(D, 1)
    shared["l2_lnb"] = _c(inputs["L2_attn_ln_b"]).reshape(D, 1)
    shared["l2_gnw"] = _c(inputs["L2_attn_g_norm_w"]).reshape(1, 512)
    shared["l2_w1"] = _c(inputs["L2_ffn_w1"])
    shared["l2_w1T"] = _t(inputs["L2_ffn_w1"])
    shared["l2_w2"] = _c(inputs["L2_ffn_w2"])
    shared["l2_w2T"] = _t(inputs["L2_ffn_w2"])
    shared["l2_b1r"] = _c(inputs["L2_ffn_b1"]).reshape(1, 4096)
    shared["l2_b2"] = _c(inputs["L2_ffn_b2"]).reshape(D, 1)
    for c in range(NCORES):
        h = c % 4
        per_core[c]["l2_Wq_my"] = _c(np.asarray(inputs["L2_attn_Wq"])[:, h * 256:(h + 1) * 256])
        per_core[c]["l2_Wk_my"] = _c(np.asarray(inputs["L2_attn_Wk"])[:, h * 256:(h + 1) * 256])
        per_core[c]["l2_Wv_my"] = _c(np.asarray(inputs["L2_attn_Wv"])[:, h * 512:(h + 1) * 512])
        per_core[c]["l2_Wg_my"] = _c(np.asarray(inputs["L2_attn_Wg"])[:, h * 512:(h + 1) * 512])
        per_core[c]["l2_Wo_my"] = _c(np.asarray(inputs["L2_attn_Wo"])[h * 512:(h + 1) * 512, :])
        gamma = 1.0 - 2.0 ** (-5.0 - h)
        i = np.arange(64)
        Dm_ = np.where(i[:, None] >= i[None, :], gamma ** (i[:, None] - i[None, :]), 0.0)
        per_core[c]["l2_DmaskT"] = _c(Dm_.T.astype(np.float32))
        per_core[c]["l2_crossT"] = _c(np.tile((gamma ** (i + 1.0)).astype(np.float32)[None, :], (128, 1)))
        per_core[c]["l2_kdec"] = _c((gamma ** (64 - 1.0 - i)).astype(np.float32).reshape(64, 1))
        per_core[c]["l2_gC"] = np.full((128, 1), gamma ** 64, np.float32)
    invf2 = 1.0 / 10000.0 ** (np.arange(128) * 2.0 / 256.0)
    tpos = np.arange(T)
    ang2 = tpos[None, :] * invf2[:, None]
    shared["l2_cosT"] = _c(np.cos(ang2).astype(np.float32))
    shared["l2_sinT"] = _c(np.sin(ang2).astype(np.float32))
    g3 = lambda n: np.asarray(inputs[n], np.float32)
    shared["l3_w1"] = _c(g3("L3_attn_w1"))
    shared["l3_a1"] = _c(g3("L3_attn_a1"))
    shared["l3_g1"] = _c(g3("L3_attn_g1"))
    for nm in ("x_r", "x_w", "x_k", "x_v", "x_a", "x_g", "w0", "a0", "k_k", "k_a"):
        shared[f"l3_{nm}"] = _c(g3(f"L3_attn_{nm}").reshape(D, 1))
    shared["l3_lnb_r"] = _c(g3("L3_attn_ln_b").reshape(1, D))
    shared["l3_lnw_r"] = _c(g3("L3_attn_ln_w").reshape(1, D))
    mskU = np.triu(np.ones((128, 128), np.float32), 1)
    mskUI = np.triu(np.ones((128, 128), np.float32), 0)
    shared["l3_mskU"] = _c(mskU)
    shared["l3_mskUI"] = _c(mskUI)
    shared["l3_mskL"] = _c(np.tril(np.ones((128, 128), np.float32), -1))
    hselT = np.zeros((2, 128), np.float32)
    hselT[0, 0:64] = 1.0
    hselT[1, 64:128] = 1.0
    shared["l3_hsel"] = _c(hselT)
    shared["l3_hselC"] = _c(hselT.T)
    shared["l3f_WrT"] = _c(_t(np.asarray(inputs["L3_ffn_Wr"], np.float32)))
    W1_3 = np.asarray(inputs["L3_ffn_W1"], np.float32)
    perm = np.empty(4096, np.int64)
    for mp in range(16):
        perm[mp * 256:mp * 256 + 128] = np.arange(mp * 128, (mp + 1) * 128)
        perm[mp * 256 + 128:(mp + 1) * 256] = 2048 + np.arange(mp * 128, (mp + 1) * 128)
    shared["l3f_W1T"] = _c(W1_3.transpose(0, 2, 1)[:, :, perm])
    W2_3 = np.asarray(inputs["L3_ffn_W2"], np.float32)
    shared["l3f_W2T"] = _c(W2_3.transpose(0, 2, 1))
    for c in range(NCORES):
        r_ = c % 4
        cs = slice(r_ * 256, (r_ + 1) * 256)
        per_core[c]["l3_Wr_my"] = _c(g3("L3_attn_Wr")[:, cs])
        per_core[c]["l3_Wk_my"] = _c(g3("L3_attn_Wk")[:, cs])
        per_core[c]["l3_Wv_my"] = _c(g3("L3_attn_Wv")[:, cs])
        per_core[c]["l3_w2_my"] = _c(g3("L3_attn_w2")[:, cs])
        per_core[c]["l3_a2_my"] = _c(g3("L3_attn_a2")[:, cs])
        per_core[c]["l3_g2_my"] = _c(g3("L3_attn_g2")[:, cs])
        per_core[c]["l3_WoT_my"] = _c(g3("L3_attn_Wo")[cs, :])
        per_core[c]["l3_w0_my"] = _c(g3("L3_attn_w0").reshape(-1)[cs].reshape(2, 128).T)
        per_core[c]["l3_a0_my"] = _c(g3("L3_attn_a0").reshape(-1)[cs].reshape(2, 128).T)
        per_core[c]["l3_kk_my"] = _c(g3("L3_attn_k_k").reshape(-1)[cs].reshape(2, 128).T)
        per_core[c]["l3_ka_my"] = _c(g3("L3_attn_k_a").reshape(-1)[cs].reshape(2, 128).T)
        rk = g3("L3_attn_r_k").reshape(16, 64)[r_ * 4:(r_ + 1) * 4].reshape(-1)
        per_core[c]["l3_rk_my"] = _c(rk.reshape(2, 128).T)
        per_core[c]["l3_lnw_my"] = _c(g3("L3_attn_ln_w").reshape(-1)[cs].reshape(1, 256))
        per_core[c]["l3_lnb_my"] = _c(g3("L3_attn_ln_b").reshape(-1)[cs].reshape(1, 256))
    seln = np.zeros((16, 16 * 128), np.float32)
    for n in range(16):
        seln[n, n * 128:(n + 1) * 128] = 1.0
    shared["seln"] = seln
    shared["ident"] = np.eye(128, dtype=np.float32)
    return per_core, shared



def fm_norm(ctx, P, src, srckeys, dst, dstkeys, eps, sub_mean, mul_pp, mulkey,
            width=WS, lo=None, lokeys=None, aff=None, affkey=None, src_f32=True):
    sq = [ctx.sb(f"nsq{j}", [128, 512], BF16) for j in range(2)]
    ones_col = ctx.sb("ones_col", [128, 1], BF16)
    ones_colf = ctx.sb("ones_colf", [128, 1], F32)
    ones1 = ctx.sb("ones1", [1, 128], BF16)
    s1 = ctx.sb("ns1", [1, WS], F32)
    s2 = ctx.sb("ns2", [1, WS], F32)
    for ci, (c0, w) in enumerate(chunks(width, 512)):
        for i in range(NDT):
            jk = f"nsq{i % 2}"
            sact(ctx, sq[i % 2][:, 0:w], src[i][:, c0:c0 + w], AF.Square,
                 reads=(srckeys[i],), writes=(jk,))
            mm(ctx, P["psB"][0:1, 1024:1024 + w], [(ones_col[:], sq[i % 2][:, 0:w])],
               reads=(jk, "ones_col"), writes=("psB2",),
               start=(i == 0), stop=(i == NDT - 1))
        vop(ctx, "tensor_copy", s2[:, c0:c0 + w], P["psB"][0:1, 1024:1024 + w],
            reads=("psB2",), writes=("ns2",))
        if sub_mean:
            oc = ones_colf if src_f32 else ones_col
            mm(ctx, P["psB"][0:1, 1024:1024 + w],
               [(oc[:], src[i][:, c0:c0 + w]) for i in range(NDT)],
               reads=tuple(srckeys) + ("ones_colf", "ones_col"), writes=("psB2",))
            vop(ctx, "tensor_copy", s1[:, c0:c0 + w], P["psB"][0:1, 1024:1024 + w],
                reads=("psB2",), writes=("ns1",))
    rstd = ctx.sb("nrstd", [1, WS], F32)
    tmp = ctx.sb("ntmp", [1, WS], F32)
    if sub_mean:
        vop(ctx, "tensor_scalar_mul", s1[:, 0:width], s1[:, 0:width], 1.0 / D,
            reads=("ns1",), writes=("ns1",))
        vop(ctx, "tensor_mul", tmp[:, 0:width], s1[:, 0:width], s1[:, 0:width],
            reads=("ns1",), writes=("ntmp",))
        vop(ctx, "scalar_tensor_tensor", rstd[:, 0:width], s2[:, 0:width], 1.0 / D,
            tmp[:, 0:width], ALU.mult, ALU.subtract,
            reads=("ns2", "ntmp"), writes=("nrstd",))
    else:
        vop(ctx, "tensor_scalar_mul", rstd[:, 0:width], s2[:, 0:width], 1.0 / D,
            reads=("ns2",), writes=("nrstd",))
    vop(ctx, "tensor_scalar_add", rstd[:, 0:width], rstd[:, 0:width], eps,
        reads=("nrstd",), writes=("nrstd",))
    sact(ctx, rstd[:, 0:width], rstd[:, 0:width], AF.Ln, reads=("nrstd",),
         writes=("nrstd",))
    sact(ctx, rstd[:, 0:width], rstd[:, 0:width], AF.Exp, scale=-0.5,
         reads=("nrstd",), writes=("nrstd",))
    mubf = ctx.sb("nmubf", [1, WS], BF16)
    rsbf = ctx.sb("nrsbf", [1, WS], BF16)
    if sub_mean:
        vop(ctx, "tensor_copy", mubf[:, 0:width], s1[:, 0:width], reads=("ns1",),
            writes=("nmubf",))
    vop(ctx, "tensor_copy", rsbf[:, 0:width], rstd[:, 0:width], reads=("nrstd",),
        writes=("nrsbf",))
    mu_bc = ctx.sb("nmubc", [128, WS], BF16)
    rs_bc = ctx.sb("nrsbc", [128, WS], BF16)
    for c0, w in chunks(width, 512):
        if sub_mean:
            mm(ctx, P["psB"][:, 0:w], [(ones1[:], mubf[:, c0:c0 + w])],
               reads=("nmubf", "ones1"), writes=("psB0",))
            vop(ctx, "tensor_copy", mu_bc[:, c0:c0 + w], P["psB"][:, 0:w],
                reads=("psB0",), writes=("nmubc",))
        mm(ctx, P["psB"][:, 512:512 + w], [(ones1[:], rsbf[:, c0:c0 + w])],
           reads=("nrsbf", "ones1"), writes=("psB1",))
        vop(ctx, "tensor_copy", rs_bc[:, c0:c0 + w], P["psB"][:, 512:512 + w],
            reads=("psB1",), writes=("nrsbc",))
    tmpn = ctx.sb("nnorm", [128, WS], F32)
    tmpf = ctx.sb("nnorm2", [128, WS], F32)
    for i in range(NDT):
        if sub_mean:
            vop(ctx, "tensor_sub", tmpn[:, 0:width], src[i][:, 0:width],
                mu_bc[:, 0:width], reads=(srckeys[i], "nmubc"), writes=("nnorm",))
            if aff is not None:
                vop(ctx, "tensor_mul", tmpf[:, 0:width], tmpn[:, 0:width],
                    rs_bc[:, 0:width], reads=("nnorm", "nrsbc"), writes=("nnorm2",))
                vop(ctx, "tensor_scalar", dst[i][:, 0:width], tmpf[:, 0:width],
                    aff[0][i], aff[1][i], ALU.mult, ALU.add,
                    reads=("nnorm2", affkey), writes=(dstkeys[i],))
            elif lo is None:
                vop(ctx, "tensor_mul", dst[i][:, 0:width], tmpn[:, 0:width],
                    rs_bc[:, 0:width], reads=("nnorm", "nrsbc"), writes=(dstkeys[i],))
            else:
                vop(ctx, "tensor_mul", tmpf[:, 0:width], tmpn[:, 0:width],
                    rs_bc[:, 0:width], reads=("nnorm", "nrsbc"), writes=("nnorm2",))
                vop(ctx, "tensor_copy", dst[i][:, 0:width], tmpf[:, 0:width],
                    reads=("nnorm2",), writes=(dstkeys[i],))
                vop(ctx, "tensor_sub", lo[i][:, 0:width], tmpf[:, 0:width],
                    dst[i][:, 0:width], reads=("nnorm2", dstkeys[i]),
                    writes=(lokeys[i],))
        else:
            vop(ctx, "scalar_tensor_tensor", dst[i][:, 0:width], src[i][:, 0:width],
                mul_pp[i], rs_bc[:, 0:width], ALU.mult, ALU.mult,
                reads=(srckeys[i], "nrsbc", mulkey), writes=(dstkeys[i],))
    ctx.free(["nsq0", "nsq1", "ns1", "ns2", "nrstd", "ntmp", "nmubf", "nrsbf",
              "nmubc", "nrsbc", "nnorm", "nnorm2"])



def emit_l0(ctx, P, x_in, dbg=()):
    AI = lambda name, shape, dt=F32: ctx.dram_t(name, shape, dt, kind="ExternalInput").ap()
    inW = AI("l0_inW", [1024, 4096])
    inb = AI("l0_inb", [4096, 1])
    rmsw = AI("l0_rmsw", [D, 1])
    convw = AI("l0_convw", [2048, 4])
    convb = AI("l0_convb", [2048, 1])
    Alog = AI("l0_Alog", [2048, 16])
    dbcW = AI("l0_dbcW", [2048, 160])
    dbcb = AI("l0_dbcb", [160, 1])
    dupW = AI("l0_dupW", [128, 2048])
    dupb = AI("l0_dupb", [2048, 1])
    outW = AI("l0_outW", [2048, 1024])
    outb = AI("l0_outb", [D, 1])
    WrT = AI("l0_WrT", [1024, 8])
    br_in = AI("l0_br", [128, 8])
    W1T = AI("l0_W1T", [8, 1024, 4096])
    W2T = AI("l0_W2T", [8, 4096, 1024])
    b1_in = AI("l0_b1", [8, 4096])
    b2_in = AI("l0_b2", [8, 1024])
    seln_in = AI("seln", [16, 2048])
    ssel_in = AI("scan_sel", [128, 3])
    ident_in = AI("ident", [128, 128])

    x1_dr = ctx.dram_t("l0_x1_dr", [16, 128, WS], F32).ap()
    x1s_dr = ctx.dram_t("l0_x1s_dr", [16, 128, TLOC], F32).ap()
    x1sb_dr = ctx.dram_t("l0_x1sb_dr", [16, 128, TLOC], BF16).ap()
    gs_dr = ctx.dram_t("l0_gs_dr", [16, 128, TLOC], BF16).ap()
    delta_dr = ctx.dram_t("l0_delta_dr", [16, 128, TLOC], F32).ap()
    y1_dr = ctx.dram_t("l0_y1_dr", [16, 128, TLOC], F32).ap()
    y2_dr = ctx.dram_t("l0_y2_dr", [16, 128, TLOC], F32).ap()
    h_dr = ctx.dram_t("l0_h_dr", [32, 128, TLOC], BF16).ap()
    mix_dr = ctx.dram_t("l0_mix_dr", [8, 128, TLOC], F32).ap()
    cc_in = ctx.dram_t("l0_cc_in", [128, 1024], F32).ap()
    cc_out = ctx.dram_t("l0_cc_out", [512, 1024], F32).ap()

    def ldc(name, rows, cols, src_slices, dt=F32, qn="sync"):
        tl = ctx.sb(name, [rows, cols], dt)
        for (dst_sl, src_ap) in src_slices:
            ctx.dma(qn, tl[dst_sl] if dst_sl else tl[:], src_ap, writes=(name,))
        return tl

    def col_slices(src, n, per=1):
        return [((slice(None), slice(i * per, (i + 1) * per)),
                 src[i * 128:(i + 1) * 128, :]) for i in range(n)]

    sb_rmsw = ldc("rmsw", 128, NDT, col_slices(rmsw, NDT))
    sb_inb = ldc("inb", 128, 32, col_slices(inb, 32))
    sb_convw = ldc("convw", 128, 64, col_slices(convw, 16, 4))
    sb_convb = ldc("convb", 128, 16, col_slices(convb, 16))
    sb_Alog = ldc("Alog", 128, 256, col_slices(Alog, 16, 16))
    sb_dbcb = ldc("dbcb", 128, 1, [(None, dbcb[0:128, :])])
    sb_bcbB = ldc("bcbB", 16, 1, [(None, dbcb[128:144, :])])
    sb_bcbC = ldc("bcbC", 16, 1, [(None, dbcb[144:160, :])])
    sb_dupb = ldc("dupb", 128, 16, col_slices(dupb, 16))
    sb_outb = ldc("outb", 128, NDT, col_slices(outb, NDT))
    sb_seln = ldc("c_seln", 16, 2048, [(None, seln_in[:])], BF16, qn="gpsimd")
    sb_ssel = ldc("ssel", 128, 3, [(None, ssel_in[:])])
    sb_br = ldc("br", 128, 8, [(None, br_in[:])])
    sb_WrT = ldc("sWrT", 128, 64,
                 [((slice(None), slice(k * 8, k * 8 + 8)), WrT[k * 128:(k + 1) * 128, :])
                  for k in range(8)], BF16, qn="gpsimd")
    sb_ident = ldc("c_ident", 128, 128, [(None, ident_in[:])], BF16, qn="gpsimd")

    sb_A = ctx.sb("A", [128, 256], F32)
    sact(ctx, sb_A[:], sb_Alog[:], AF.Exp, reads=("Alog",), writes=("A",))
    ctx.op("scalar", lambda e: e.mul(sb_A[:], sb_A[:], -1.0), reads=("A",), writes=("A",))

    xn = [ctx.sb(f"l0xn{i}", [128, WS], BF16, side="right") for i in range(NDT)]
    xlo = [ctx.sb(f"l0lo{i}", [128, WS], BF16) for i in range(NDT)]
    xt = [ctx.sb(f"xs{i}", [128, WS], F32) for i in range(NDT)]
    for i in range(NDT):
        ctx.dma("sync", xt[i][:], x_in[i * 128:(i + 1) * 128, :], writes=(f"x{i}",))
    xkeys = [f"x{i}" for i in range(NDT)]
    xnkeys = [f"l0xn{i}" for i in range(NDT)]
    hkeys = [f"l0hm{i}" for i in range(NDT)]
    lokeys = [f"l0lo{i}" for i in range(NDT)]
    fm_norm(ctx, P, xt, xkeys, xn, xnkeys, 1e-5, True, None, None,
            lo=xlo, lokeys=lokeys)
    ctx.free([f"xs{i}" for i in range(NDT)])
    meq = ctx.sb("meq", [128, 16 * 8], F32, side="right")
    sWrL = ctx.sb("sWrL", [128, 64], BF16)
    WrTf = ctx.sb("WrTf", [128, 64], F32)
    for k in range(8):
        ctx.dma("sync", WrTf[:, k * 8:(k + 1) * 8], WrT[k * 128:(k + 1) * 128, :],
                writes=("WrTf",))
    vop(ctx, "tensor_sub", sWrL[:], WrTf[:], sb_WrT[:], reads=("WrTf", "sWrT"),
        writes=("sWrL",))
    lg = ctx.sb("lgt", [128, 8], F32)
    mx8 = ctx.sb("mx8", [128, 8], F32)
    for tt in range(16):
        parts = []
        for k in range(8):
            xh = xn[k][:, PAD + tt * 128:PAD + (tt + 1) * 128]
            xl = xlo[k][:, PAD + tt * 128:PAD + (tt + 1) * 128]
            parts.append((xh, sb_WrT[:, k * 8:(k + 1) * 8]))
            parts.append((xh, sWrL[:, k * 8:(k + 1) * 8]))
            parts.append((xl, sb_WrT[:, k * 8:(k + 1) * 8]))
        mm(ctx, P["psB"][:, 0:8], parts,
           reads=tuple(xnkeys) + tuple(lokeys) + ("sWrT", "sWrL"), writes=("psB0",))
        vop(ctx, "tensor_add", lg[:], P["psB"][0:128, 0:8], sb_br[:],
            reads=("psB0", "br"), writes=("lgt",))
        vop(ctx, "max", mx8[:], lg[:], reads=("lgt",), writes=("mx8",))
        vop(ctx, "tensor_scalar", meq[:, tt * 8:(tt + 1) * 8], lg[:], mx8[:, 0:1],
            None, ALU.is_equal, reads=("lgt", "mx8"), writes=("meq",))
    ctx.free(["lgt", "mx8", "sWrL", "WrTf"])
    ctx.free([f"l0lo{i}" for i in range(NDT)])
    hmx = [ctx.sb(f"l0hm{i}", [128, WS], BF16, side="right") for i in range(NDT)]
    fm_norm(ctx, P, xn, xnkeys, hmx, hkeys, 1e-6, False,
            [sb_rmsw[:, i:i + 1] for i in range(NDT)], "rmsw")

    if "xn" in dbg:
        o = ctx.dram_t("dbg_xn", [NDT * 128, WS], BF16, kind="ExternalOutput").ap()
        for i in range(NDT):
            ctx.dma("sync", o[i * 128:(i + 1) * 128, :], xn[i][:], reads=(xnkeys[i],))

    wbuf = [ctx.sb(f"wb{p}", [128, 8 * 1024], BF16) for p in range(2)]
    evac = [ctx.sb(f"ev{p}", [128, 512], F32) for p in range(2)]
    gbf = [ctx.sb(f"gbf{p}", [128, 512], BF16) for p in range(2)]
    CH = chunks(WS, 512)
    for mb in range(4):
        wk = f"wb{mb % 2}"
        for k in range(8):
            ctx.dma("gpsimd", wbuf[mb % 2][:, k * 1024:(k + 1) * 1024],
                    inW[k * 128:(k + 1) * 128, mb * 1024:(mb + 1) * 1024], writes=(wk,))
        for mi in range(8):
            m_ = mb * 8 + mi
            for ci, (c0, w) in enumerate(CH):
                pp = (mi * len(CH) + ci) % 2
                pk = ["psA0", "psA1"][pp]
                pv = [P["psA"][:, 0:512], P["psA"][:, 512:1024]][pp]
                ek = f"ev{pp}"
                mm(ctx, pv[:, 0:w],
                   [(wbuf[mb % 2][:, k * 1024 + mi * 128:k * 1024 + (mi + 1) * 128],
                     hmx[k][:, c0:c0 + w]) for k in range(8)],
                   reads=(wk,) + tuple(hkeys), writes=(pk,))
                if m_ < 16:
                    ctx.op("scalar", lambda e, pv=pv, pp=pp, w=w, m_=m_:
                           e.add(evac[pp][:, 0:w], pv[:, 0:w], sb_inb[:, m_:m_ + 1]),
                           reads=(pk, "inb"), writes=(ek,))
                    ctx.dma("sync", x1_dr[m_, :, c0:c0 + w], evac[pp][:, 0:w],
                            reads=(ek,), writes=("x1_dr",))
                else:
                    sact(ctx, evac[pp][:, 0:w], pv[:, 0:w], AF.Silu,
                         bias=sb_inb[:, m_:m_ + 1], reads=(pk, "inb"), writes=(ek,))
                    gk = f"gbf{pp}"
                    vop(ctx, "tensor_copy", gbf[pp][:, 0:w], evac[pp][:, 0:w],
                        reads=(ek,), writes=(gk,))
                    lo = max(c0, PAD)
                    if lo < c0 + w:
                        ctx.dma("sync", gs_dr[m_ - 16, :, lo - PAD:c0 + w - PAD],
                                gbf[pp][:, lo - c0:w], reads=(gk,), writes=("gs_dr",))
    ctx.free(["wb0", "wb1", "ev0", "ev1", "gbf0", "gbf1"])
    ctx.free([f"l0hm{i}" for i in range(NDT)])

    cx = ctx.sb("cx", [128, WS], F32)
    cacc = ctx.sb("cacc", [128, TLOC], F32)
    cs = ctx.sb("cs", [128, TLOC], F32)
    csb = ctx.sb("csb", [128, TLOC], BF16)
    for i in range(16):
        ctx.dma("sync", cx[:], x1_dr[i, :, :], reads=("x1_dr",), writes=("cx",))
        vop(ctx, "tensor_scalar", cacc[:], cx[:, 5:5 + TLOC],
            sb_convw[:, i * 4:i * 4 + 1], None, ALU.mult,
            reads=("cx", "convw"), writes=("cacc",))
        for kk in range(1, 4):
            vop(ctx, "scalar_tensor_tensor", cacc[:], cx[:, 5 + kk:5 + kk + TLOC],
                sb_convw[:, i * 4 + kk:i * 4 + kk + 1], cacc[:], ALU.mult, ALU.add,
                reads=("cx", "convw", "cacc"), writes=("cacc",))
        sact(ctx, cs[:], cacc[:], AF.Silu, bias=sb_convb[:, i:i + 1],
             reads=("cacc", "convb"), writes=("cs",))
        vop(ctx, "tensor_copy", csb[:], cs[:], reads=("cs",), writes=("csb",),
            eng="gpsimd")
        ctx.dma("sync", x1s_dr[i, :, :], cs[:], reads=("cs",), writes=("x1s_dr",))
        ctx.dma("sync", x1sb_dr[i, :, :], csb[:], reads=("csb",), writes=("x1sb_dr",))
    ctx.free(["cx", "cacc", "cs", "csb"])

    sb_dbcW = ctx.sb("dbcWs", [128, 16 * 160], BF16)
    for k in range(16):
        ctx.dma("gpsimd", sb_dbcW[:, k * 160:(k + 1) * 160],
                dbcW[k * 128:(k + 1) * 128, :], writes=("dbcWs",))
    Bsb = ctx.sb("Bsb", [16, TLOC], BF16, side="right")
    Csb = ctx.sb("Csb", [16, TLOC], BF16, side="right")
    dpre = ctx.sb("dpre", [128, TLOC], BF16, side="right")
    xall = ctx.sb("xall", [128, 16 * 512], BF16)
    for ci, (c0, w) in enumerate(chunks(TLOC, 512)):
        for k in range(16):
            ctx.dma("sync", xall[:, k * 512:k * 512 + w], x1sb_dr[k, :, c0:c0 + w],
                    reads=("x1sb_dr",), writes=("xall",))
        mm(ctx, P["psA"][:, 0:w],
           [(sb_dbcW[:, k * 160:k * 160 + 128], xall[:, k * 512:k * 512 + w])
            for k in range(16)], reads=("dbcWs", "xall"), writes=("psA0",))
        ctx.op("scalar", lambda e, w=w, c0=c0:
               e.add(dpre[:, c0:c0 + w], P["psA"][:, 0:w], sb_dbcb[:, 0:1]),
               reads=("psA0", "dbcb"), writes=("dpre",))
        mm(ctx, P["psB"][0:16, 0:w],
           [(sb_dbcW[:, k * 160 + 128:k * 160 + 144], xall[:, k * 512:k * 512 + w])
            for k in range(16)], reads=("dbcWs", "xall"), writes=("psB0",))
        ctx.op("scalar", lambda e, w=w, c0=c0:
               e.add(Bsb[:, c0:c0 + w], P["psB"][0:16, 0:w], sb_bcbB[:, 0:1]),
               reads=("psB0", "bcbB"), writes=("Bsb",))
        mm(ctx, P["psB"][0:16, 512:512 + w],
           [(sb_dbcW[:, k * 160 + 144:k * 160 + 160], xall[:, k * 512:k * 512 + w])
            for k in range(16)], reads=("dbcWs", "xall"), writes=("psB1",))
        ctx.op("scalar", lambda e, w=w, c0=c0:
               e.add(Csb[:, c0:c0 + w], P["psB"][0:16, 512:512 + w], sb_bcbC[:, 0:1]),
               reads=("psB1", "bcbC"), writes=("Csb",))
    ctx.free(["dbcWs", "xall"])

    sb_dupW = ctx.sb("dupWs", [128, 2048], BF16)
    ctx.dma("gpsimd", sb_dupW[:], dupW[:], writes=("dupWs",))
    tt_ = ctx.sb("spt", [128, 512], F32)
    ab_ = ctx.sb("spa", [128, 512], F32)
    ee_ = ctx.sb("spe", [128, 512], F32)
    rr_ = ctx.sb("spr", [128, 512], F32)
    dd_ = ctx.sb("spd", [128, 512], F32)
    for m_ in range(16):
        for ci, (c0, w) in enumerate(chunks(TLOC, 512)):
            mm(ctx, P["psA"][:, 0:w],
               [(sb_dupW[:, m_ * 128:(m_ + 1) * 128], dpre[:, c0:c0 + w])],
               reads=("dupWs", "dpre"), writes=("psA0",))
            vop(ctx, "tensor_scalar_add", tt_[:, 0:w], P["psA"][:, 0:w],
                sb_dupb[:, m_:m_ + 1], reads=("psA0", "dupb"), writes=("spt",))
            sact(ctx, ab_[:, 0:w], tt_[:, 0:w], AF.Abs,
                 reads=("spt",), writes=("spa",))
            sact(ctx, ee_[:, 0:w], ab_[:, 0:w], AF.Exp, scale=-1.0,
                 reads=("spa",), writes=("spe",))
            sact(ctx, ee_[:, 0:w], ee_[:, 0:w], AF.Ln, bias=1.0,
                 reads=("spe",), writes=("spe",))
            sact(ctx, rr_[:, 0:w], tt_[:, 0:w], AF.Relu,
                 reads=("spt",), writes=("spr",))
            vop(ctx, "tensor_add", dd_[:, 0:w], ee_[:, 0:w], rr_[:, 0:w],
                eng="gpsimd", reads=("spe", "spr"), writes=("spd",))
            ctx.dma("sync", delta_dr[m_, :, c0:c0 + w], dd_[:, 0:w],
                    reads=("spd",), writes=("delta_dr",))
    ctx.free(["dupWs", "spt", "spa", "spe", "spr", "spd"])
    ctx.free(["dpre"])

    HW = 1024
    bcB = ctx.sb("bcB", [128, 16 * HW], BF16)
    bcC = ctx.sb("bcC", [128, 16 * HW], BF16)
    sstate = ctx.sb("sstate", [128, 256], F32)
    cstate = ctx.sb("cstate", [128, 16], F32)
    sdh = ctx.sb("sdh", [128, 2], F32)
    delt = ctx.sb("delt", [128, HW], F32)
    x1sf = ctx.sb("x1sf", [128, HW], F32)
    dxb = ctx.sb("dxb", [128, HW], BF16)
    a_s = ctx.sb("a_s", [128, HW], F32)
    b_s = ctx.sb("b_s", [128, HW], BF16)
    hs = ctx.sb("hs", [128, HW], F32)
    ya = ctx.sb("ya", [128, HW], F32)
    ymt = ctx.sb("ymt", [128, HW], F32)
    fin = ctx.sb("fin", [128, 16], F32)
    cpk = ctx.sb("cpk", [128, 16], F32)

    def build_bc(half, which_b, which_c):
        for n in range(16):
            for sub in range(2):
                c0 = half * HW + sub * 512
                if which_b:
                    mm(ctx, P["psA"][:, sub * 512:sub * 512 + 512],
                       [(sb_seln[:, n * 128:(n + 1) * 128], Bsb[:, c0:c0 + 512])],
                       reads=("c_seln", "Bsb"), writes=(f"psA{sub}",))
                if which_c:
                    mm(ctx, P["psA"][:, 1024 + sub * 512:1024 + sub * 512 + 512],
                       [(sb_seln[:, n * 128:(n + 1) * 128], Csb[:, c0:c0 + 512])],
                       reads=("c_seln", "Csb"), writes=(f"psA{2 + sub}",))
            if which_b:
                vop(ctx, "tensor_copy", bcB[:, n * HW:(n + 1) * HW],
                    P["psA"][:, 0:HW], reads=("psA0", "psA1"), writes=("bcB",))
            if which_c:
                vop(ctx, "tensor_copy", bcC[:, n * HW:(n + 1) * HW],
                    P["psA"][:, 1024:1024 + HW], reads=("psA2", "psA3"), writes=("bcC",))

    for half in range(2):
        build_bc(half, True, True)
        for dt in range(16):
            ctx.dma("sync", delt[:], delta_dr[dt, :, half * HW:(half + 1) * HW],
                    reads=("delta_dr",), writes=("delt",))
            ctx.dma("sync", x1sf[:], x1s_dr[dt, :, half * HW:(half + 1) * HW],
                    reads=("x1s_dr",), writes=("x1sf",))
            vop(ctx, "tensor_mul", dxb[:], delt[:], x1sf[:],
                reads=("delt", "x1sf"), writes=("dxb",))
            vop(ctx, "reduce_sum", sdh[:, half:half + 1], delt[:], AX.X,
                reads=("delt",), writes=("sdh",))
            for n in range(16):
                sact(ctx, a_s[:], delt[:], AF.Exp,
                     scale=sb_A[:, dt * 16 + n:dt * 16 + n + 1],
                     reads=("delt", "A"), writes=("a_s",))
                vop(ctx, "tensor_mul", b_s[:], dxb[:], bcB[:, n * HW:(n + 1) * HW],
                    eng="gpsimd", reads=("dxb", "bcB"), writes=("b_s",))
                ini = 0.0 if half == 0 else sstate[:, dt * 16 + n:dt * 16 + n + 1]
                rd = ("a_s", "b_s") if half == 0 else ("a_s", "b_s", "sstate")
                vop(ctx, "tensor_tensor_scan", hs[:], a_s[:], b_s[:], ini,
                    ALU.mult, ALU.add, reads=rd, writes=("hs",))
                ctx.op("scalar", lambda e, dt=dt, n=n:
                       e.copy(sstate[:, dt * 16 + n:dt * 16 + n + 1],
                              hs[:, HW - 1:HW]),
                       reads=("hs",), writes=("sstate",))
                vop(ctx, "tensor_mul", ymt[:], hs[:], bcC[:, n * HW:(n + 1) * HW],
                    reads=("hs", "bcC"), writes=("ymt",))
                if n == 0:
                    vop(ctx, "tensor_copy", ya[:], ymt[:], eng="gpsimd",
                        reads=("ymt",), writes=("ya",))
                else:
                    vop(ctx, "tensor_add", ya[:], ya[:], ymt[:], eng="gpsimd",
                        reads=("ymt", "ya"), writes=("ya",))
            ctx.dma("sync", y1_dr[dt, :, half * HW:(half + 1) * HW], ya[:],
                    reads=("ya",), writes=("y1_dr",))
            if half == 1:
                vop(ctx, "tensor_add", sdh[:, 0:1], sdh[:, 0:1], sdh[:, 1:2],
                    reads=("sdh",), writes=("sdh",))
                for n in range(16):
                    sact(ctx, cpk[:, n:n + 1], sdh[:, 0:1], AF.Exp,
                         scale=sb_A[:, dt * 16 + n:dt * 16 + n + 1],
                         reads=("sdh", "A"), writes=("cpk",))
                vop(ctx, "tensor_copy", fin[:],
                    sstate[:, dt * 16:(dt + 1) * 16], reads=("sstate",),
                    writes=("fin",))
                ctx.dma("sync", cc_in[:, dt * 32:dt * 32 + 16], fin[:],
                        reads=("fin",), writes=("cc_in",))
                ctx.dma("sync", cc_in[:, dt * 32 + 16:dt * 32 + 32], cpk[:],
                        reads=("cpk",), writes=("cc_in",))

    ctx.collective("AllGather", ALU.bypass, GRP, cc_in, cc_out,
                   reads=("cc_in",), writes=("cc_out",))
    h0 = ctx.sb("h0", [128, 16 * 16], F32)
    s1_ = ctx.sb("cs1", [128, 16], F32)
    s2_ = ctx.sb("cs2", [128, 16], F32)
    s3_ = ctx.sb("cs3", [128, 16], F32)
    fj = [ctx.sb(f"fj{j}", [128, 16], F32) for j in range(3)]
    pj = [ctx.sb(f"pj{j}", [128, 16], F32) for j in range(3)]
    for dt in range(16):
        for j in range(3):
            ctx.dma("sync", fj[j][:], cc_out[j * 128:(j + 1) * 128, dt * 32:dt * 32 + 16],
                    reads=("cc_out",), writes=(f"fj{j}",))
            ctx.dma("sync", pj[j][:],
                    cc_out[j * 128:(j + 1) * 128, dt * 32 + 16:dt * 32 + 32],
                    reads=("cc_out",), writes=(f"pj{j}",))
        vop(ctx, "tensor_copy", s1_[:], fj[0][:], reads=("fj0",), writes=("cs1",))
        vop(ctx, "tensor_mul", s2_[:], pj[1][:], s1_[:], reads=("pj1", "cs1"),
            writes=("cs2",))
        vop(ctx, "tensor_add", s2_[:], s2_[:], fj[1][:], reads=("cs2", "fj1"),
            writes=("cs2",))
        vop(ctx, "tensor_mul", s3_[:], pj[2][:], s2_[:], reads=("pj2", "cs2"),
            writes=("cs3",))
        vop(ctx, "tensor_add", s3_[:], s3_[:], fj[2][:], reads=("cs3", "fj2"),
            writes=("cs3",))
        hsl = h0[:, dt * 16:(dt + 1) * 16]
        vop(ctx, "tensor_scalar", hsl, s1_[:], sb_ssel[:, 0:1], None, ALU.mult,
            reads=("cs1", "ssel"), writes=("h0",))
        vop(ctx, "scalar_tensor_tensor", hsl, s2_[:], sb_ssel[:, 1:2], hsl,
            ALU.mult, ALU.add, reads=("cs2", "ssel", "h0"), writes=("h0",))
        vop(ctx, "scalar_tensor_tensor", hsl, s3_[:], sb_ssel[:, 2:3], hsl,
            ALU.mult, ALU.add, reads=("cs3", "ssel", "h0"), writes=("h0",))
    ctx.free(["cs1", "cs2", "cs3", "fj0", "fj1", "fj2", "pj0", "pj1", "pj2"])

    cd = ctx.sb("cdt", [128, HW], F32)
    ec = ctx.sb("ec", [128, HW], F32)
    wcb = ctx.sb("wcb", [128, HW], F32)
    y2 = ctx.sb("y2", [128, HW], F32)
    zz = ctx.sb("zz", [128, HW], F32)
    vop(ctx, "memset", zz[:], 0.0, writes=("zz",))
    for half in range(2):
        build_bc(half, False, True)
        for dt in range(16):
            ctx.dma("sync", delt[:], delta_dr[dt, :, half * HW:(half + 1) * HW],
                    reads=("delta_dr",), writes=("delt",))
            ini = 0.0 if half == 0 else cstate[:, dt:dt + 1]
            rd = ("delt", "zz") if half == 0 else ("delt", "zz", "cstate")
            vop(ctx, "tensor_tensor_scan", cd[:], delt[:], zz[:], ini,
                ALU.add, ALU.add, reads=rd, writes=("cdt",))
            ctx.op("scalar", lambda e, dt=dt: e.copy(cstate[:, dt:dt + 1],
                                                     cd[:, HW - 1:HW]),
                   reads=("cdt",), writes=("cstate",))
            for n in range(16):
                sact(ctx, ec[:], cd[:], AF.Exp,
                     scale=sb_A[:, dt * 16 + n:dt * 16 + n + 1],
                     reads=("cdt", "A"), writes=("ec",))
                vop(ctx, "tensor_mul", wcb[:], ec[:], bcC[:, n * HW:(n + 1) * HW],
                    eng="gpsimd", reads=("ec", "bcC"), writes=("wcb",))
                if n == 0:
                    vop(ctx, "tensor_scalar", y2[:], wcb[:],
                        h0[:, dt * 16 + n:dt * 16 + n + 1], None, ALU.mult,
                        reads=("wcb", "h0"), writes=("y2",))
                else:
                    vop(ctx, "scalar_tensor_tensor", y2[:], wcb[:],
                        h0[:, dt * 16 + n:dt * 16 + n + 1], y2[:], ALU.mult, ALU.add,
                        reads=("wcb", "h0", "y2"), writes=("y2",))
            ctx.dma("sync", y2_dr[dt, :, half * HW:(half + 1) * HW], y2[:],
                    reads=("y2",), writes=("y2_dr",))
    ctx.free(["bcB", "bcC", "sstate", "cstate", "sdh", "delt", "x1sf", "dxb",
              "a_s", "b_s", "hs", "ya", "ymt", "fin", "cpk", "h0", "cdt", "ec",
              "wcb", "y2", "zz"])
    ctx.free(["dpre", "Csb", "Bsb"])

    sb_outW = ctx.sb("outWs", [128, 16 * 1024], BF16)
    for k in range(16):
        ctx.dma("gpsimd", sb_outW[:, k * 1024:(k + 1) * 1024],
                outW[k * 128:(k + 1) * 128, :], writes=("outWs",))
    ygb = ctx.sb("ygb", [128, 16 * 512], BF16)
    y1t = ctx.sb("y1t", [128, 512], F32)
    y2t = ctx.sb("y2t", [128, 512], F32)
    gst = ctx.sb("gst", [128, 512], BF16)
    ysum = ctx.sb("ysum", [128, 512], F32)
    mxev = [ctx.sb(f"mxev{p}", [128, 512], F32) for p in range(2)]
    for ci, (c0, w) in enumerate(chunks(TLOC, 512)):
        for k in range(16):
            ctx.dma("sync", y1t[:, 0:w], y1_dr[k, :, c0:c0 + w], reads=("y1_dr",),
                    writes=("y1t",))
            ctx.dma("sync", y2t[:, 0:w], y2_dr[k, :, c0:c0 + w], reads=("y2_dr",),
                    writes=("y2t",))
            ctx.dma("sync", gst[:, 0:w], gs_dr[k, :, c0:c0 + w], reads=("gs_dr",),
                    writes=("gst",))
            vop(ctx, "tensor_add", ysum[:, 0:w], y1t[:, 0:w], y2t[:, 0:w],
                reads=("y1t", "y2t"), writes=("ysum",))
            vop(ctx, "tensor_mul", ygb[:, k * 512:k * 512 + w], ysum[:, 0:w],
                gst[:, 0:w], reads=("ysum", "gst"), writes=("ygb",))
        for m_ in range(8):
            pp = m_ % 2
            pv = [P["psA"][:, 0:512], P["psA"][:, 512:1024]][pp]
            pk = ["psA0", "psA1"][pp]
            ek = f"mxev{pp}"
            mm(ctx, pv[:, 0:w],
               [(sb_outW[:, k * 1024 + m_ * 128:k * 1024 + (m_ + 1) * 128],
                 ygb[:, k * 512:k * 512 + w]) for k in range(16)],
               reads=("outWs", "ygb"), writes=(pk,))
            ctx.op("scalar", lambda e, pv=pv, pp=pp, w=w, m_=m_:
                   e.add(mxev[pp][:, 0:w], pv[:, 0:w], sb_outb[:, m_:m_ + 1]),
                   reads=(pk, "outb"), writes=(ek,))
            ctx.dma("sync", mix_dr[m_, :, c0:c0 + w], mxev[pp][:, 0:w],
                    reads=(ek,), writes=("mix_dr",))
    ctx.free(["outWs", "ygb", "y1t", "y2t", "gst", "ysum", "mxev0", "mxev1"])


    onesr = ctx.sb("onesr", [1, 512], BF16)
    moe = ctx.sb("moe", [128, 16 * 1024], BF16, side="right")
    yac = ctx.sb("yac", [128, 16 * 1024], BF16, side="right")
    b1row = ctx.sb("b1row", [1, 4096], BF16)
    b2row = ctx.sb("b2row", [1, 1024], BF16)

    for e_ in range(8):
        ctx.dma("gpsimd", b1row[:], b1_in[e_:e_ + 1, :], writes=("b1row",))
        ctx.dma("gpsimd", b2row[:], b2_in[e_:e_ + 1, :], writes=("b2row",))
        wbuf = [ctx.sb(f"wb{p}", [128, 8 * 1024], BF16) for p in range(2)]
        hst = [ctx.sb(f"hst{p}", [128, 512], BF16) for p in range(2)]
        for mb in range(4):
            wk = f"wb{mb % 2}"
            for k in range(8):
                ctx.dma("gpsimd", wbuf[mb % 2][:, k * 1024:(k + 1) * 1024],
                        W1T[e_, k * 128:(k + 1) * 128, mb * 1024:(mb + 1) * 1024],
                        writes=(wk,))
            for mi in range(8):
                m_ = mb * 8 + mi
                for ci, (c0, w) in enumerate(chunks(TLOC, 512)):
                    pp = (mi * 4 + ci) % 2
                    pk = ["psA0", "psA1"][pp]
                    pv = [P["psA"][:, 0:512], P["psA"][:, 512:1024]][pp]
                    parts = [(wbuf[mb % 2][:, k * 1024 + mi * 128:k * 1024 + (mi + 1) * 128],
                              xn[k][:, PAD + c0:PAD + c0 + w]) for k in range(8)]
                    parts.append((b1row[:, m_ * 128:(m_ + 1) * 128], onesr[:, 0:w]))
                    mm(ctx, pv[:, 0:w], parts,
                       reads=(wk,) + tuple(xnkeys) + ("onesr", "b1row"), writes=(pk,))
                    hk = f"hst{pp}"
                    sact(ctx, hst[pp][:, 0:w], pv[:, 0:w], AF.Relu,
                         reads=(pk,), writes=(hk,))
                    ctx.dma("sync", h_dr[m_, :, c0:c0 + w], hst[pp][:, 0:w],
                            reads=(hk,), writes=("h_dr",))
        ctx.free(["wb0", "wb1", "hst0", "hst1"])
        hgb = ctx.sb("hgb", [128, 8 * TLOC], BF16)
        w2b = ctx.sb("w2b", [128, 8 * 1024], BF16)
        for g_ in range(4):
            for j in range(8):
                ctx.dma("gpsimd", w2b[:, j * 1024:(j + 1) * 1024],
                        W2T[e_, (g_ * 8 + j) * 128:(g_ * 8 + j + 1) * 128, :],
                        writes=("w2b",))
                ctx.dma("sync", hgb[:, j * TLOC:(j + 1) * TLOC],
                        h_dr[g_ * 8 + j, :, :], reads=("h_dr",), writes=("hgb",))
            for tt in range(16):
                for hf in range(2):
                    pp = (tt * 2 + hf) % 2
                    pk = ["psA0", "psA1"][pp]
                    pv = [P["psA"][:, 0:512], P["psA"][:, 512:1024]][pp]
                    parts = [(hgb[:, j * TLOC + tt * 128:j * TLOC + (tt + 1) * 128],
                              w2b[:, j * 1024 + hf * 512:j * 1024 + (hf + 1) * 512])
                             for j in range(8)]
                    if g_ == 3:
                        parts.append((onesr[:, 0:128],
                                      b2row[:, hf * 512:(hf + 1) * 512]))
                    mm(ctx, pv[:, 0:512], parts,
                       reads=("w2b", "hgb", "onesr", "b2row"), writes=(pk,))
                    ys = yac[:, tt * 1024 + hf * 512:tt * 1024 + (hf + 1) * 512]
                    if g_ == 0:
                        vop(ctx, "tensor_copy", ys, pv[:, 0:512],
                            reads=(pk,), writes=("yac",))
                    else:
                        vop(ctx, "tensor_add", ys, ys, pv[:, 0:512],
                            reads=(pk, "yac"), writes=("yac",))
        ctx.free(["hgb", "w2b"])
        for tt in range(16):
            msl = moe[:, tt * 1024:(tt + 1) * 1024]
            ysl = yac[:, tt * 1024:(tt + 1) * 1024]
            if e_ == 0:
                vop(ctx, "tensor_scalar", msl, ysl, meq[:, tt * 8 + e_:tt * 8 + e_ + 1],
                    None, ALU.mult, reads=("yac", "meq"), writes=("moe",))
            else:
                vop(ctx, "scalar_tensor_tensor", msl, ysl,
                    meq[:, tt * 8 + e_:tt * 8 + e_ + 1], msl, ALU.mult, ALU.add,
                    reads=("yac", "meq", "moe"), writes=("moe",))
    if "dbg2" not in dbg:
        ctx.free(["b1row", "b2row"])
        ctx.free(["yac"])

    if "dbg2" in dbg:
        o1 = ctx.dram_t("dbg_mix", [8, 128, TLOC], F32, kind="ExternalOutput").ap()
        ctx.dma("sync", o1[:, :, :], mix_dr[:, :, :], reads=("mix_dr",))
        o2 = ctx.dram_t("dbg_y1", [16, 128, TLOC], F32, kind="ExternalOutput").ap()
        ctx.dma("sync", o2[:, :, :], y1_dr[:, :, :], reads=("y1_dr",))
        o3 = ctx.dram_t("dbg_y2", [16, 128, TLOC], F32, kind="ExternalOutput").ap()
        ctx.dma("sync", o3[:, :, :], y2_dr[:, :, :], reads=("y2_dr",))
        o4 = ctx.dram_t("dbg_delta", [16, 128, TLOC], F32, kind="ExternalOutput").ap()
        ctx.dma("sync", o4[:, :, :], delta_dr[:, :, :], reads=("delta_dr",))
        o5 = ctx.dram_t("dbg_x1s", [16, 128, TLOC], F32, kind="ExternalOutput").ap()
        ctx.dma("sync", o5[:, :, :], x1s_dr[:, :, :], reads=("x1s_dr",))
        o6 = ctx.dram_t("dbg_gs", [16, 128, TLOC], BF16, kind="ExternalOutput").ap()
        ctx.dma("sync", o6[:, :, :], gs_dr[:, :, :], reads=("gs_dr",))
        o7 = ctx.dram_t("dbg_moe", [128, 16 * 1024], BF16, kind="ExternalOutput").ap()
        ctx.dma("sync", o7[:, :], moe[:], reads=("moe",))
        o8 = ctx.dram_t("dbg_meq", [128, 128], F32, kind="ExternalOutput").ap()
        ctx.dma("sync", o8[:, :], meq[:], reads=("meq",))
        o9 = ctx.dram_t("dbg_yac", [128, 16 * 1024], BF16, kind="ExternalOutput").ap()
        ctx.dma("sync", o9[:, :], yac[:], reads=("yac",))
        o10 = ctx.dram_t("dbg_hdr", [32, 128, TLOC], BF16, kind="ExternalOutput").ap()
        ctx.dma("sync", o10[:, :, :], h_dr[:, :, :], reads=("h_dr",))
    xo = [ctx.sb(f"xo{i}", [128, TLOC], F32) for i in range(NDT)]
    mixt = ctx.sb("mixt", [128, TLOC], F32)
    for dm in range(NDT):
        ctx.dma("sync", xo[dm][:], x_in[dm * 128:(dm + 1) * 128, PAD:WS],
                writes=(f"xo{dm}",))
        ctx.dma("sync", mixt[:], mix_dr[dm, :, :], reads=("mix_dr",), writes=("mixt",))
        vop(ctx, "tensor_add", xo[dm][:], xo[dm][:], mixt[:],
            reads=("mixt", f"xo{dm}"), writes=(f"xo{dm}",))
        vop(ctx, "tensor_add", xo[dm][:], xo[dm][:], xn[dm][:, PAD:WS],
            reads=(f"l0xn{dm}", f"xo{dm}"), writes=(f"xo{dm}",))
        for tt in range(16):
            ctx.op("tensor", lambda t, tt=tt, dm=dm:
                   t.transpose(P["psT"][:, 0:128],
                               moe[:, tt * 1024 + dm * 128:tt * 1024 + (dm + 1) * 128],
                               sb_ident[:]),
                   reads=("moe", "c_ident"), writes=("psT",))
            vop(ctx, "tensor_add", xo[dm][:, tt * 128:(tt + 1) * 128],
                xo[dm][:, tt * 128:(tt + 1) * 128], P["psT"][:, 0:128],
                reads=("psT", f"xo{dm}"), writes=(f"xo{dm}",))
    if "dbg2" not in dbg:
        ctx.free(["mixt", "moe", "meq"] + [f"l0xn{i}" for i in range(NDT)])
    else:
        ctx.free(["mixt", "yac", "moe", "meq"] + [f"l0xn{i}" for i in range(NDT)])
    return xo



def build_block_program(block_id, dbg=(), params=None):
    params = params or {}
    nc = bass.Bass("TRN2", target_bir_lowering=False)
    ctx = Ctx(nc)
    P = {
        "psA": ctx.psum_t("psA", [128, 2048], F32),
        "psB": ctx.psum_t("psB", [128, 1536], F32),
        "psT": ctx.psum_t("psT", [128, 512], BF16),
    }
    win = WS if block_id == 0 else TLOC
    x_in = ctx.dram_t("x_fm", [D, win], F32, kind="ExternalInput").ap()
    x_out = ctx.dram_t("x_out", [D, TLOC], F32, kind="ExternalOutput").ap()
    with nc.Block() as block:
        ones_col = ctx.sb("ones_col", [128, 1], BF16)
        ones_colf = ctx.sb("ones_colf", [128, 1], F32)
        ones1 = ctx.sb("ones1", [1, 128], BF16)
        onesr = ctx.sb("onesr", [1, 512], BF16)
        vop(ctx, "memset", ones_col[:], 1.0, writes=("ones_col",))
        vop(ctx, "memset", ones_colf[:], 1.0, writes=("ones_colf",))
        vop(ctx, "memset", ones1[:], 1.0, writes=("ones1",))
        vop(ctx, "memset", onesr[:], 1.0, writes=("onesr",))
        if block_id == 0:
            xo = emit_l0(ctx, P, x_in, dbg=dbg)
            okeys = [f"xo{i}" for i in range(NDT)]
        else:
            xt = [ctx.sb(f"xs{i}", [128, TLOC], F32) for i in range(NDT)]
            if block_id not in (32,):
                for i in range(NDT):
                    ctx.dma("sync", xt[i][:], x_in[i * 128:(i + 1) * 128, :],
                            writes=(f"x{i}",))
            if block_id == 1:
                xo = emit_l1(ctx, P, xt, params.get("l1_R", 8))
            elif block_id == 2:
                xo = emit_l2(ctx, P, xt)
            elif block_id == 3:
                xo = emit_l3(ctx, P, xt, dbg=dbg)
            elif block_id == 31:
                xo = emit_l3(ctx, P, xt, parts=("attn",), dbg=dbg)
            elif block_id == 32:
                xacc = ctx.dram_t("xacc_fm", [D, TLOC], F32,
                                  kind="ExternalInput").ap()
                for i in range(NDT):
                    ctx.dma("sync", xt[i][:], x_in[i * 128:(i + 1) * 128, :],
                            writes=(f"x{i}",))
                xo = emit_l3(ctx, P, xt, parts=("ffn",), reload=xacc, dbg=dbg)
            okeys = [f"x{i}" for i in range(NDT)]
        for i in range(NDT):
            ctx.dma("sync", x_out[i * 128:(i + 1) * 128, :], xo[i][:, 0:TLOC],
                    reads=(okeys[i],), writes=("x_out",))
        ctx.drain_all()
        ctx.finalize(block)
    ctx.es.close()
    return nc


def run_block(block_id, per_core, shared, dbg=(), params=None):
    nc = build_block_program(block_id, dbg=dbg, params=params)
    names = {a.memorylocations[0].name for a in nc.m.functions[0].allocations
             if isinstance(a, mybir.MemoryLocationSet)
             and a.kind == "ExternalInput"}
    names.discard("partition_id")
    in_maps = []
    for c in range(NCORES):
        m = {}
        for k in names:
            if k in per_core[c]:
                m[k] = per_core[c][k]
            elif k in shared:
                m[k] = shared[k]
            else:
                raise KeyError(f"missing input {k}")
        in_maps.append(m)
    res = run_bass_kernel_spmd(nc, in_maps, core_ids=list(range(NCORES)))
    return res.results


def build_full_program(R1, dbg=()):
    nc = bass.Bass("TRN2", target_bir_lowering=False)
    ctx = Ctx(nc)
    P = {
        "psA": ctx.psum_t("psA", [128, 2048], F32),
        "psB": ctx.psum_t("psB", [128, 1536], F32),
        "psT": ctx.psum_t("psT", [128, 512], BF16),
    }
    x_in = ctx.dram_t("x_fm", [D, WS], F32, kind="ExternalInput").ap()
    x_out = ctx.dram_t("x_out", [D, TLOC], F32, kind="ExternalOutput").ap()
    with nc.Block() as block:
        ones_col = ctx.sb("ones_col", [128, 1], BF16)
        ones_colf = ctx.sb("ones_colf", [128, 1], F32)
        ones1 = ctx.sb("ones1", [1, 128], BF16)
        onesr = ctx.sb("onesr", [1, 512], BF16)
        vop(ctx, "memset", ones_col[:], 1.0, writes=("ones_col",))
        vop(ctx, "memset", ones_colf[:], 1.0, writes=("ones_colf",))
        vop(ctx, "memset", ones1[:], 1.0, writes=("ones1",))
        vop(ctx, "memset", onesr[:], 1.0, writes=("onesr",))
        xt = emit_l0(ctx, P, x_in, dbg=dbg)
        for i in range(NDT):
            ctx.rename(f"xo{i}", f"x{i}")
        emit_l1(ctx, P, xt, R1)
        emit_l2(ctx, P, xt)
        emit_l3(ctx, P, xt, dbg=dbg)
        for i in range(NDT):
            ctx.dma("sync", x_out[i * 128:(i + 1) * 128, :], xt[i][:, 0:TLOC],
                    reads=(f"x{i}",), writes=("x_out",))
        ctx.drain_all()
        ctx.finalize(block)
    ctx.es.close()
    return nc


def run_nc(nc, per_core, shared):
    names = {a.memorylocations[0].name for a in nc.m.functions[0].allocations
             if isinstance(a, mybir.MemoryLocationSet)
             and a.kind == "ExternalInput"}
    names.discard("partition_id")
    in_maps = []
    for c in range(NCORES):
        m = {}
        for k in names:
            if k in per_core[c]:
                m[k] = per_core[c][k]
            elif k in shared:
                m[k] = shared[k]
            else:
                raise KeyError(f"missing input {k}")
        in_maps.append(m)
    res = run_bass_kernel_spmd(nc, in_maps, core_ids=list(range(NCORES)))
    return res.results


def kernel(**inputs):
    per_core, shared = host_prep(inputs)
    R1 = per_core[0]["l1_rix"].shape[0]
    nc = build_full_program(R1)
    results = run_nc(nc, per_core, shared)
    out = np.zeros((B, T, D), np.float32)
    for c in range(NCORES):
        b, r = c // 4, c % 4
        out[b, r * TLOC:(r + 1) * TLOC, :] = results[c]["x_out"].T
    return out



def emit_l2(ctx, P, xt):
    AI = lambda name, shape, dt=F32: ctx.dram_t(name, shape, dt, kind="ExternalInput").ap()
    lnw = AI("l2_lnw", [D, 1])
    lnb = AI("l2_lnb", [D, 1])
    Wq = AI("l2_Wq_my", [1024, 256])
    Wk = AI("l2_Wk_my", [1024, 256])
    Wv = AI("l2_Wv_my", [1024, 512])
    Wg = AI("l2_Wg_my", [1024, 512])
    Wo = AI("l2_Wo_my", [512, 1024])
    gnw = AI("l2_gnw", [1, 512])
    DmT = AI("l2_DmaskT", [64, 64])
    crT = AI("l2_crossT", [128, 64])
    kde = AI("l2_kdec", [64, 1])
    gC = AI("l2_gC", [128, 1])
    cosT = AI("l2_cosT", [128, T])
    sinT = AI("l2_sinT", [128, T])
    w1 = AI("l2_w1", [4096, 1024])
    w1T = AI("l2_w1T", [1024, 4096])
    w2 = AI("l2_w2", [1024, 4096])
    w2T = AI("l2_w2T", [4096, 1024])
    b1r = AI("l2_b1r", [1, 4096])
    b2 = AI("l2_b2", [D, 1])
    idin = ctx.dram["ident"].ap() if "ident" in ctx.dram else AI("ident", [128, 128])

    ccx_in = ctx.dram_t("l2_ccx_in", [1024, TLOC], BF16).ap()
    ccx_out = ctx.dram_t("l2_ccx_out", [4096, TLOC], BF16).ap()
    q_dr = ctx.dram_t("l2_q_dr", [256, T], BF16).ap()
    k_dr = ctx.dram_t("l2_k_dr", [256, T], BF16).ap()
    v_dr = ctx.dram_t("l2_v_dr", [T, 512], BF16).ap()
    g_dr = ctx.dram_t("l2_g_dr", [T, 512], BF16).ap()
    wop_in = ctx.dram_t("l2_wop_in", [T, 1024], BF16).ap()
    wop_out = ctx.dram_t("l2_wop_out", [TLOC, 1024], BF16).ap()
    bw1_dr = ctx.dram_t("l2_bw1_dr", [1024, 4096], BF16).ap()
    bw2_dr = ctx.dram_t("l2_bw2_dr", [4096, 1024], BF16).ap()
    h2_dr = ctx.dram_t("l2_h2_dr", [32, 128, TLOC], BF16).ap()

    def ldc(name, rows, cols, src_slices, dt=F32, qn="sync"):
        tl = ctx.sb(name, [rows, cols], dt)
        for (dst_sl, src_ap) in src_slices:
            ctx.dma(qn, tl[dst_sl] if dst_sl else tl[:], src_ap, writes=(name,))
        return tl

    def col_slices(src, n, per=1):
        return [((slice(None), slice(i * per, (i + 1) * per)),
                 src[i * 128:(i + 1) * 128, :]) for i in range(n)]

    sb_lnw = ldc("l2lnw", 128, NDT, col_slices(lnw, NDT))
    sb_lnb = ldc("l2lnb", 128, NDT, col_slices(lnb, NDT))
    sb_DmT = ldc("l2DmT", 64, 64, [(None, DmT[:])], BF16, qn="gpsimd")
    sb_crT = ldc("l2crT", 128, 64, [(None, crT[:])], BF16, qn="gpsimd")
    sb_kde = ldc("l2kde", 64, 1, [(None, kde[:])])
    sb_gC = ldc("l2gC", 128, 1, [(None, gC[:])])
    sb_gnwr = ldc("l2gnwr", 1, 512, [(None, gnw[:])], BF16, qn="gpsimd")
    sb_b2 = ldc("l2b2", 128, 8, col_slices(b2, 8))
    sb_id2 = ldc("l2id", 128, 128, [(None, idin[:])], BF16, qn="gpsimd")
    ones1 = ctx.sb("ones1", [1, 128], BF16)
    onesr = ctx.sb("onesr", [1, 512], BF16)

    gnw_bc = ctx.sb("l2gnbc", [128, 512], BF16)
    mm(ctx, P["psB"][:, 0:512], [(ones1[:], sb_gnwr[:, 0:512])],
       reads=("l2gnwr", "ones1"), writes=("psB0",))
    vop(ctx, "tensor_copy", gnw_bc[:], P["psB"][:, 0:512], reads=("psB0",),
        writes=("l2gnbc",))

    xn2 = [ctx.sb(f"l2xn{i}", [128, TLOC], BF16, side="right") for i in range(NDT)]
    xn2k = [f"l2xn{i}" for i in range(NDT)]
    fm_norm(ctx, P, xt, [f"x{i}" for i in range(NDT)], xn2, xn2k, 1e-5, True,
            None, None, width=TLOC)
    nrm = [ctx.sb(f"l2nr{i}", [128, TLOC], BF16) for i in range(NDT)]
    nrmk = [f"l2nr{i}" for i in range(NDT)]
    fm_norm(ctx, P, xn2, xn2k, nrm, nrmk, 1e-5, True, None, None, width=TLOC,
            aff=([sb_lnw[:, i:i + 1] for i in range(NDT)],
                 [sb_lnb[:, i:i + 1] for i in range(NDT)]), affkey="l2lnw",
            src_f32=False)
    for i in range(NDT):
        ctx.dma("sync", ccx_in[i * 128:(i + 1) * 128, :], nrm[i][:],
                reads=(nrmk[i],), writes=("ccx_in",))
    ctx.collective("AllGather", ALU.bypass, GRP, ccx_in, ccx_out,
                   reads=("ccx_in",), writes=("ccx_out",))
    ctx.free(nrmk)

    sb_wq = ldc("l2wq", 128, 8 * 256, [((slice(None), slice(k * 256, (k + 1) * 256)),
                Wq[k * 128:(k + 1) * 128, :]) for k in range(8)], BF16, qn="gpsimd")
    sb_wk = ldc("l2wk", 128, 8 * 256, [((slice(None), slice(k * 256, (k + 1) * 256)),
                Wk[k * 128:(k + 1) * 128, :]) for k in range(8)], BF16, qn="gpsimd")
    sb_wv = ldc("l2wv", 128, 8 * 512, [((slice(None), slice(k * 512, (k + 1) * 512)),
                Wv[k * 128:(k + 1) * 128, :]) for k in range(8)], BF16, qn="gpsimd")
    sb_wg = ldc("l2wg", 128, 8 * 512, [((slice(None), slice(k * 512, (k + 1) * 512)),
                Wg[k * 128:(k + 1) * 128, :]) for k in range(8)], BF16, qn="gpsimd")
    nrm_sb = ctx.sb("l2nrmsb", [128, 8 * 512], BF16)
    cosb = ctx.sb("l2cos", [128, 512], BF16)
    sinb = ctx.sb("l2sin", [128, 512], BF16)
    qa = ctx.sb("l2qa", [128, 512], F32)
    qb = ctx.sb("l2qb", [128, 512], F32)
    t1 = ctx.sb("l2t1", [128, 512], F32)
    t2 = ctx.sb("l2t2", [128, 512], F32)
    ro = ctx.sb("l2ro", [128, 512], BF16)
    vev = ctx.sb("l2vev", [128, 512], BF16)
    for tb in range(4):
        for ci in range(4):
            c0 = ci * 512
            gcol = tb * TLOC + c0
            for k in range(8):
                ctx.dma("sync", nrm_sb[:, k * 512:(k + 1) * 512],
                        ccx_out[tb * 1024 + k * 128:tb * 1024 + (k + 1) * 128, c0:c0 + 512],
                        reads=("ccx_out",), writes=("l2nrmsb",))
            ctx.dma("gpsimd", cosb[:], cosT[:, gcol:gcol + 512], writes=("l2cos",))
            ctx.dma("gpsimd", sinb[:], sinT[:, gcol:gcol + 512], writes=("l2sin",))
            for (wsb, wk_, dst, dk_, scl) in ((sb_wq, "l2wq", q_dr, "q_dr", 0.0625),
                                              (sb_wk, "l2wk", k_dr, "k_dr", 1.0)):
                for mh in range(2):
                    pv = [P["psA"][:, 0:512], P["psA"][:, 512:1024]][mh]
                    pk = ["psA0", "psA1"][mh]
                    mm(ctx, pv, [(wsb[:, k * 256 + mh * 128:k * 256 + (mh + 1) * 128],
                                  nrm_sb[:, k * 512:(k + 1) * 512]) for k in range(8)],
                       reads=(wk_, "l2nrmsb"), writes=(pk,))
                ctx.op("scalar", lambda e, scl=scl: e.mul(qa[:], P["psA"][:, 0:512], scl),
                       reads=("psA0",), writes=("l2qa",))
                ctx.op("scalar", lambda e, scl=scl: e.mul(qb[:], P["psA"][:, 512:1024], scl),
                       reads=("psA1",), writes=("l2qb",))
                vop(ctx, "tensor_mul", t1[:], qa[:], cosb[:], reads=("l2qa", "l2cos"),
                    writes=("l2t1",))
                vop(ctx, "tensor_mul", t2[:], qb[:], sinb[:], reads=("l2qb", "l2sin"),
                    writes=("l2t2",))
                vop(ctx, "tensor_sub", ro[:], t1[:], t2[:], reads=("l2t1", "l2t2"),
                    writes=("l2ro",))
                ctx.dma("sync", dst[0:128, gcol:gcol + 512], ro[:], reads=("l2ro",),
                        writes=(dk_,))
                vop(ctx, "tensor_mul", t1[:], qb[:], cosb[:], reads=("l2qb", "l2cos"),
                    writes=("l2t1",))
                vop(ctx, "tensor_mul", t2[:], qa[:], sinb[:], reads=("l2qa", "l2sin"),
                    writes=("l2t2",))
                vop(ctx, "tensor_add", ro[:], t1[:], t2[:], reads=("l2t1", "l2t2"),
                    writes=("l2ro",))
                ctx.dma("sync", dst[128:256, gcol:gcol + 512], ro[:], reads=("l2ro",),
                        writes=(dk_,))
            for (wsb, wk_, dst, dk_) in ((sb_wv, "l2wv", v_dr, "v_dr"),
                                         (sb_wg, "l2wg", g_dr, "g_dr")):
                for ts in range(4):
                    mm(ctx, P["psA"][:, 0:512],
                       [(nrm_sb[:, k * 512 + ts * 128:k * 512 + (ts + 1) * 128],
                         wsb[:, k * 512:(k + 1) * 512]) for k in range(8)],
                       reads=(wk_, "l2nrmsb"), writes=("psA0",))
                    vop(ctx, "tensor_copy", vev[:], P["psA"][:, 0:512],
                        reads=("psA0",), writes=("l2vev",))
                    ctx.dma("sync", dst[gcol + ts * 128:gcol + (ts + 1) * 128, :],
                            vev[:], reads=("l2vev",), writes=(dk_,))
    ctx.free(["l2wq", "l2wk", "l2wv", "l2wg", "l2nrmsb", "l2cos", "l2sin",
              "l2qa", "l2qb", "l2t1", "l2t2", "l2ro", "l2vev"])

    wop_sb = ctx.sb("l2wop", [128, 4 * 1024], BF16)
    for k in range(4):
        ctx.dma("gpsimd", wop_sb[:, k * 1024:(k + 1) * 1024],
                Wo[k * 128:(k + 1) * 128, :], writes=("l2wop",))
    S_sb = [ctx.sb(f"l2S{i}", [128, 512], BF16) for i in range(2)]
    vop(ctx, "memset", S_sb[0][:], 0.0, writes=("l2S0",))
    vop(ctx, "memset", S_sb[1][:], 0.0, writes=("l2S1",))
    qc = ctx.sb("l2qc", [128, 128], BF16)
    kc = ctx.sb("l2kc", [128, 128], BF16)
    qcc = ctx.sb("l2qcc", [128, 128], BF16)
    adt = ctx.sb("l2adt", [64, 64], BF16)
    kdt = ctx.sb("l2kdt", [64, 256], BF16)
    vc = ctx.sb("l2vc", [64, 512], BF16)
    o2 = ctx.sb("l2o2", [128, 512], F32)
    gp = ctx.sb("l2gp", [128, 512], BF16)
    gss = ctx.sb("l2gss", [128, 512], BF16)
    sq2 = ctx.sb("l2sq2", [128, 512], F32)
    ms2 = ctx.sb("l2ms2", [128, 1], F32)
    onrm = ctx.sb("l2onrm", [128, 512], F32)
    ofb = ctx.sb("l2ofb", [128, 512], BF16)
    wpe = ctx.sb("l2wpe", [128, 1024], BF16)
    for pair in range(64):
        for sub in range(2):
            n = pair * 2 + sub
            c0 = n * 64
            ctx.dma("sync", qc[:, 0:64], q_dr[0:128, c0:c0 + 64], reads=("q_dr",), writes=("l2qc",))
            ctx.dma("sync", qc[:, 64:128], q_dr[128:256, c0:c0 + 64], reads=("q_dr",), writes=("l2qc",))
            ctx.dma("sync", kc[:, 0:64], k_dr[0:128, c0:c0 + 64], reads=("k_dr",), writes=("l2kc",))
            ctx.dma("sync", kc[:, 64:128], k_dr[128:256, c0:c0 + 64], reads=("k_dr",), writes=("l2kc",))
            ctx.dma("sync", vc[:], v_dr[c0:c0 + 64, :], reads=("v_dr",), writes=("l2vc",))
            mm(ctx, P["psB"][0:64, 1024:1088],
               [(kc[:, 0:64], qc[:, 0:64]), (kc[:, 64:128], qc[:, 64:128])],
               reads=("l2kc", "l2qc"), writes=("psB2",))
            vop(ctx, "tensor_mul", adt[:], P["psB"][0:64, 1024:1088], sb_DmT[:],
                reads=("psB2", "l2DmT"), writes=("l2adt",))
            vop(ctx, "tensor_mul", qcc[:, 0:64], qc[:, 0:64], sb_crT[:, 0:64],
                reads=("l2qc", "l2crT"), writes=("l2qcc",))
            vop(ctx, "tensor_mul", qcc[:, 64:128], qc[:, 64:128], sb_crT[:, 0:64],
                reads=("l2qc", "l2crT"), writes=("l2qcc",))
            mm(ctx, P["psB"][0:64, 0:512],
               [(adt[:], vc[:]),
                (qcc[:, 0:64], S_sb[0][:]), (qcc[:, 64:128], S_sb[1][:])],
               reads=("l2adt", "l2vc", "l2qcc", "l2S0", "l2S1"), writes=("psB0",))
            ctx.op("tensor", lambda t: t.transpose(P["psT"][0:64, 0:128], kc[:, 0:64], sb_id2[:]),
                   reads=("l2kc", "l2id"), writes=("psT",))
            vop(ctx, "tensor_copy", kdt[:, 0:128], P["psT"][0:64, 0:128],
                reads=("psT",), writes=("l2kdt",))
            ctx.op("tensor", lambda t: t.transpose(P["psT"][0:64, 0:128], kc[:, 64:128], sb_id2[:]),
                   reads=("l2kc", "l2id"), writes=("psT",))
            vop(ctx, "tensor_copy", kdt[:, 128:256], P["psT"][0:64, 0:128],
                reads=("psT",), writes=("l2kdt",))
            vop(ctx, "tensor_scalar", kdt[:], kdt[:], sb_kde[:, 0:1], None, ALU.mult,
                reads=("l2kdt", "l2kde"), writes=("l2kdt",))
            mm(ctx, P["psA"][:, 0:512], [(kdt[:, 0:128], vc[:])],
               reads=("l2kdt", "l2vc"), writes=("psA0",))
            mm(ctx, P["psA"][:, 512:1024], [(kdt[:, 128:256], vc[:])],
               reads=("l2kdt", "l2vc"), writes=("psA1",))
            vop(ctx, "scalar_tensor_tensor", S_sb[0][:], S_sb[0][:], sb_gC[:, 0:1],
                P["psA"][:, 0:512], ALU.mult, ALU.add,
                reads=("l2S0", "l2gC", "psA0"), writes=("l2S0",))
            vop(ctx, "scalar_tensor_tensor", S_sb[1][:], S_sb[1][:], sb_gC[:, 0:1],
                P["psA"][:, 512:1024], ALU.mult, ALU.add,
                reads=("l2S1", "l2gC", "psA1"), writes=("l2S1",))
            vop(ctx, "tensor_copy", o2[sub * 64:(sub + 1) * 64, :],
                P["psB"][0:64, 0:512], reads=("psB0",), writes=("l2o2",))
        sact(ctx, sq2[:], o2[:], AF.Square, reads=("l2o2",), writes=("l2sq2",))
        vop(ctx, "reduce_sum", ms2[:], sq2[:], AX.X, reads=("l2sq2",), writes=("l2ms2",))
        vop(ctx, "tensor_scalar", ms2[:], ms2[:], 1.0 / 512, 1e-5, ALU.mult, ALU.add,
            reads=("l2ms2",), writes=("l2ms2",))
        sact(ctx, ms2[:], ms2[:], AF.Ln, reads=("l2ms2",), writes=("l2ms2",))
        sact(ctx, ms2[:], ms2[:], AF.Exp, scale=-0.5, reads=("l2ms2",), writes=("l2ms2",))
        vop(ctx, "tensor_scalar", onrm[:], o2[:], ms2[:, 0:1], None, ALU.mult,
            reads=("l2o2", "l2ms2"), writes=("l2onrm",))
        vop(ctx, "tensor_mul", onrm[:], onrm[:], gnw_bc[:], reads=("l2onrm", "l2gnbc"),
            writes=("l2onrm",))
        ctx.dma("sync", gp[:], g_dr[pair * 128:(pair + 1) * 128, :], reads=("g_dr",),
                writes=("l2gp",))
        sact(ctx, gss[:], gp[:], AF.Silu, reads=("l2gp",), writes=("l2gss",))
        vop(ctx, "tensor_mul", ofb[:], onrm[:], gss[:], reads=("l2onrm", "l2gss"),
            writes=("l2ofb",))
        for hh in range(2):
            pv = [P["psA"][:, 0:512], P["psA"][:, 512:1024]][hh]
            pk = ["psA0", "psA1"][hh]
            parts = []
            for db in range(4):
                pass
        offm = ctx.sb("l2offm", [128, 4 * 128], BF16)
        for db in range(4):
            ctx.op("tensor", lambda t, db=db: t.transpose(
                P["psT"][:, 0:128], ofb[:, db * 128:(db + 1) * 128], sb_id2[:]),
                reads=("l2ofb", "l2id"), writes=("psT",))
            vop(ctx, "tensor_copy", offm[:, db * 128:(db + 1) * 128], P["psT"][:, 0:128],
                reads=("psT",), writes=("l2offm",))
        for hh in range(2):
            pv = [P["psA"][:, 0:512], P["psA"][:, 512:1024]][hh]
            pk = ["psA0", "psA1"][hh]
            mm(ctx, pv, [(offm[:, db * 128:(db + 1) * 128],
                          wop_sb[:, db * 1024 + hh * 512:db * 1024 + (hh + 1) * 512])
                         for db in range(4)],
               reads=("l2offm", "l2wop"), writes=(pk,))
            vop(ctx, "tensor_copy", wpe[:, hh * 512:(hh + 1) * 512], pv,
                reads=(pk,), writes=("l2wpe",))
        ctx.dma("sync", wop_in[pair * 128:(pair + 1) * 128, :], wpe[:],
                reads=("l2wpe",), writes=("wop_in",))
    ctx.free(["l2wop", "l2S0", "l2S1", "l2qc", "l2kc", "l2qcc", "l2adt", "l2kdt",
              "l2vc", "l2o2", "l2gp", "l2gss", "l2sq2", "l2ms2", "l2onrm",
              "l2ofb", "l2wpe", "l2offm"])

    ctx.collective("ReduceScatter", ALU.add, GRP, wop_in, wop_out,
                   reads=("wop_in",), writes=("wop_out",))
    wrow = ctx.sb("l2wrow", [128, 1024], BF16)
    for i in range(NDT):
        vop(ctx, "tensor_add", xt[i][:, 0:TLOC], xt[i][:, 0:TLOC], xn2[i][:],
            reads=(xn2k[i], f"x{i}"), writes=(f"x{i}",))
    for tt in range(16):
        ctx.dma("sync", wrow[:], wop_out[tt * 128:(tt + 1) * 128, :],
                reads=("wop_out",), writes=("l2wrow",))
        for dm in range(8):
            ctx.op("tensor", lambda t, dm=dm: t.transpose(
                P["psT"][:, 0:128], wrow[:, dm * 128:(dm + 1) * 128], sb_id2[:]),
                reads=("l2wrow", "l2id"), writes=("psT",))
            vop(ctx, "tensor_add", xt[dm][:, tt * 128:(tt + 1) * 128],
                xt[dm][:, tt * 128:(tt + 1) * 128], P["psT"][:, 0:128],
                reads=("psT", f"x{dm}"), writes=(f"x{dm}",))
    ctx.free(["l2wrow"])

    wrow_f = ctx.sb("l2wf", [128, 1024], F32)
    alc = ctx.sb("l2alc", [128, 1], F32)
    alr = ctx.sb("l2alr", [1, 4096], BF16)
    alf = ctx.sb("l2alf", [1, 4096], F32)
    al_dr = ctx.dram_t("l2_al_dr", [4096], F32).ap()
    for r in range(32):
        ctx.dma("sync", wrow_f[:], w1[r * 128:(r + 1) * 128, :], writes=("l2wf",))
        vop(ctx, "reduce_sum", alc[:], wrow_f[:], AX.X, apply_absolute_value=True,
            reads=("l2wf",), writes=("l2alc",))
        vop(ctx, "tensor_scalar_mul", alc[:], alc[:], 1.0 / 1024,
            reads=("l2alc",), writes=("l2alc",))
        ctx.dma("sync", al_dr[r * 128:(r + 1) * 128], alc[:, 0:1], reads=("l2alc",),
                writes=("al_dr",))
    ctx.dma("sync", alf[:], al_dr[:], reads=("al_dr",), writes=("l2alf",))
    vop(ctx, "tensor_copy", alr[:], alf[:], reads=("l2alf",), writes=("l2alr",))
    sgn = ctx.sb("l2sgn", [128, 4096], BF16)
    wTf = ctx.sb("l2wTf", [128, 4096], F32)
    bwt = ctx.sb("l2bwt", [128, 4096], BF16)
    for k in range(8):
        ctx.dma("sync", wTf[:], w1T[k * 128:(k + 1) * 128, :], writes=("l2wTf",))
        ctx.op("scalar", lambda e: e.sign(sgn[:], wTf[:]), reads=("l2wTf",),
               writes=("l2sgn",))
        for sb_ in range(8):
            mm(ctx, P["psB"][:, 0:512], [(ones1[:], alr[:, sb_ * 512:(sb_ + 1) * 512])],
               reads=("l2alr", "ones1"), writes=("psB0",))
            vop(ctx, "tensor_mul", bwt[:, sb_ * 512:(sb_ + 1) * 512],
                sgn[:, sb_ * 512:(sb_ + 1) * 512], P["psB"][:, 0:512],
                reads=("l2sgn", "psB0"), writes=("l2bwt",))
        ctx.dma("sync", bw1_dr[k * 128:(k + 1) * 128, :], bwt[:], reads=("l2bwt",),
                writes=("bw1_dr",))
    for r in range(8):
        ctx.dma("sync", wrow_f[:, 0:1024], w2[r * 128:(r + 1) * 128, 0:1024],
                writes=("l2wf",))
        ctx.dma("sync", wrow_f[:, 0:1024], w2[r * 128:(r + 1) * 128, 1024:2048],
                writes=("l2wf",)) if False else None
        vop(ctx, "reduce_sum", alc[:], wrow_f[:, 0:1024], AX.X,
            apply_absolute_value=True, reads=("l2wf",), writes=("l2alc",))
        acc2 = ctx.sb("l2acc2", [128, 1], F32)
        vop(ctx, "tensor_copy", acc2[:], alc[:], reads=("l2alc",), writes=("l2acc2",))
        for hseg in range(1, 4):
            ctx.dma("sync", wrow_f[:, 0:1024],
                    w2[r * 128:(r + 1) * 128, hseg * 1024:(hseg + 1) * 1024],
                    writes=("l2wf",))
            vop(ctx, "reduce_sum", alc[:], wrow_f[:, 0:1024], AX.X,
                apply_absolute_value=True, reads=("l2wf",), writes=("l2alc",))
            vop(ctx, "tensor_add", acc2[:], acc2[:], alc[:], reads=("l2alc", "l2acc2"),
                writes=("l2acc2",))
        vop(ctx, "tensor_scalar_mul", acc2[:], acc2[:], 1.0 / 4096,
            reads=("l2acc2",), writes=("l2acc2",))
        ctx.dma("sync", al_dr[r * 128:(r + 1) * 128], acc2[:, 0:1], reads=("l2acc2",),
                writes=("al_dr",))
    ctx.dma("sync", alf[:, 0:1024], al_dr[0:1024], reads=("al_dr",), writes=("l2alf",))
    vop(ctx, "tensor_copy", alr[:, 0:1024], alf[:, 0:1024], reads=("l2alf",),
        writes=("l2alr",))
    for k in range(32):
        ctx.dma("sync", wTf[:, 0:1024], w2T[k * 128:(k + 1) * 128, :], writes=("l2wTf",))
        ctx.op("scalar", lambda e: e.sign(sgn[:, 0:1024], wTf[:, 0:1024]),
               reads=("l2wTf",), writes=("l2sgn",))
        for sb_ in range(2):
            mm(ctx, P["psB"][:, 0:512], [(ones1[:], alr[:, sb_ * 512:(sb_ + 1) * 512])],
               reads=("l2alr", "ones1"), writes=("psB0",))
            vop(ctx, "tensor_mul", bwt[:, sb_ * 512:(sb_ + 1) * 512],
                sgn[:, sb_ * 512:(sb_ + 1) * 512], P["psB"][:, 0:512],
                reads=("l2sgn", "psB0"), writes=("l2bwt",))
        ctx.dma("sync", bw2_dr[k * 128:(k + 1) * 128, :], bwt[:, 0:1024],
                reads=("l2bwt",), writes=("bw2_dr",))
    ctx.free(["l2wf", "l2alc", "l2alr", "l2alf", "l2sgn", "l2wTf", "l2bwt", "l2acc2"])

    b1row = ctx.sb("l2b1row", [1, 4096], BF16)
    ctx.dma("gpsimd", b1row[:], b1r[:], writes=("l2b1row",))
    wbuf = [ctx.sb(f"l2wb{p}", [128, 8 * 1024], BF16) for p in range(2)]
    hst = [ctx.sb(f"l2hst{p}", [128, 512], BF16) for p in range(2)]
    for mb in range(4):
        wk_ = f"l2wb{mb % 2}"
        for k in range(8):
            ctx.dma("sync", wbuf[mb % 2][:, k * 1024:(k + 1) * 1024],
                    bw1_dr[k * 128:(k + 1) * 128, mb * 1024:(mb + 1) * 1024],
                    reads=("bw1_dr",), writes=(wk_,))
        for mi in range(8):
            m_ = mb * 8 + mi
            for ci, (c0, w) in enumerate(chunks(TLOC, 512)):
                pp = (mi * 4 + ci) % 2
                pk = ["psA0", "psA1"][pp]
                pv = [P["psA"][:, 0:512], P["psA"][:, 512:1024]][pp]
                parts = [(wbuf[mb % 2][:, k * 1024 + mi * 128:k * 1024 + (mi + 1) * 128],
                          xn2[k][:, c0:c0 + w]) for k in range(8)]
                parts.append((b1row[:, m_ * 128:(m_ + 1) * 128], onesr[:, 0:w]))
                mm(ctx, pv[:, 0:w], parts,
                   reads=(wk_,) + tuple(xn2k) + ("onesr", "l2b1row"), writes=(pk,))
                hk = f"l2hst{pp}"
                sact(ctx, hst[pp][:, 0:w], pv[:, 0:w], AF.Gelu, reads=(pk,), writes=(hk,))
                ctx.dma("sync", h2_dr[m_, :, c0:c0 + w], hst[pp][:, 0:w],
                        reads=(hk,), writes=("h2_dr",))
    ctx.free(["l2wb0", "l2wb1", "l2hst0", "l2hst1"])
    yac2 = [ctx.sb(f"l2ya{i}", [128, TLOC], BF16) for i in range(NDT)]
    hgb = ctx.sb("l2hgb", [128, 8 * TLOC], BF16)
    w2bf = ctx.sb("l2w2b", [128, 8 * 1024], BF16)
    for g_ in range(4):
        for j in range(8):
            ctx.dma("sync", w2bf[:, j * 1024:(j + 1) * 1024],
                    bw2_dr[(g_ * 8 + j) * 128:(g_ * 8 + j + 1) * 128, :],
                    reads=("bw2_dr",), writes=("l2w2b",))
            ctx.dma("sync", hgb[:, j * TLOC:(j + 1) * TLOC], h2_dr[g_ * 8 + j, :, :],
                    reads=("h2_dr",), writes=("l2hgb",))
        for dm in range(8):
            for ci, (c0, w) in enumerate(chunks(TLOC, 512)):
                pp = (dm * 4 + ci) % 2
                pk = ["psA0", "psA1"][pp]
                pv = [P["psA"][:, 0:512], P["psA"][:, 512:1024]][pp]
                mm(ctx, pv[:, 0:w],
                   [(w2bf[:, j * 1024 + dm * 128:j * 1024 + (dm + 1) * 128],
                     hgb[:, j * TLOC + c0:j * TLOC + c0 + w]) for j in range(8)],
                   reads=("l2w2b", "l2hgb"), writes=(pk,))
                ys = yac2[dm][:, c0:c0 + w]
                if g_ == 0:
                    vop(ctx, "tensor_copy", ys, pv[:, 0:w], reads=(pk,),
                        writes=(f"l2ya{dm}",))
                else:
                    vop(ctx, "tensor_add", ys, ys, pv[:, 0:w],
                        reads=(pk, f"l2ya{dm}"), writes=(f"l2ya{dm}",))
    for dm in range(8):
        vop(ctx, "scalar_tensor_tensor", xt[dm][:, 0:TLOC], yac2[dm][:],
            sb_b2[:, dm:dm + 1], xt[dm][:, 0:TLOC], ALU.add, ALU.add,
            reads=(f"l2ya{dm}", "l2b2", f"x{dm}"), writes=(f"x{dm}",))
    if False:
        for nm, ap_, shp, dt in (("dbg2_q", q_dr, [256, T], BF16),
                                 ("dbg2_k", k_dr, [256, T], BF16),
                                 ("dbg2_v", v_dr, [T, 512], BF16),
                                 ("dbg2_g", g_dr, [T, 512], BF16),
                                 ("dbg2_wop", wop_in, [T, 1024], BF16),
                                 ("dbg2_wout", wop_out, [TLOC, 1024], BF16),
                                 ("dbg2_bw1", bw1_dr, [1024, 4096], BF16),
                                 ("dbg2_h", h2_dr, [32, 128, TLOC], BF16)):
            o = ctx.dram_t(nm, shp, dt, kind="ExternalOutput").ap()
            ctx.dma("sync", o, ap_, reads=())
    ctx.free(["l2hgb", "l2w2b", "l2b1row"] + [f"l2ya{i}" for i in range(NDT)])
    ctx.free([f"l2xn{i}" for i in range(NDT)])
    return xt



def emit_l1(ctx, P, xt, R1):
    AI = lambda name, shape, dt=F32: ctx.dram_t(name, shape, dt, kind="ExternalInput").ap()
    WqT = AI("l1_WqT_my", [1024, 256])
    WkT = AI("l1_WkT_my", [1024, 256])
    WvT = AI("l1_WvT_my", [1024, 256])
    WoT = AI("l1_WoT_my", [256, 1024])
    bqm = AI("l1_bq_my", [128, 2])
    bkm = AI("l1_bk_my", [128, 2])
    bvm = AI("l1_bv_my", [1, 256])
    bo = AI("l1_bo", [D, 1])
    RT = AI("l1_RT", [128, 128])
    cosT = AI("l1_cosT", [128, T])
    sinT = AI("l1_sinT", [128, T])
    cmask = AI("l1_cmask", [128, 2048])
    W1m = AI("l1_W1_my", [1024, 4096])
    W2m = AI("l1_W2_my", [4096, 1024])
    b1m = AI("l1_b1_my", [1, 4096])
    b2m = AI("l1_b2_my", [1, 1024])
    gidx = ctx.dram_t("l1_gidx", [16, 128], I32, kind="ExternalInput").ap()
    rix = ctx.dram_t("l1_rix", [R1, 16, 128], I32, kind="ExternalInput").ap()
    cwf = AI("l1_cw2", [T, 8])
    esel = AI("l1_esel", [128, 8])
    idin = ctx.dram["ident"].ap() if "ident" in ctx.dram else AI("ident", [128, 128])

    ccx_in = ctx.dram_t("l1_ccx_in", [1024, TLOC], BF16).ap()
    ccx_out = ctx.dram_t("l1_ccx_out", [4096, TLOC], BF16).ap()
    xtm_in = ctx.dram_t("l1_xtm_in", [TLOC, 1024], BF16).ap()
    xtm_out = ctx.dram_t("l1_xtm_out", [NCORES * TLOC, 1024], BF16).ap()
    q_dr = ctx.dram_t("l1_q_dr", [256, T], BF16).ap()
    k_dr = ctx.dram_t("l1_k_dr", [256, T], BF16).ap()
    v_dr = ctx.dram_t("l1_v_dr", [T, 256], BF16).ap()
    o_dr = ctx.dram_t("l1_o_dr", [256, T], BF16).ap()
    wop_in = ctx.dram_t("l1_wop_in", [T, 1024], BF16).ap()
    wop_out = ctx.dram_t("l1_wop_out", [TLOC, 1024], BF16).ap()
    wout_in = ctx.dram_t("l1_wout_in", [TLOC, 1024], BF16).ap()
    wout_out = ctx.dram_t("l1_wout_out", [NCORES * TLOC + 128, 1024], BF16).ap()
    h1_dr = ctx.dram_t("l1_h1_dr", [32, 128, TLOC], BF16).ap()

    def ldc(name, rows, cols, src_slices, dt=F32, qn="sync"):
        tl = ctx.sb(name, [rows, cols], dt)
        for (dst_sl, src_ap) in src_slices:
            ctx.dma(qn, tl[dst_sl] if dst_sl else tl[:], src_ap, writes=(name,))
        return tl

    def col_slices(src, n, per=1):
        return [((slice(None), slice(i * per, (i + 1) * per)),
                 src[i * 128:(i + 1) * 128, :]) for i in range(n)]

    sb_id = ldc("l1id", 128, 128, [(None, idin[:])], BF16, qn="gpsimd")
    sb_RT = ldc("l1RT", 128, 128, [(None, RT[:])], BF16, qn="gpsimd")
    sb_bq = ldc("l1bq", 128, 2, [(None, bqm[:])])
    sb_bk = ldc("l1bk", 128, 2, [(None, bkm[:])])
    sb_bvr = ldc("l1bvr", 1, 256, [(None, bvm[:])], BF16, qn="gpsimd")
    sb_bo = ldc("l1bo", 128, 8, col_slices(bo, 8))
    sb_cm = ldc("l1cm", 128, 2048, [(None, cmask[:])], BF16, qn="gpsimd")
    ones1 = ctx.sb("ones1", [1, 128], BF16)
    onesr = ctx.sb("onesr", [1, 512], BF16)
    ones_col = ctx.sb("ones_col", [128, 1], BF16)

    xn1 = [ctx.sb(f"l1xn{i}", [128, TLOC], BF16, side="right") for i in range(NDT)]
    xn1k = [f"l1xn{i}" for i in range(NDT)]
    fm_norm(ctx, P, xt, [f"x{i}" for i in range(NDT)], xn1, xn1k, 1e-5, True,
            None, None, width=TLOC)
    for i in range(NDT):
        ctx.dma("sync", ccx_in[i * 128:(i + 1) * 128, :], xn1[i][:],
                reads=(xn1k[i],), writes=("ccx_in",))
    ctx.collective("AllGather", ALU.bypass, GRP, ccx_in, ccx_out,
                   reads=("ccx_in",), writes=("ccx_out",))
    xtt = ctx.sb("l1xtt", [128, 128], BF16)
    for tti in range(16):
        for dmi in range(8):
            ctx.op("tensor", lambda t, tti=tti, dmi=dmi: t.transpose(
                P["psT"][:, 0:128], xn1[dmi][:, tti * 128:(tti + 1) * 128], sb_id[:]),
                reads=(xn1k[dmi], "l1id"), writes=("psT",))
            vop(ctx, "tensor_copy", xtt[:], P["psT"][:, 0:128], reads=("psT",),
                writes=("l1xtt",))
            ctx.dma("sync", xtm_in[tti * 128:(tti + 1) * 128, dmi * 128:(dmi + 1) * 128],
                    xtt[:], reads=("l1xtt",), writes=("xtm_in",))
    ctx.collective("AllGather", ALU.bypass, GRP8, xtm_in, xtm_out,
                   reads=("xtm_in",), writes=("xtm_out",))
    ctx.free(["l1xtt"])

    sb_wq = ldc("l1wq", 128, 8 * 256, [((slice(None), slice(k * 256, (k + 1) * 256)),
                WqT[k * 128:(k + 1) * 128, :]) for k in range(8)], BF16, qn="gpsimd")
    sb_wk = ldc("l1wk", 128, 8 * 256, [((slice(None), slice(k * 256, (k + 1) * 256)),
                WkT[k * 128:(k + 1) * 128, :]) for k in range(8)], BF16, qn="gpsimd")
    sb_wv = ldc("l1wv", 128, 8 * 256, [((slice(None), slice(k * 256, (k + 1) * 256)),
                WvT[k * 128:(k + 1) * 128, :]) for k in range(8)], BF16, qn="gpsimd")
    nrm_sb = ctx.sb("l1nrm", [128, 8 * 512], BF16)
    cosb = ctx.sb("l1cos", [128, 512], BF16)
    sinb = ctx.sb("l1sin", [128, 512], BF16)
    qtmp = ctx.sb("l1qt", [128, 512], BF16)
    rq = ctx.sb("l1rq", [128, 512], F32)
    t1b = ctx.sb("l1t1", [128, 512], F32)
    rob = ctx.sb("l1ro", [128, 512], BF16)
    vev = ctx.sb("l1vev", [128, 256], BF16)
    for tb in range(4):
        for ci in range(4):
            c0 = ci * 512
            gcol = tb * TLOC + c0
            for k in range(8):
                ctx.dma("sync", nrm_sb[:, k * 512:(k + 1) * 512],
                        ccx_out[tb * 1024 + k * 128:tb * 1024 + (k + 1) * 128, c0:c0 + 512],
                        reads=("ccx_out",), writes=("l1nrm",))
            ctx.dma("gpsimd", cosb[:], cosT[:, gcol:gcol + 512], writes=("l1cos",))
            ctx.dma("gpsimd", sinb[:], sinT[:, gcol:gcol + 512], writes=("l1sin",))
            for (wsb, wkk, bb, bk_, dst, dk_) in (
                    (sb_wq, "l1wq", sb_bq, "l1bq", q_dr, "q_dr"),
                    (sb_wk, "l1wk", sb_bk, "l1bk", k_dr, "k_dr")):
                for pk_ in range(2):
                    mm(ctx, P["psA"][:, 0:512],
                       [(wsb[:, k * 256 + pk_ * 128:k * 256 + (pk_ + 1) * 128],
                         nrm_sb[:, k * 512:(k + 1) * 512]) for k in range(8)],
                       reads=(wkk, "l1nrm"), writes=("psA0",))
                    ctx.op("scalar", lambda e, pk_=pk_, bb=bb: e.add(
                        qtmp[:], P["psA"][:, 0:512], bb[:, pk_:pk_ + 1]),
                        reads=("psA0", bk_), writes=("l1qt",))
                    mm(ctx, P["psA"][:, 512:1024], [(sb_RT[:], qtmp[:])],
                       reads=("l1RT", "l1qt"), writes=("psA1",))
                    vop(ctx, "tensor_mul", t1b[:], qtmp[:], cosb[:],
                        reads=("l1qt", "l1cos"), writes=("l1t1",))
                    vop(ctx, "tensor_mul", rq[:], P["psA"][:, 512:1024], sinb[:],
                        reads=("psA1", "l1sin"), writes=("l1rq",))
                    vop(ctx, "tensor_add", rob[:], t1b[:], rq[:],
                        reads=("l1t1", "l1rq"), writes=("l1ro",))
                    ctx.dma("sync", dst[pk_ * 128:(pk_ + 1) * 128, gcol:gcol + 512],
                            rob[:], reads=("l1ro",), writes=(dk_,))
            for ts in range(4):
                parts = [(nrm_sb[:, k * 512 + ts * 128:k * 512 + (ts + 1) * 128],
                          sb_wv[:, k * 256:(k + 1) * 256]) for k in range(8)]
                parts.append((onesr[:, 0:128], sb_bvr[:, 0:256]))
                mm(ctx, P["psA"][:, 0:256], parts,
                   reads=("l1wv", "l1nrm", "onesr", "l1bvr"), writes=("psA0",))
                vop(ctx, "tensor_copy", vev[:], P["psA"][:, 0:256],
                    reads=("psA0",), writes=("l1vev",))
                ctx.dma("sync", v_dr[gcol + ts * 128:gcol + (ts + 1) * 128, :],
                        vev[:], reads=("l1vev",), writes=("v_dr",))
    ctx.free(["l1wq", "l1wk", "l1wv", "l1nrm", "l1cos", "l1sin", "l1qt", "l1rq",
              "l1t1", "l1ro", "l1vev"])

    qch = ctx.sb("l1qch", [64, 512], BF16)
    kti = ctx.sb("l1kti", [64, 2 * 128], BF16)
    vti = ctx.sb("l1vti", [128, 2 * 64], BF16)
    pt = ctx.sb("l1pt", [128, 1024], BF16)
    rl = ctx.sb("l1rl", [1, 512], F32)
    rlb = ctx.sb("l1rlb", [1, 512], BF16)
    rlbc = ctx.sb("l1rlbc", [64, 512], BF16)
    ofm = ctx.sb("l1ofm", [64, 512], BF16)
    for h in range(4):
        qrow = (h // 2) * 128 + (h % 2) * 64
        for tq in range(16):
            ctx.dma("sync", qch[:], q_dr[qrow:qrow + 64, tq * 512:(tq + 1) * 512],
                    reads=("q_dr",), writes=("l1qch",))
            ntk = (tq + 1) * 4
            for tk2 in range((ntk + 1) // 2):
                npair = min(2, ntk - tk2 * 2)
                for j in range(npair):
                    tk = tk2 * 2 + j
                    ctx.dma("sync", kti[:, j * 128:(j + 1) * 128],
                            k_dr[qrow:qrow + 64, tk * 128:(tk + 1) * 128],
                            reads=("k_dr",), writes=("l1kti",))
                    ctx.dma("sync", vti[:, j * 64:(j + 1) * 64],
                            v_dr[tk * 128:(tk + 1) * 128, h * 64:(h + 1) * 64],
                            reads=("v_dr",), writes=("l1vti",))
                    mm(ctx, P["psA"][:, j * 512:(j + 1) * 512],
                       [(kti[:, j * 128:(j + 1) * 128], qch[:])],
                       reads=("l1kti", "l1qch"), writes=(f"psA{j}",))
                sact(ctx, pt[:, 0:npair * 512], P["psA"][:, 0:npair * 512], AF.Exp,
                     scale=0.125, reads=("psA0", "psA1"), writes=("l1pt",))
                for j in range(npair):
                    tk = tk2 * 2 + j
                    moff = tk - tq * 4
                    if 0 <= moff <= 3:
                        vop(ctx, "tensor_mul", pt[:, j * 512:(j + 1) * 512],
                            pt[:, j * 512:(j + 1) * 512],
                            sb_cm[:, moff * 512:(moff + 1) * 512],
                            reads=("l1pt", "l1cm"), writes=("l1pt",))
                for j in range(npair):
                    tk = tk2 * 2 + j
                    first = (tk == 0)
                    mm(ctx, P["psB"][0:64, 0:512],
                       [(vti[:, j * 64:(j + 1) * 64], pt[:, j * 512:(j + 1) * 512])],
                       reads=("l1vti", "l1pt"), writes=("psB0",),
                       start=first, stop=(tk == ntk - 1))
                    mm(ctx, P["psB"][64:65, 0:512],
                       [(ones_col[:], pt[:, j * 512:(j + 1) * 512])],
                       reads=("ones_col", "l1pt"), writes=("psB1",),
                       start=first, stop=(tk == ntk - 1))
            vop(ctx, "reciprocal", rl[:], P["psB"][64:65, 0:512],
                reads=("psB1",), writes=("l1rl",))
            vop(ctx, "tensor_copy", rlb[:], rl[:], reads=("l1rl",), writes=("l1rlb",))
            mm(ctx, P["psA"][0:64, 0:512], [(ones1[:, 0:64], rlb[:])],
               reads=("ones1", "l1rlb"), writes=("psA0",))
            vop(ctx, "tensor_copy", rlbc[:], P["psA"][0:64, 0:512],
                reads=("psA0",), writes=("l1rlbc",))
            vop(ctx, "tensor_mul", ofm[:], P["psB"][0:64, 0:512], rlbc[:],
                reads=("psB0", "l1rlbc"), writes=("l1ofm",))
            ctx.dma("sync", o_dr[qrow:qrow + 64, tq * 512:(tq + 1) * 512],
                    ofm[:], reads=("l1ofm",), writes=("o_dr",))
    ctx.free(["l1qch", "l1kti", "l1vti", "l1pt", "l1rl", "l1rlb", "l1rlbc", "l1ofm"])

    sb_wo = ctx.sb("l1wo", [128, 2 * 1024], BF16)
    for k in range(2):
        ctx.dma("gpsimd", sb_wo[:, k * 1024:(k + 1) * 1024],
                WoT[k * 128:(k + 1) * 128, :], writes=("l1wo",))
    oblk = ctx.sb("l1oblk", [128, 2 * 128], BF16)
    wpe = ctx.sb("l1wpe", [128, 1024], BF16)
    for tt in range(64):
        for k in range(2):
            ctx.dma("sync", oblk[:, k * 128:(k + 1) * 128],
                    o_dr[k * 128:(k + 1) * 128, tt * 128:(tt + 1) * 128],
                    reads=("o_dr",), writes=("l1oblk",))
        for hf in range(2):
            pv = [P["psA"][:, 0:512], P["psA"][:, 512:1024]][hf]
            pk = ["psA0", "psA1"][hf]
            mm(ctx, pv, [(oblk[:, k * 128:(k + 1) * 128],
                          sb_wo[:, k * 1024 + hf * 512:k * 1024 + (hf + 1) * 512])
                         for k in range(2)],
               reads=("l1oblk", "l1wo"), writes=(pk,))
            vop(ctx, "tensor_copy", wpe[:, hf * 512:(hf + 1) * 512], pv,
                reads=(pk,), writes=("l1wpe",))
        ctx.dma("sync", wop_in[tt * 128:(tt + 1) * 128, :], wpe[:],
                reads=("l1wpe",), writes=("wop_in",))
    ctx.collective("ReduceScatter", ALU.add, GRP, wop_in, wop_out,
                   reads=("wop_in",), writes=("wop_out",))
    wrow = ctx.sb("l1wrow", [128, 1024], BF16)
    for tt in range(16):
        ctx.dma("sync", wrow[:], wop_out[tt * 128:(tt + 1) * 128, :],
                reads=("wop_out",), writes=("l1wrow",))
        for dm in range(8):
            ctx.op("tensor", lambda t, dm=dm: t.transpose(
                P["psT"][:, 0:128], wrow[:, dm * 128:(dm + 1) * 128], sb_id[:]),
                reads=("l1wrow", "l1id"), writes=("psT",))
            vop(ctx, "scalar_tensor_tensor", xt[dm][:, tt * 128:(tt + 1) * 128],
                P["psT"][:, 0:128], sb_bo[:, dm:dm + 1],
                xt[dm][:, tt * 128:(tt + 1) * 128], ALU.add, ALU.add,
                reads=("psT", "l1bo", f"x{dm}"), writes=(f"x{dm}",))
    ctx.free(["l1wo", "l1oblk", "l1wpe", "l1wrow"])

    sb_esel = ldc("l1esel", 128, 8, [(None, esel[:])])
    wvals = ctx.sb("l1wv2", [128, 16], F32)
    wst = ctx.sb("l1wst", [128, 8], F32)
    wtmp = ctx.sb("l1wtmp", [128, 8], F32)
    gix = ctx.sb("l1gix", [128, 1], I32)
    xg_tm = ctx.sb("l1xgt", [128, 1024], BF16)
    xgf = [ctx.sb(f"l1xg{i}", [128, TLOC], BF16, side="right") for i in range(NDT)]
    xgfk = [f"l1xg{i}" for i in range(NDT)]
    for gt in range(16):
        ctx.dma("sync", gix[:], gidx[gt:gt + 1, :], writes=("l1gix",))
        ctx.idma(xg_tm[:], xtm_out[:], gix[:, 0:1],
                 reads=("l1gix", "xtm_out"), writes=("l1xgt",))
        for dmi in range(8):
            ctx.op("tensor", lambda t, dmi=dmi: t.transpose(
                P["psT"][:, 0:128], xg_tm[:, dmi * 128:(dmi + 1) * 128], sb_id[:]),
                reads=("l1xgt", "l1id"), writes=("psT",))
            vop(ctx, "tensor_copy", xgf[dmi][:, gt * 128:(gt + 1) * 128],
                P["psT"][:, 0:128], reads=("psT",), writes=(xgfk[dmi],))
    for gt in range(16):
        ctx.dma("sync", gix[:], gidx[gt:gt + 1, :], writes=("l1gix",))
        ctx.idma(wst[:], cwf[:], gix[:, 0:1], reads=("l1gix",), writes=("l1wst",))
        vop(ctx, "tensor_mul", wtmp[:], wst[:], sb_esel[:], reads=("l1wst", "l1esel"),
            writes=("l1wtmp",))
        vop(ctx, "reduce_sum", wvals[:, gt:gt + 1], wtmp[:], AX.X,
            reads=("l1wtmp",), writes=("l1wv2",))
    ctx.free(["l1gix", "l1xgt", "l1wtmp", "l1wst"])

    b1row = ctx.sb("l1b1r", [1, 4096], BF16)
    ctx.dma("gpsimd", b1row[:], b1m[:], writes=("l1b1r",))
    b2row = ctx.sb("l1b2r", [1, 1024], BF16)
    ctx.dma("gpsimd", b2row[:], b2m[:], writes=("l1b2r",))
    wbuf = [ctx.sb(f"l1wb{p}", [128, 8 * 1024], BF16) for p in range(2)]
    sgel = ctx.sb("l1sg", [128, 512], F32)
    tgel = ctx.sb("l1tg", [128, 512], F32)
    ugel = ctx.sb("l1ug", [128, 512], F32)
    hst = [ctx.sb(f"l1hs{p}", [128, 512], BF16) for p in range(2)]
    for mb in range(4):
        wkk = f"l1wb{mb % 2}"
        for k in range(8):
            ctx.dma("gpsimd", wbuf[mb % 2][:, k * 1024:(k + 1) * 1024],
                    W1m[k * 128:(k + 1) * 128, mb * 1024:(mb + 1) * 1024], writes=(wkk,))
        for mi in range(8):
            m_ = mb * 8 + mi
            for ci, (c0, w) in enumerate(chunks(TLOC, 512)):
                pp = (mi * 4 + ci) % 2
                pk = ["psA0", "psA1"][pp]
                pv = [P["psA"][:, 0:512], P["psA"][:, 512:1024]][pp]
                parts = [(wbuf[mb % 2][:, k * 1024 + mi * 128:k * 1024 + (mi + 1) * 128],
                          xgf[k][:, c0:c0 + w]) for k in range(8)]
                parts.append((b1row[:, m_ * 128:(m_ + 1) * 128], onesr[:, 0:w]))
                mm(ctx, pv[:, 0:w], parts,
                   reads=(wkk,) + tuple(xgfk) + ("onesr", "l1b1r"), writes=(pk,))
                sact(ctx, sgel[:, 0:w], pv[:, 0:w], AF.Square, reads=(pk,), writes=("l1sg",))
                vop(ctx, "tensor_scalar", tgel[:, 0:w], sgel[:, 0:w], 0.044715, 1.0,
                    ALU.mult, ALU.add, reads=("l1sg",), writes=("l1tg",))
                vop(ctx, "scalar_tensor_tensor", ugel[:, 0:w], tgel[:, 0:w],
                    0.7978845608028654, pv[:, 0:w], ALU.mult, ALU.mult,
                    reads=("l1tg", pk), writes=("l1ug",))
                sact(ctx, ugel[:, 0:w], ugel[:, 0:w], AF.Tanh, reads=("l1ug",),
                     writes=("l1ug",))
                vop(ctx, "tensor_scalar", tgel[:, 0:w], ugel[:, 0:w], 0.5, 0.5,
                    ALU.mult, ALU.add, reads=("l1ug",), writes=("l1tg",))
                hk = f"l1hs{pp}"
                vop(ctx, "tensor_mul", hst[pp][:, 0:w], tgel[:, 0:w], pv[:, 0:w],
                    reads=("l1tg", pk), writes=(hk,))
                ctx.dma("sync", h1_dr[m_, :, c0:c0 + w], hst[pp][:, 0:w],
                        reads=(hk,), writes=("h1_dr",))
    ctx.free(["l1wb0", "l1wb1", "l1sg", "l1tg", "l1ug", "l1hs0", "l1hs1"])
    ctx.free(xgfk)
    yac = ctx.sb("l1yac", [128, 16 * 1024], BF16, side="right")
    hgb = ctx.sb("l1hgb", [128, 8 * TLOC], BF16)
    w2b = ctx.sb("l1w2b", [128, 8 * 1024], BF16)
    for g_ in range(4):
        for j in range(8):
            ctx.dma("gpsimd", w2b[:, j * 1024:(j + 1) * 1024],
                    W2m[(g_ * 8 + j) * 128:(g_ * 8 + j + 1) * 128, :], writes=("l1w2b",))
            ctx.dma("sync", hgb[:, j * TLOC:(j + 1) * TLOC], h1_dr[g_ * 8 + j, :, :],
                    reads=("h1_dr",), writes=("l1hgb",))
        for tt in range(16):
            for hf in range(2):
                pp = (tt * 2 + hf) % 2
                pk = ["psA0", "psA1"][pp]
                pv = [P["psA"][:, 0:512], P["psA"][:, 512:1024]][pp]
                parts = [(hgb[:, j * TLOC + tt * 128:j * TLOC + (tt + 1) * 128],
                          w2b[:, j * 1024 + hf * 512:j * 1024 + (hf + 1) * 512])
                         for j in range(8)]
                if g_ == 3:
                    parts.append((onesr[:, 0:128], b2row[:, hf * 512:(hf + 1) * 512]))
                mm(ctx, pv[:, 0:512], parts,
                   reads=("l1w2b", "l1hgb", "onesr", "l1b2r"), writes=(pk,))
                ys = yac[:, tt * 1024 + hf * 512:tt * 1024 + (hf + 1) * 512]
                if g_ == 0:
                    vop(ctx, "tensor_copy", ys, pv[:, 0:512], reads=(pk,),
                        writes=("l1yac",))
                else:
                    vop(ctx, "tensor_add", ys, ys, pv[:, 0:512],
                        reads=(pk, "l1yac"), writes=("l1yac",))
    wosc = ctx.sb("l1wos", [128, 1024], BF16)
    for tt in range(16):
        vop(ctx, "tensor_scalar", wosc[:], yac[:, tt * 1024:(tt + 1) * 1024],
            wvals[:, tt:tt + 1], None, ALU.mult, reads=("l1yac", "l1wv2"),
            writes=("l1wos",))
        ctx.dma("sync", wout_in[tt * 128:(tt + 1) * 128, :], wosc[:],
                reads=("l1wos",), writes=("wout_in",))
    ctx.free(["l1hgb", "l1w2b", "l1wos", "l1b2r", "l1b1r"])
    ctx.collective("AllGather", ALU.bypass, GRP8, wout_in, wout_out[0:NCORES * TLOC, :],
                   reads=("wout_in",), writes=("wout_out",))
    zpad = ctx.sb("l1zp", [128, 1024], BF16)
    vop(ctx, "memset", zpad[:], 0.0, writes=("l1zp",))
    ctx.dma("sync", wout_out[NCORES * TLOC:NCORES * TLOC + 128, :], zpad[:],
            reads=("l1zp",), writes=("wout_out",))

    acc = ctx.sb("l1acc", [128, 1024], BF16)
    rixt = ctx.sb("l1rixt", [128, 1], I32)
    for tt in range(16):
        vop(ctx, "memset", acc[:], 0.0, writes=("l1acc",), eng="gpsimd")
        for r_ in range(R1):
            ctx.dma("sync", rixt[:], rix[r_, tt:tt + 1, :], writes=("l1rixt",))
            ctx.idma(acc[:], wout_out[:], rixt[:, 0:1],
                     reads=("l1rixt", "wout_out", "l1acc"), writes=("l1acc",),
                     compute_op=ALU.add)
        for dm in range(8):
            ctx.op("tensor", lambda t, dm=dm: t.transpose(
                P["psT"][:, 0:128], acc[:, dm * 128:(dm + 1) * 128], sb_id[:]),
                reads=("l1acc", "l1id"), writes=("psT",))
            vop(ctx, "tensor_add", xt[dm][:, tt * 128:(tt + 1) * 128],
                xt[dm][:, tt * 128:(tt + 1) * 128], P["psT"][:, 0:128],
                reads=("psT", f"x{dm}"), writes=(f"x{dm}",))
    if False:
        for nm, ap_, shp, dt in (("dbg1_q", q_dr, [256, T], BF16),
                                 ("dbg1_k", k_dr, [256, T], BF16),
                                 ("dbg1_v", v_dr, [T, 256], BF16),
                                 ("dbg1_o", o_dr, [256, T], BF16),
                                 ("dbg1_wopi", wop_in, [T, 1024], BF16),
                                 ("dbg1_wopo", wop_out, [TLOC, 1024], BF16),
                                 ("dbg1_wouto", wout_out, [NCORES * TLOC + 128, 1024], BF16)):
            o_ = ctx.dram_t(nm, shp, dt, kind="ExternalOutput").ap()
            ctx.dma("sync", o_, ap_, reads=())
    ctx.free(["l1acc", "l1rixt", "l1zp", "l1yac"])
    ctx.free(xn1k)
    ctx.free(["l1wv2", "l1esel", "l1cm", "l1bo", "l1bvr", "l1bk", "l1bq",
              "l1RT", "l1id"])
    return xt



def emit_l3(ctx, P, xt, xsrc=None, parts=("attn", "ffn"), reload=None, dbg=()):
    AI = lambda name, shape, dt=F32: ctx.dram_t(name, shape, dt, kind="ExternalInput").ap()
    do_a = "attn" in parts
    do_f = "ffn" in parts
    if xsrc is None:
        xsrc = xt
    xsrck = [f"x{i}" for i in range(NDT)] if xsrc is xt else [f"xsrc{i}" for i in range(NDT)]
    idin = ctx.dram["ident"].ap() if "ident" in ctx.dram else AI("ident", [128, 128])
    sb_id3 = ctx.sb("l3id", [128, 128], BF16)
    ctx.dma("gpsimd", sb_id3[:], idin[:], writes=("l3id",))
    ones1 = ctx.sb("ones1", [1, 128], BF16)

    def ldc(name, rows, cols, src_slices, dt=F32, qn="sync"):
        tl = ctx.sb(name, [rows, cols], dt)
        for (dst_sl, src_ap) in src_slices:
            ctx.dma(qn, tl[dst_sl] if dst_sl else tl[:], src_ap, writes=(name,))
        return tl

    xn3 = [ctx.sb(f"l3xn{i}", [128, TLOC], BF16, side="right") for i in range(NDT)]
    xn3k = [f"l3xn{i}" for i in range(NDT)]
    if do_f:
        we_sb = ctx.sb("l3we", [128, 128], F32)
        xlo = [ctx.sb(f"l3xl{i}", [128, TLOC], BF16) for i in range(NDT)]
        xlok = [f"l3xl{i}" for i in range(NDT)]
        fm_norm(ctx, P, xsrc, xsrck, xn3, xn3k, 1e-5, True, None, None,
                width=TLOC, lo=xlo, lokeys=xlok)
    else:
        fm_norm(ctx, P, xsrc, xsrck, xn3, xn3k, 1e-5, True, None, None, width=TLOC)

    if do_f:
        WrT3 = AI("l3f_WrT", [1024, 8])
        wrtf = ldc("l3wrtf", 128, 64, [((slice(None), slice(kb * 8, (kb + 1) * 8)),
                   WrT3[kb * 128:(kb + 1) * 128, :]) for kb in range(8)])
        wrtb = ctx.sb("l3wrtb", [128, 64], BF16)
        wrtl = ctx.sb("l3wrtl", [128, 64], BF16)
        vop(ctx, "tensor_copy", wrtb[:], wrtf[:], reads=("l3wrtf",), writes=("l3wrtb",))
        vop(ctx, "tensor_sub", wrtl[:], wrtf[:], wrtb[:], reads=("l3wrtf", "l3wrtb"),
            writes=("l3wrtl",))
        lgs = ctx.sb("l3lgs", [128, 8], F32)
        mxs = ctx.sb("l3mxs", [128, 8], F32)
        dcol = ctx.sb("l3dcol", [128, 1], F32)
        w1c = ctx.sb("l3w1c", [128, 1], F32)
        w2c = ctx.sb("l3w2c", [128, 1], F32)
        me1 = ctx.sb("l3me1", [128, 8], F32)
        me2 = ctx.sb("l3me2", [128, 8], F32)
        for tt in range(16):
            ts_ = slice(tt * 128, (tt + 1) * 128)
            pp = []
            for kb in range(8):
                pp.append((xn3[kb][:, ts_], wrtb[:, kb * 8:(kb + 1) * 8]))
                pp.append((xlo[kb][:, ts_], wrtb[:, kb * 8:(kb + 1) * 8]))
                pp.append((xn3[kb][:, ts_], wrtl[:, kb * 8:(kb + 1) * 8]))
            mm(ctx, P["psB"][:, 0:8], pp,
               reads=tuple(xn3k) + tuple(xlok) + ("l3wrtb", "l3wrtl"), writes=("psB0",))
            vop(ctx, "tensor_copy", lgs[:], P["psB"][:, 0:8], reads=("psB0",),
                writes=("l3lgs",))
            vop(ctx, "max", mxs[:], lgs[:], reads=("l3lgs",), writes=("l3mxs",))
            vop(ctx, "tensor_sub", dcol[:], mxs[:, 0:1], mxs[:, 1:2], reads=("l3mxs",),
                writes=("l3dcol",))
            sact(ctx, w1c[:], dcol[:], AF.Sigmoid, reads=("l3dcol",), writes=("l3w1c",))
            vop(ctx, "tensor_scalar", w2c[:], w1c[:], -1.0, 1.0, ALU.mult, ALU.add,
                reads=("l3w1c",), writes=("l3w2c",))
            vop(ctx, "tensor_scalar", me1[:], lgs[:], mxs[:, 0:1], None, ALU.is_equal,
                reads=("l3lgs", "l3mxs"), writes=("l3me1",))
            vop(ctx, "tensor_scalar", me1[:], me1[:], w1c[:, 0:1], None, ALU.mult,
                reads=("l3me1", "l3w1c"), writes=("l3me1",))
            vop(ctx, "tensor_scalar", me2[:], lgs[:], mxs[:, 1:2], None, ALU.is_equal,
                reads=("l3lgs", "l3mxs"), writes=("l3me2",))
            vop(ctx, "tensor_scalar", me2[:], me2[:], w2c[:, 0:1], None, ALU.mult,
                reads=("l3me2", "l3w2c"), writes=("l3me2",))
            vop(ctx, "tensor_add", we_sb[:, tt * 8:(tt + 1) * 8], me1[:], me2[:],
                reads=("l3me1", "l3me2"), writes=("l3we",))
        ctx.free(["l3wrtf", "l3wrtb", "l3wrtl", "l3lgs", "l3mxs", "l3dcol",
                  "l3w1c", "l3w2c", "l3me1", "l3me2"])
        ctx.free(xlok)

    ccx3_in = ctx.dram_t("l3_ccx_in", [1024, TLOC], BF16).ap()
    if do_a:
        ccx3_out = ctx.dram_t("l3_ccx_out", [4096, TLOC], BF16).ap()
    for i in range(NDT):
        ctx.dma("sync", ccx3_in[i * 128:(i + 1) * 128, :], xn3[i][:],
                reads=(xn3k[i],), writes=("ccx3_in",))
    if do_a:
        ctx.collective("AllGather", ALU.bypass, GRP, ccx3_in, ccx3_out,
                       reads=("ccx3_in",), writes=("ccx3_out",))
        ctx.free(xn3k)
        emit_l3_attn(ctx, P, xt, ccx3_out, sb_id3, ones1, AI, ldc, dbg=dbg)
    if do_f:
        if reload is not None:
            for i in range(NDT):
                ctx.dma("sync", xt[i][:], reload[i * 128:(i + 1) * 128, :],
                        writes=(f"x{i}",))
        if do_a:
            xn3f = [ctx.sb(f"l3xf{i}", [128, TLOC], BF16) for i in range(NDT)]
            xn3fk = [f"l3xf{i}" for i in range(NDT)]
            for i in range(NDT):
                ctx.dma("sync", xn3f[i][:], ccx3_in[i * 128:(i + 1) * 128, :],
                        reads=("ccx3_in",), writes=(xn3fk[i],))
        else:
            xn3f, xn3fk = xn3, xn3k
        emit_l3_ffn(ctx, P, xt, xn3f, xn3fk, we_sb, sb_id3, AI)
    return xt


def emit_l3_attn(ctx, P, xt, ccx, sb_id3, ones1, AI, ldc, dbg=()):
    def col_slices(src, n, per=1):
        return [((slice(None), slice(i * per, (i + 1) * per)),
                 src[i * 128:(i + 1) * 128, :]) for i in range(n)]
    xm = {}
    for nm in ("x_r", "x_w", "x_k", "x_v", "x_a", "x_g"):
        ap = AI(f"l3_{nm}", [D, 1])
        t_ = ldc(f"l3m{nm}", 128, 8, col_slices(ap, 8))
        s_ = ctx.sb(f"l3s{nm}", [128, 8], F32)
        vop(ctx, "tensor_scalar", s_[:], t_[:], -1.0, 1.0, ALU.mult, ALU.add,
            reads=(f"l3m{nm}",), writes=(f"l3s{nm}",))
        xm[nm] = (t_, s_, f"l3m{nm}", f"l3s{nm}")
    wtmp = ctx.sb("l3wt", [128, 256], F32)
    folded = {}
    for (nm, wname, cols, mixnm) in (("r", "l3_Wr_my", 256, "x_r"),
                                     ("k", "l3_Wk_my", 256, "x_k"),
                                     ("v", "l3_Wv_my", 256, "x_v"),
                                     ("w", "l3_w1", 64, "x_w"),
                                     ("a", "l3_a1", 32, "x_a"),
                                     ("g", "l3_g1", 128, "x_g")):
        wap = AI(wname, [1024, cols])
        ft = ctx.sb(f"l3W{nm}", [128, 8 * 2 * cols], BF16)
        fk = f"l3W{nm}"
        mt, st, mk, sk = xm[mixnm]
        for kb in range(8):
            ctx.dma("sync", wtmp[:, 0:cols], wap[kb * 128:(kb + 1) * 128, :],
                    writes=("l3wt",))
            base = kb * 2 * cols
            vop(ctx, "tensor_scalar", ft[:, base:base + cols], wtmp[:, 0:cols],
                st[:, kb:kb + 1], None, ALU.mult, reads=("l3wt", sk), writes=(fk,))
            vop(ctx, "tensor_scalar", ft[:, base + cols:base + 2 * cols],
                wtmp[:, 0:cols], mt[:, kb:kb + 1], None, ALU.mult,
                reads=("l3wt", mk), writes=(fk,))
        folded[nm] = ft
    w2s = ldc("l3w2s", 64, 256, [(None, AI("l3_w2_my", [64, 256])[:])], BF16, qn="gpsimd")
    a2s = ldc("l3a2s", 32, 256, [(None, AI("l3_a2_my", [32, 256])[:])], BF16, qn="gpsimd")
    g2s = ldc("l3g2s", 128, 256, [(None, AI("l3_g2_my", [128, 256])[:])], BF16, qn="gpsimd")
    w0c = ldc("l3w0c", 128, 2, [(None, AI("l3_w0_my", [128, 2])[:])])
    a0c = ldc("l3a0c", 128, 2, [(None, AI("l3_a0_my", [128, 2])[:])])
    kkcol = ldc("l3kkc", 128, 2, [(None, AI("l3_kk_my", [128, 2])[:])])
    kacol = ldc("l3kac", 128, 2, [(None, AI("l3_ka_my", [128, 2])[:])])
    rkcol = ldc("l3rkc", 128, 2, [(None, AI("l3_rk_my", [128, 2])[:])])
    negw0 = ctx.sb("l3nw0", [128, 2], F32)
    omka = ctx.sb("l3omka", [128, 2], F32)
    vop(ctx, "tensor_scalar_mul", negw0[:], w0c[:], -1.0, reads=("l3w0c",),
        writes=("l3nw0",))
    vop(ctx, "tensor_scalar", omka[:], kacol[:], -1.0, 1.0, ALU.mult, ALU.add,
        reads=("l3kac",), writes=("l3omka",))
    hselC = ldc("l3hsC", 128, 2, [(None, AI("l3_hselC", [128, 2])[:])], BF16, qn="gpsimd")
    hselR = ldc("l3hsR", 2, 128, [(None, AI("l3_hsel", [2, 128])[:])], BF16, qn="gpsimd")

    ew_dr = ctx.dram_t("l3_ew_dr", [256, T], F32).ap()
    r_dr = ctx.dram_t("l3_r_dr", [256, T], BF16).ap()
    k2_dr = ctx.dram_t("l3_k2_dr", [256, T], BF16).ap()
    kkn_dr = ctx.dram_t("l3_kkn_dr", [256, T], BF16).ap()
    ag_dr = ctx.dram_t("l3_ag_dr", [256, T], BF16).ap()
    v_dr = ctx.dram_t("l3_v_dr", [256, T], BF16).ap()
    g_dr = ctx.dram_t("l3_g_dr", [T, 256], BF16).ap()
    vtm_dr = ctx.dram_t("l3_vtm_dr", [T, 256], BF16).ap()
    y_dr = ctx.dram_t("l3_y_dr", [T, 256], BF16).ap()
    s_dr = ctx.dram_t("l3_s_dr", [2, 2, T], BF16).ap()
    wop3_in = ctx.dram_t("l3_wop_in", [T, 1024], BF16).ap()
    wop3_out = ctx.dram_t("l3_wop_out", [TLOC, 1024], BF16).ap()

    nsb = ctx.sb("l3nsb", [128, 8 * 513], BF16)
    rev = ctx.sb("l3rev", [128, 1024], BF16)
    kev = ctx.sb("l3kev", [128, 1024], BF16)
    vev = ctx.sb("l3vev", [128, 1024], BF16)
    agt = ctx.sb("l3agt", [128, 1024], BF16)
    t1 = ctx.sb("l3t1", [64, 512], BF16)
    sa1 = ctx.sb("l3sa1", [32, 512], BF16)
    sg1 = ctx.sb("l3sg1", [128, 512], BF16)
    au = ctx.sb("l3au", [128, 512], F32)
    eu = ctx.sb("l3eu", [128, 512], F32)
    lp = ctx.sb("l3lp", [128, 512], F32)
    mn = ctx.sb("l3mn", [128, 512], F32)
    dd = ctx.sb("l3dd", [128, 512], F32)
    ewt = ctx.sb("l3ewt", [128, 512], F32)
    kkt = ctx.sb("l3kkt", [128, 512], BF16)
    sqk = ctx.sb("l3sqk", [128, 512], BF16)
    nrm2 = ctx.sb("l3nr2", [2, 512], F32)
    nrb = ctx.sb("l3nrb", [2, 512], BF16)
    kknt = ctx.sb("l3kkn", [128, 512], BF16)
    k2t = ctx.sb("l3k2t", [128, 512], BF16)
    tt2 = ctx.sb("l3tt2", [128, 512], BF16)
    gtile = ctx.sb("l3gti", [128, 256], BF16)
    srow = ctx.sb("l3srw", [2, 512], BF16)
    stash = ctx.sb("l3stash", [128, 8], BF16)
    prA = ("l3Wr", "l3Wk", "l3Wv", "l3Ww", "l3Wa", "l3Wg", "l3nsb")
    for tb in (() if "nopre" in dbg else range(4)):
        for ci in range(4):
            gcol = tb * TLOC + ci * 512
            if ci > 0:
                for kb in range(8):
                    ctx.dma("sync", nsb[:, kb * 513:kb * 513 + 513],
                            ccx[tb * 1024 + kb * 128:tb * 1024 + (kb + 1) * 128,
                                ci * 512 - 1:ci * 512 + 512],
                            reads=("ccx3_out",), writes=("l3nsb",))
            else:
                for kb in range(8):
                    ctx.dma("sync", nsb[:, kb * 513 + 1:kb * 513 + 513],
                            ccx[tb * 1024 + kb * 128:tb * 1024 + (kb + 1) * 128,
                                0:512],
                            reads=("ccx3_out",), writes=("l3nsb",))
                if gcol == 0:
                    for kb in range(8):
                        vop(ctx, "memset", nsb[:, kb * 513:kb * 513 + 1], 0.0,
                            writes=("l3nsb",))
                else:
                    for kb in range(8):
                        vop(ctx, "tensor_copy", nsb[:, kb * 513:kb * 513 + 1],
                            stash[:, kb:kb + 1], reads=("l3stash",),
                            writes=("l3nsb",))

            if ci == 3:
                for kb in range(8):
                    vop(ctx, "tensor_copy", stash[:, kb:kb + 1],
                        nsb[:, kb * 513 + 512:kb * 513 + 513], reads=("l3nsb",),
                        writes=("l3stash",))

            def pparts(nm, cols, m):
                ft = folded[nm]
                ps = []
                for kb in range(8):
                    base = kb * 2 * cols
                    ps.append((ft[:, base + m * 128:base + m * 128 + min(128, cols)],
                               nsb[:, kb * 513 + 1:kb * 513 + 513]))
                    ps.append((ft[:, base + cols + m * 128:
                                  base + cols + m * 128 + min(128, cols)],
                               nsb[:, kb * 513:kb * 513 + 512]))
                return ps
            mm(ctx, P["psB"][0:32, 1024:1536], pparts("a", 32, 0), reads=prA,
               writes=("psB2",))
            vop(ctx, "tensor_copy", sa1[:], P["psB"][0:32, 1024:1536],
                reads=("psB2",), writes=("l3sa1",))
            for m in range(2):
                mm(ctx, P["psA"][:, 0:512], [(a2s[:, m * 128:(m + 1) * 128], sa1[:])],
                   reads=("l3a2s", "l3sa1"), writes=("psA0",))
                sact(ctx, agt[:, m * 512:(m + 1) * 512], P["psA"][:, 0:512],
                     AF.Sigmoid, bias=a0c[:, m:m + 1],
                     reads=("psA0", "l3a0c"), writes=("l3agt",))
                ctx.dma("sync", ag_dr[m * 128:(m + 1) * 128, gcol:gcol + 512],
                        agt[:, m * 512:(m + 1) * 512], reads=("l3agt",),
                        writes=("ag_dr",))
            for (nm, dst, dk_, sbuf, sk) in (("r", r_dr, "r_dr", rev, "l3rev"),
                                             ("k", None, None, kev, "l3kev"),
                                             ("v", v_dr, "v_dr", vev, "l3vev")):
                for m in range(2):
                    pv = [P["psA"][:, 0:512], P["psA"][:, 512:1024]][m]
                    pk = ["psA0", "psA1"][m]
                    mm(ctx, pv, pparts(nm, 256, m), reads=prA, writes=(pk,))
                    vop(ctx, "tensor_copy", sbuf[:, m * 512:(m + 1) * 512], pv,
                        reads=(pk,), writes=(sk,))
                    if dst is not None:
                        ctx.dma("sync", dst[m * 128:(m + 1) * 128, gcol:gcol + 512],
                                sbuf[:, m * 512:(m + 1) * 512], reads=(sk,),
                                writes=(dk_,))
            mm(ctx, P["psB"][0:64, 1024:1536], pparts("w", 64, 0), reads=prA,
               writes=("psB2",))
            sact(ctx, t1[:], P["psB"][0:64, 1024:1536], AF.Tanh, reads=("psB2",),
                 writes=("l3t1",))
            for m in range(2):
                mm(ctx, P["psA"][:, 0:512], [(w2s[:, m * 128:(m + 1) * 128], t1[:])],
                   reads=("l3w2s", "l3t1"), writes=("psA0",))
                vop(ctx, "tensor_scalar", au[:], P["psA"][:, 0:512],
                    w0c[:, m:m + 1], None, ALU.add, reads=("psA0", "l3w0c"),
                    writes=("l3au",))
                vop(ctx, "tensor_scalar", mn[:], au[:], 0.0, None, ALU.min,
                    reads=("l3au",), writes=("l3mn",))
                sact(ctx, eu[:], au[:], AF.Abs, reads=("l3au",), writes=("l3eu",))
                sact(ctx, eu[:], eu[:], AF.Exp, scale=-1.0, reads=("l3eu",),
                     writes=("l3eu",))
                sact(ctx, lp[:], eu[:], AF.Ln, bias=1.0, reads=("l3eu",),
                     writes=("l3lp",))
                vop(ctx, "scalar_tensor_tensor", dd[:], mn[:], -0.5, lp[:],
                    ALU.add, ALU.subtract, reads=("l3mn", "l3lp"), writes=("l3dd",))
                sact(ctx, ewt[:], dd[:], AF.Exp, reads=("l3dd",), writes=("l3ewt",))
                ctx.dma("sync", ew_dr[m * 128:(m + 1) * 128, gcol:gcol + 512],
                        ewt[:], reads=("l3ewt",), writes=("ew_dr",))
            mm(ctx, P["psB"][:, 1024:1536], pparts("g", 128, 0), reads=prA,
               writes=("psB2",))
            sact(ctx, sg1[:], P["psB"][:, 1024:1536], AF.Sigmoid, reads=("psB2",),
                 writes=("l3sg1",))
            for ts in range(4):
                mm(ctx, P["psA"][:, 512:768], [(sg1[:, ts * 128:(ts + 1) * 128], g2s[:])],
                   reads=("l3sg1", "l3g2s"), writes=("psA1",))
                vop(ctx, "tensor_copy", gtile[:], P["psA"][:, 512:768],
                    reads=("psA1",), writes=("l3gti",))
                ctx.dma("sync", g_dr[gcol + ts * 128:gcol + (ts + 1) * 128, :],
                        gtile[:], reads=("l3gti",), writes=("g_dr",))
            for m in range(2):
                ksl = kev[:, m * 512:(m + 1) * 512]
                vop(ctx, "tensor_scalar", kkt[:], ksl, kkcol[:, m:m + 1], None,
                    ALU.mult, reads=("l3kev", "l3kkc"), writes=("l3kkt",))
                sact(ctx, sqk[:], kkt[:], AF.Square, reads=("l3kkt",), writes=("l3sqk",))
                mm(ctx, P["psB"][0:2, 1024:1536], [(hselC[:], sqk[:])],
                   reads=("l3hsC", "l3sqk"), writes=("psB2",))
                vop(ctx, "tensor_copy", nrm2[:], P["psB"][0:2, 1024:1536],
                    reads=("psB2",), writes=("l3nr2",))
                sact(ctx, nrm2[:], nrm2[:], AF.Ln, reads=("l3nr2",), writes=("l3nr2",))
                sact(ctx, nrm2[:], nrm2[:], AF.Exp, scale=-0.5, reads=("l3nr2",),
                     writes=("l3nr2",))
                vop(ctx, "tensor_scalar", nrm2[:], nrm2[:], 1e12, None, ALU.min,
                    reads=("l3nr2",), writes=("l3nr2",))
                vop(ctx, "tensor_copy", nrb[:], nrm2[:], reads=("l3nr2",),
                    writes=("l3nrb",))
                mm(ctx, P["psA"][:, 1024:1536], [(hselR[:], nrb[:])],
                   reads=("l3hsR", "l3nrb"), writes=("psA2",))
                vop(ctx, "tensor_mul", kknt[:], kkt[:], P["psA"][:, 1024:1536],
                    reads=("l3kkt", "psA2"), writes=("l3kkn",))
                ctx.dma("sync", kkn_dr[m * 128:(m + 1) * 128, gcol:gcol + 512],
                        kknt[:], reads=("l3kkn",), writes=("kkn_dr",))
                vop(ctx, "tensor_scalar", tt2[:], agt[:, m * 512:(m + 1) * 512],
                    kacol[:, m:m + 1], None, ALU.mult,
                    reads=("l3agt", "l3kac"), writes=("l3tt2",))
                vop(ctx, "tensor_scalar", tt2[:], tt2[:], omka[:, m:m + 1], None,
                    ALU.add, reads=("l3tt2", "l3omka"), writes=("l3tt2",))
                vop(ctx, "tensor_mul", k2t[:], ksl, tt2[:], reads=("l3kev", "l3tt2"),
                    writes=("l3k2t",))
                ctx.dma("sync", k2_dr[m * 128:(m + 1) * 128, gcol:gcol + 512],
                        k2t[:], reads=("l3k2t",), writes=("k2_dr",))
                vop(ctx, "tensor_mul", tt2[:], rev[:, m * 512:(m + 1) * 512], k2t[:],
                    reads=("l3rev", "l3k2t"), writes=("l3tt2",))
                vop(ctx, "tensor_scalar", tt2[:], tt2[:], rkcol[:, m:m + 1], None,
                    ALU.mult, reads=("l3tt2", "l3rkc"), writes=("l3tt2",))
                mm(ctx, P["psB"][0:2, 1024:1536], [(hselC[:], tt2[:])],
                   reads=("l3hsC", "l3tt2"), writes=("psB2",))
                vop(ctx, "tensor_copy", srow[:], P["psB"][0:2, 1024:1536],
                    reads=("psB2",), writes=("l3srw",))
                ctx.dma("sync", s_dr[m, :, gcol:gcol + 512], srow[:],
                        reads=("l3srw",), writes=("s_dr",))
    ctx.free(["l3nsb", "l3rev", "l3kev", "l3vev", "l3agt", "l3t1", "l3sa1",
              "l3sg1", "l3au", "l3eu", "l3lp", "l3mn", "l3dd", "l3ewt", "l3kkt",
              "l3sqk", "l3nr2", "l3nrb", "l3kkn", "l3k2t", "l3tt2", "l3gti",
              "l3srw", "l3stash"])
    ctx.free(["l3wt"] + [f"l3W{n}" for n in ("r", "k", "v", "w", "a", "g")] +
             ["l3w2s", "l3a2s", "l3g2s", "l3w0c", "l3a0c", "l3kkc", "l3kac",
              "l3rkc", "l3nw0", "l3omka", "l3hsC", "l3hsR"])
    ctx.free([f"l3m{nm}" for nm in ("x_r", "x_w", "x_k", "x_v", "x_a", "x_g")] +
             [f"l3s{nm}" for nm in ("x_r", "x_w", "x_k", "x_v", "x_a", "x_g")])

    mskL = ldc("l3mkL", 128, 128, [(None, AI("l3_mskL", [128, 128])[:])], BF16, qn="gpsimd")
    mskU = ldc("l3mkU", 128, 128, [(None, AI("l3_mskU", [128, 128])[:])], BF16, qn="gpsimd")
    mskUI = ldc("l3mkI", 128, 128, [(None, AI("l3_mskUI", [128, 128])[:])], BF16, qn="gpsimd")
    zz2k = ctx.sb("l3zz", [128, 2048], F32)
    vop(ctx, "memset", zz2k[:], 0.0, writes=("l3zz",))
    ewin = ctx.sb("l3ewi", [128, T], F32)
    ewc = ctx.sb("l3ewc", [128, T + 1], F32)
    ST = ctx.sb("l3ST", [128, 64], F32)
    STb = ctx.sb("l3STb", [128, 64], BF16)
    kkc = ctx.sb("l3c_kk", [128, 128], BF16)
    agc = ctx.sb("l3c_ag", [128, 128], BF16)
    kc2 = ctx.sb("l3c_k2", [128, 128], BF16)
    rc = ctx.sb("l3c_r", [128, 128], BF16)
    vcf = ctx.sb("l3c_v", [128, 128], BF16)
    P1 = ctx.sb("l3P1", [128, 128], F32)
    Pp1 = ctx.sb("l3Pp1", [128, 128], F32)
    Pi1 = ctx.sb("l3Pi1", [128, 128], F32)
    nPp1 = ctx.sb("l3nP1", [128, 128], F32)
    P2h = ctx.sb("l3P2", [128, 64], F32)
    Pp2h = ctx.sb("l3Pp2", [128, 64], F32)
    Pi2h = ctx.sb("l3Pi2", [128, 64], F32)
    nPp2 = ctx.sb("l3nP2", [128, 64], F32)
    noff = ctx.sb("l3nof", [128, 2], F32)
    tmpa = ctx.sb("l3tma", [128, 128], F32)
    tmpb = ctx.sb("l3tmb", [128, 128], F32)
    tm2a = ctx.sb("l3t2a", [128, 64], F32)
    tm2b = ctx.sb("l3t2b", [128, 64], F32)
    ah1 = ctx.sb("l3ah1", [128, 128], BF16)
    bh1 = ctx.sb("l3bh1", [128, 128], BF16)
    kh1 = ctx.sb("l3kh1", [128, 128], BF16)
    rh1 = ctx.sb("l3rh1", [128, 128], BF16)
    bb_ = ctx.sb("l3bb", [128, 128], BF16)
    ah2 = ctx.sb("l3ah2", [128, 64], BF16)
    bh2 = ctx.sb("l3bh2", [128, 64], BF16)
    kh2 = ctx.sb("l3kh2", [128, 64], BF16)
    rh2 = ctx.sb("l3rh2", [128, 64], BF16)
    bbar = ctx.sb("l3bbr", [128, 128], BF16)
    kbar = ctx.sb("l3kbr", [128, 128], BF16)
    vtm = ctx.sb("l3vtm", [128, 128], BF16)
    btm = ctx.sb("l3btm", [128, 128], BF16)
    ktm = ctx.sb("l3ktm", [128, 128], BF16)
    Ms = ctx.sb("l3Ms", [128, 128], BF16)
    MTs = ctx.sb("l3MTs", [128, 128], BF16)
    NTs = ctx.sb("l3NTs", [128, 128], BF16)
    MrTs = ctx.sb("l3MrT", [128, 128], BF16)
    NrTs = ctx.sb("l3NrT", [128, 128], BF16)
    M2s = ctx.sb("l3M2s", [128, 128], BF16)
    MT2s = ctx.sb("l3MT2", [128, 128], BF16)
    IMp = ctx.sb("l3IMp", [128, 128], BF16)
    xsb = ctx.sb("l3xsb", [128, 64], BF16)
    Hsb = ctx.sb("l3Hsb", [128, 64], BF16)
    ysb = ctx.sb("l3ysb", [128, 64], BF16)
    for p in (() if "nowkv" in dbg else range(2)):
        ctx.dma("sync", ewin[:], ew_dr[p * 128:(p + 1) * 128, :], reads=("ew_dr",),
                writes=("l3ewi",))
        vop(ctx, "memset", ewc[:, 0:1], 0.0, writes=("l3ewc",))
        for sc in range(4):
            ini = 0.0 if sc == 0 else ewc[:, sc * 2048:sc * 2048 + 1]
            vop(ctx, "tensor_tensor_scan", ewc[:, 1 + sc * 2048:1 + (sc + 1) * 2048],
                ewin[:, sc * 2048:(sc + 1) * 2048], zz2k[:], ini, ALU.add, ALU.add,
                reads=("l3ewi", "l3zz", "l3ewc"), writes=("l3ewc",))
        vop(ctx, "memset", ST[:], 0.0, writes=("l3ST",))
        vop(ctx, "memset", STb[:], 0.0, writes=("l3STb",))
        for c in range(64):
            tok0 = c * 128
            pr = slice(p * 128, (p + 1) * 128)
            ctx.dma("sync", kkc[:], kkn_dr[pr, tok0:tok0 + 128], reads=("kkn_dr",),
                    writes=("l3c_kk",))
            ctx.dma("gpsimd", agc[:], ag_dr[pr, tok0:tok0 + 128], reads=("ag_dr",),
                    writes=("l3c_ag",))
            ctx.dma("sync", kc2[:], k2_dr[pr, tok0:tok0 + 128], reads=("k2_dr",),
                    writes=("l3c_k2",))
            ctx.dma("gpsimd", rc[:], r_dr[pr, tok0:tok0 + 128], reads=("r_dr",),
                    writes=("l3c_r",))
            ctx.dma("sync", vcf[:], v_dr[pr, tok0:tok0 + 128], reads=("v_dr",),
                    writes=("l3c_v",))
            off = ewc[:, tok0:tok0 + 1]
            off2 = ewc[:, tok0 + 64:tok0 + 65]
            vop(ctx, "tensor_scalar", tmpa[:], ewc[:, tok0 + 1:tok0 + 129], off,
                None, ALU.subtract, reads=("l3ewc",), writes=("l3tma",))
            vop(ctx, "tensor_scalar", tmpb[:], ewc[:, tok0:tok0 + 128], off,
                None, ALU.subtract, reads=("l3ewc",), writes=("l3tmb",))
            vop(ctx, "tensor_scalar", tm2a[:], ewc[:, tok0 + 65:tok0 + 129], off2,
                None, ALU.subtract, reads=("l3ewc",), writes=("l3t2a",))
            vop(ctx, "tensor_scalar", tm2b[:], ewc[:, tok0 + 64:tok0 + 128], off2,
                None, ALU.subtract, reads=("l3ewc",), writes=("l3t2b",))
            sact(ctx, P1[:], tmpa[:], AF.Exp, scale=-1.0, reads=("l3tma",),
                 writes=("l3P1",))
            sact(ctx, Pi1[:], tmpa[:], AF.Exp, reads=("l3tma",), writes=("l3Pi1",))
            sact(ctx, Pp1[:], tmpb[:], AF.Exp, scale=-1.0, reads=("l3tmb",),
                 writes=("l3Pp1",))
            sact(ctx, P2h[:], tm2a[:], AF.Exp, scale=-1.0, reads=("l3t2a",),
                 writes=("l3P2",))
            sact(ctx, Pi2h[:], tm2a[:], AF.Exp, reads=("l3t2a",), writes=("l3Pi2",))
            sact(ctx, Pp2h[:], tm2b[:], AF.Exp, scale=-1.0, reads=("l3t2b",),
                 writes=("l3Pp2",))
            vop(ctx, "tensor_scalar_mul", nPp1[:], Pp1[:], -1.0, eng="gpsimd",
                reads=("l3Pp1",), writes=("l3nP1",))
            vop(ctx, "tensor_scalar_mul", nPp2[:], Pp2h[:], -1.0, eng="gpsimd",
                reads=("l3Pp2",), writes=("l3nP2",))
            vop(ctx, "tensor_mul", ah1[:], kkc[:], nPp1[:], eng="gpsimd",
                reads=("l3c_kk", "l3nP1"), writes=("l3ah1",))
            vop(ctx, "tensor_mul", bb_[:], kkc[:], agc[:], eng="gpsimd",
                reads=("l3c_kk", "l3c_ag"), writes=("l3bb",))
            vop(ctx, "tensor_mul", bh1[:], bb_[:], Pi1[:], eng="gpsimd",
                reads=("l3bb", "l3Pi1"), writes=("l3bh1",))
            vop(ctx, "tensor_mul", kh1[:], kc2[:], Pi1[:], eng="gpsimd",
                reads=("l3c_k2", "l3Pi1"), writes=("l3kh1",))
            vop(ctx, "tensor_mul", rh1[:], rc[:], P1[:], eng="gpsimd",
                reads=("l3c_r", "l3P1"), writes=("l3rh1",))
            vop(ctx, "tensor_mul", ah2[:], kkc[:, 64:128], nPp2[:], eng="gpsimd",
                reads=("l3c_kk", "l3nP2"), writes=("l3ah2",))
            vop(ctx, "tensor_mul", bh2[:], bb_[:, 64:128], Pi2h[:], eng="gpsimd",
                reads=("l3bb", "l3Pi2"), writes=("l3bh2",))
            vop(ctx, "tensor_mul", kh2[:], kc2[:, 64:128], Pi2h[:], eng="gpsimd",
                reads=("l3c_k2", "l3Pi2"), writes=("l3kh2",))
            vop(ctx, "tensor_mul", rh2[:], rc[:, 64:128], P2h[:], eng="gpsimd",
                reads=("l3c_r", "l3P2"), writes=("l3rh2",))
            PL = P1[:, 127:128]
            vop(ctx, "tensor_scalar", bbar[:], bh1[:], PL, None, ALU.mult,
                reads=("l3bh1", "l3P1"), writes=("l3bbr",))
            vop(ctx, "tensor_scalar", kbar[:], kh1[:], PL, None, ALU.mult,
                reads=("l3kh1", "l3P1"), writes=("l3kbr",))
            for (src_, sk_, dst_, dk_) in ((vcf, "l3c_v", vtm, "l3vtm"),
                                           (bbar, "l3bbr", btm, "l3btm"),
                                           (kbar, "l3kbr", ktm, "l3ktm")):
                ctx.op("tensor", lambda t, src_=src_: t.transpose(
                    P["psT"][:, 0:128], src_[:], sb_id3[:]),
                    reads=(sk_, "l3id"), writes=("psT",))
                vop(ctx, "tensor_copy", dst_[:], P["psT"][:, 0:128],
                    reads=("psT",), writes=(dk_,))
            ctx.dma("sync", vtm_dr[tok0:tok0 + 128, p * 128:(p + 1) * 128], vtm[:],
                    reads=("l3vtm",), writes=("vtm_dr",))
            for h in range(2):
                hs = slice(h * 64, (h + 1) * 64)
                hc = slice(h * 64, (h + 1) * 64)
                pM = P["psA"][:, 0:128]
                pMT = P["psA"][:, 512:640]
                pNT = P["psA"][:, 1024:1152]
                pMrT = P["psA"][:, 1536:1664]
                pNrT = P["psB"][:, 0:128]
                for (pv, pk, A1, A2, B1, B2, msk, mk, dst_, dk_, du) in (
                    (pM, "psA0", ah1, ah2, bh1, bh2, mskL, "l3mkL", Ms, "l3Ms", 0),
                    (pMT, "psA1", bh1, bh2, ah1, ah2, mskU, "l3mkU", MTs, "l3MTs", 1),
                    (pNT, "psA2", kh1, kh2, ah1, ah2, mskU, "l3mkU", NTs, "l3NTs", 1),
                    (pMrT, "psA3", bh1, bh2, rh1, rh2, mskUI, "l3mkI", MrTs, "l3MrT", 1),
                    (pNrT, "psB0", kh1, kh2, rh1, rh2, mskUI, "l3mkI", NrTs, "l3NrT", 1),
                ):
                    rds = ("l3ah1", "l3ah2", "l3bh1", "l3bh2", "l3kh1", "l3kh2",
                           "l3rh1", "l3rh2")
                    mm(ctx, pv[0:64, 0:64], [(A1[hs, 0:64], B1[hs, 0:64])],
                       reads=rds, writes=(pk,))
                    mm(ctx, pv[64:128, 64:128], [(A2[hs, :], B2[hs, :])],
                       reads=rds, writes=(pk,))
                    if du == 0:
                        mm(ctx, pv[64:128, 0:64], [(A1[hs, 64:128], B1[hs, 0:64])],
                           reads=rds, writes=(pk,))
                    else:
                        mm(ctx, pv[0:64, 64:128], [(A1[hs, 0:64], B1[hs, 64:128])],
                           reads=rds, writes=(pk,))
                    vop(ctx, "tensor_mul", dst_[:], pv[0:128, 0:128], msk[:],
                        reads=(pk, mk), writes=(dk_,))
                pCa = P["psB"][:, 512:576]
                pCb = P["psB"][:, 1024:1088]
                mm(ctx, pCa, [(ah1[hs, :], STb[hs, :]), (NTs[:], vtm[:, hc])],
                   reads=("l3ah1", "l3STb", "l3NTs", "l3vtm"), writes=("psB1",))
                cur, oth = (pCa, "psB1"), (pCb, "psB2")
                curM, curMT, nxtM, nxtMT = (Ms, "l3Ms"), (MTs, "l3MTs"), \
                    (M2s, "l3M2s"), (MT2s, "l3MT2")
                for lev in range(7):
                    vop(ctx, "tensor_copy", xsb[:], cur[0], reads=(cur[1],),
                        writes=("l3xsb",))
                    vop(ctx, "tensor_add", IMp[:], curMT[0][:], sb_id3[:],
                        eng="gpsimd", reads=(curMT[1], "l3id"), writes=("l3IMp",))
                    mm(ctx, oth[0], [(IMp[:], xsb[:])],
                       reads=("l3IMp", "l3xsb"), writes=(oth[1],))
                    cur, oth = oth, cur
                    if lev < 6:
                        mm(ctx, pM, [(curMT[0][:], curM[0][:])],
                           reads=(curMT[1], curM[1]), writes=("psA0",))
                        vop(ctx, "tensor_copy", nxtM[0][:], pM[0:128, 0:128],
                            reads=("psA0",), writes=(nxtM[1],))
                        mm(ctx, pMT, [(curM[0][:], curMT[0][:])],
                           reads=(curM[1], curMT[1]), writes=("psA1",))
                        vop(ctx, "tensor_copy", nxtMT[0][:], pMT[0:128, 0:128],
                            reads=("psA1",), writes=(nxtMT[1],))
                        curM, curMT, nxtM, nxtMT = nxtM, nxtMT, curM, curMT
                vop(ctx, "tensor_copy", Hsb[:], cur[0], reads=(cur[1],),
                    writes=("l3Hsb",))
                pY = P["psA"][:, 1024:1088]
                mm(ctx, pY, [(rh1[hs, :], STb[hs, :]), (MrTs[:], Hsb[:]),
                             (NrTs[:], vtm[:, hc])],
                   reads=("l3rh1", "l3STb", "l3MrT", "l3NrT", "l3Hsb", "l3vtm"),
                   writes=("psA2",))
                vop(ctx, "tensor_copy", ysb[:], pY, reads=("psA2",), writes=("l3ysb",))
                ctx.dma("sync", y_dr[tok0:tok0 + 128, (2 * p + h) * 64:
                                     (2 * p + h + 1) * 64], ysb[:],
                        reads=("l3ysb",), writes=("y_dr",))
                pS = P["psA"][h * 64:(h + 1) * 64, 1536:1600]
                mm(ctx, pS, [(btm[:, hc], Hsb[:]), (ktm[:, hc], vtm[:, hc])],
                   reads=("l3btm", "l3ktm", "l3Hsb", "l3vtm"), writes=("psA3",))
                vop(ctx, "scalar_tensor_tensor", ST[hs, :], ST[hs, :],
                    P1[hs, 127:128], pS, ALU.mult, ALU.add,
                    reads=("l3ST", "l3P1", "psA3"), writes=("l3ST",))
            vop(ctx, "tensor_copy", STb[:], ST[:], reads=("l3ST",), writes=("l3STb",))
    ctx.free(["l3zz", "l3ewi", "l3ewc", "l3ST", "l3STb", "l3c_kk", "l3c_ag",
              "l3c_k2", "l3c_r", "l3c_v", "l3P1", "l3Pp1", "l3Pi1", "l3nP1",
              "l3P2", "l3Pp2", "l3Pi2", "l3nP2", "l3nof", "l3tma", "l3tmb", "l3t2a", "l3t2b", "l3ah1", "l3bh1",
              "l3kh1", "l3rh1", "l3bb", "l3ah2", "l3bh2", "l3kh2", "l3rh2",
              "l3bbr", "l3kbr", "l3vtm", "l3btm", "l3ktm", "l3Ms", "l3MTs",
              "l3NTs", "l3MrT", "l3NrT", "l3M2s", "l3MT2", "l3IMp", "l3xsb",
              "l3Hsb", "l3ysb"])

    EPS3 = 0.00064
    lnwr = AI("l3_lnw_my", [1, 256])
    lnbr = AI("l3_lnb_my", [1, 256])
    lnwrow = ldc("l3lwr", 1, 256, [(None, lnwr[:])], BF16, qn="gpsimd")
    lnbrow = ldc("l3lbr", 1, 256, [(None, lnbr[:])], BF16, qn="gpsimd")
    lnw_bc = ctx.sb("l3lwb", [128, 256], BF16)
    lnb_bc = ctx.sb("l3lbb", [128, 256], BF16)
    for (row, rk_, bc, bk_) in ((lnwrow, "l3lwr", lnw_bc, "l3lwb"),
                                (lnbrow, "l3lbr", lnb_bc, "l3lbb")):
        mm(ctx, P["psB"][:, 0:256], [(ones1[:], row[:])],
           reads=(rk_, "ones1"), writes=("psB0",))
        vop(ctx, "tensor_copy", bc[:], P["psB"][:, 0:256], reads=("psB0",),
            writes=(bk_,))
    WoT3 = AI("l3_WoT_my", [256, 1024])
    wosb = ldc("l3wos", 128, 2048, [((slice(None), slice(cb * 1024, (cb + 1) * 1024)),
               WoT3[cb * 128:(cb + 1) * 128, :]) for cb in range(2)], BF16, qn="gpsimd")
    ysb2 = ctx.sb("l3py", [128, 256], BF16)
    vtt = ctx.sb("l3pv", [128, 256], BF16)
    gtt = ctx.sb("l3pg", [128, 256], BF16)
    s0t = ctx.sb("l3ps0", [2, 128], BF16)
    s_tm = ctx.sb("l3psm", [128, 4], F32)
    sqt = ctx.sb("l3psq", [128, 256], F32)
    mcol = ctx.sb("l3pm1", [128, 4], F32)
    m2col = ctx.sb("l3pm2", [128, 4], F32)
    musq = ctx.sb("l3pmu", [128, 4], F32)
    varc = ctx.sb("l3pva", [128, 4], F32)
    ynt = ctx.sb("l3pyn", [128, 256], F32)
    ygb = ctx.sb("l3pyg", [128, 256], BF16)
    ygf = ctx.sb("l3pyf", [128, 256], BF16)
    wpe3 = ctx.sb("l3pwp", [128, 1024], BF16)
    for tt in (() if "nopost" in dbg else range(64)):
        tok0 = tt * 128
        ctx.dma("sync", ysb2[:], y_dr[tok0:tok0 + 128, :], reads=("y_dr",),
                writes=("l3py",))
        ctx.dma("gpsimd", vtt[:], vtm_dr[tok0:tok0 + 128, :], reads=("vtm_dr",),
                writes=("l3pv",))
        ctx.dma("gpsimd", gtt[:], g_dr[tok0:tok0 + 128, :], reads=("g_dr",),
                writes=("l3pg",))
        for m in range(2):
            ctx.dma("sync", s0t[:, 0:128] if m == 0 else s0t[:, 0:128],
                    s_dr[m, :, tok0:tok0 + 128], reads=("s_dr",), writes=("l3ps0",))
            ctx.op("tensor", lambda t: t.transpose(P["psT"][:, 0:2], s0t[:],
                                                   sb_id3[0:2, 0:2]),
                   reads=("l3ps0", "l3id"), writes=("psT",))
            vop(ctx, "tensor_copy", s_tm[:, m * 2:(m + 1) * 2], P["psT"][:, 0:2],
                reads=("psT",), writes=("l3psm",))
        sact(ctx, sqt[:], ysb2[:], AF.Square, reads=("l3py",), writes=("l3psq",))
        for h4 in range(4):
            seg = slice(h4 * 64, (h4 + 1) * 64)
            vop(ctx, "reduce_sum", mcol[:, h4:h4 + 1], ysb2[:, seg], AX.X,
                reads=("l3py",), writes=("l3pm1",))
            vop(ctx, "reduce_sum", m2col[:, h4:h4 + 1], sqt[:, seg], AX.X,
                reads=("l3psq",), writes=("l3pm2",))
        vop(ctx, "tensor_scalar_mul", mcol[:], mcol[:], 1.0 / 64,
            reads=("l3pm1",), writes=("l3pm1",))
        vop(ctx, "tensor_scalar_mul", m2col[:], m2col[:], 1.0 / 64,
            reads=("l3pm2",), writes=("l3pm2",))
        vop(ctx, "tensor_mul", musq[:], mcol[:], mcol[:], reads=("l3pm1",),
            writes=("l3pmu",))
        vop(ctx, "tensor_sub", varc[:], m2col[:], musq[:],
            reads=("l3pm2", "l3pmu"), writes=("l3pva",))
        vop(ctx, "tensor_scalar_add", varc[:], varc[:], EPS3, reads=("l3pva",),
            writes=("l3pva",))
        sact(ctx, varc[:], varc[:], AF.Ln, reads=("l3pva",), writes=("l3pva",))
        sact(ctx, varc[:], varc[:], AF.Exp, scale=-0.5, reads=("l3pva",),
             writes=("l3pva",))
        for h4 in range(4):
            seg = slice(h4 * 64, (h4 + 1) * 64)
            vop(ctx, "tensor_scalar", ynt[:, seg], ysb2[:, seg],
                mcol[:, h4:h4 + 1], None, ALU.subtract,
                reads=("l3py", "l3pm1"), writes=("l3pyn",))
            vop(ctx, "tensor_scalar", ynt[:, seg], ynt[:, seg],
                varc[:, h4:h4 + 1], None, ALU.mult,
                reads=("l3pyn", "l3pva"), writes=("l3pyn",))
        vop(ctx, "tensor_mul", ynt[:], ynt[:], lnw_bc[:], reads=("l3pyn", "l3lwb"),
            writes=("l3pyn",))
        vop(ctx, "tensor_add", ynt[:], ynt[:], lnb_bc[:], reads=("l3pyn", "l3lbb"),
            writes=("l3pyn",))
        for h4 in range(4):
            seg = slice(h4 * 64, (h4 + 1) * 64)
            vop(ctx, "scalar_tensor_tensor", ynt[:, seg], vtt[:, seg],
                s_tm[:, h4:h4 + 1], ynt[:, seg], ALU.mult, ALU.add,
                reads=("l3pv", "l3psm", "l3pyn"), writes=("l3pyn",))
        vop(ctx, "tensor_mul", ygb[:], ynt[:], gtt[:], reads=("l3pyn", "l3pg"),
            writes=("l3pyg",))
        for half in range(2):
            ctx.op("tensor", lambda t, half=half: t.transpose(
                P["psT"][:, 0:128], ygb[:, half * 128:(half + 1) * 128], sb_id3[:]),
                reads=("l3pyg", "l3id"), writes=("psT",))
            vop(ctx, "tensor_copy", ygf[:, half * 128:(half + 1) * 128],
                P["psT"][:, 0:128], reads=("psT",), writes=("l3pyf",))
        for hh in range(2):
            pv = [P["psA"][:, 0:512], P["psA"][:, 512:1024]][hh]
            pk = ["psA0", "psA1"][hh]
            mm(ctx, pv, [(ygf[:, cb * 128:(cb + 1) * 128],
                          wosb[:, cb * 1024 + hh * 512:cb * 1024 + (hh + 1) * 512])
                         for cb in range(2)],
               reads=("l3pyf", "l3wos"), writes=(pk,))
            vop(ctx, "tensor_copy", wpe3[:, hh * 512:(hh + 1) * 512], pv,
                reads=(pk,), writes=("l3pwp",))
        ctx.dma("sync", wop3_in[tok0:tok0 + 128, :], wpe3[:], reads=("l3pwp",),
                writes=("wop3_in",))
    ctx.collective("ReduceScatter", ALU.add, GRP, wop3_in, wop3_out,
                   reads=("wop3_in",), writes=("wop3_out",))
    wrow3 = ctx.sb("l3wr3", [128, 1024], BF16)
    for tt in range(16):
        ctx.dma("sync", wrow3[:], wop3_out[tt * 128:(tt + 1) * 128, :],
                reads=("wop3_out",), writes=("l3wr3",))
        for dm in range(8):
            ctx.op("tensor", lambda t, dm=dm: t.transpose(
                P["psT"][:, 0:128], wrow3[:, dm * 128:(dm + 1) * 128], sb_id3[:]),
                reads=("l3wr3", "l3id"), writes=("psT",))
            vop(ctx, "tensor_add", xt[dm][:, tt * 128:(tt + 1) * 128],
                xt[dm][:, tt * 128:(tt + 1) * 128], P["psT"][:, 0:128],
                reads=("psT", f"x{dm}"), writes=(f"x{dm}",))
    if "dbg3" in dbg:
        for nm, ap_, shp in (("dbg3_ew", ew_dr, [256, T]),):
            o = ctx.dram_t(nm, shp, F32, kind="ExternalOutput").ap()
            ctx.dma("sync", o, ap_, reads=())
        for nm, ap_, shp in (("dbg3_r", r_dr, [256, T]), ("dbg3_k2", k2_dr, [256, T]),
                             ("dbg3_kkn", kkn_dr, [256, T]), ("dbg3_ag", ag_dr, [256, T]),
                             ("dbg3_v", v_dr, [256, T]), ("dbg3_g", g_dr, [T, 256]),
                             ("dbg3_y", y_dr, [T, 256]), ("dbg3_wop", wop3_in, [T, 1024]),
                             ("dbg3_wout", wop3_out, [TLOC, 1024])):
            o = ctx.dram_t(nm, shp, BF16, kind="ExternalOutput").ap()
            ctx.dma("sync", o, ap_, reads=())
    ctx.free(["l3mkL", "l3mkU", "l3mkI", "l3lwr", "l3lbr", "l3lwb", "l3lbb",
              "l3wos", "l3py", "l3pv", "l3pg", "l3ps0", "l3psm", "l3psq",
              "l3pm1", "l3pm2", "l3pmu", "l3pva", "l3pyn", "l3pyg", "l3pyf",
              "l3pwp", "l3wr3"])


def emit_l3_ffn(ctx, P, xt, xn3f, xn3fk, we_sb, sb_id3, AI):
    W1T3 = AI("l3f_W1T", [8, 1024, 4096])
    W2T3 = AI("l3f_W2T", [8, 2048, 1024])
    act_dr = ctx.dram_t("l3_act_dr", [16, 128, TLOC], BF16).ap()
    moe3 = ctx.sb("l3moe", [128, 16 * 1024], BF16)
    w1b = ctx.sb("l3w1b", [128, 8 * 2048], BF16)
    w2b = w1b
    hgb = ctx.sb("l3hgb", [128, 8 * TLOC], BF16)
    sgt = ctx.sb("l3sgt", [128, 512], BF16)
    acb = ctx.sb("l3acb", [128, 512], BF16)
    for e in range(8):
        for half in range(2):
            for kb in range(8):
                ctx.dma("gpsimd", w1b[:, kb * 2048:(kb + 1) * 2048],
                        W1T3[e, kb * 128:(kb + 1) * 128,
                             half * 2048:(half + 1) * 2048], writes=("l3w1b",))
            for mp in range(8):
                for ci in range(4):
                    c0 = ci * 512
                    mm(ctx, P["psA"][:, 0:512],
                       [(w1b[:, kb * 2048 + mp * 256:kb * 2048 + mp * 256 + 128],
                         xn3f[kb][:, c0:c0 + 512]) for kb in range(8)],
                       reads=("l3w1b",) + tuple(xn3fk), writes=("psA0",))
                    mm(ctx, P["psA"][:, 512:1024],
                       [(w1b[:, kb * 2048 + mp * 256 + 128:kb * 2048 + (mp + 1) * 256],
                         xn3f[kb][:, c0:c0 + 512]) for kb in range(8)],
                       reads=("l3w1b",) + tuple(xn3fk), writes=("psA1",))
                    sact(ctx, sgt[:], P["psA"][:, 512:1024], AF.Silu,
                         reads=("psA1",), writes=("l3sgt",))
                    vop(ctx, "tensor_mul", acb[:], P["psA"][:, 0:512], sgt[:],
                        reads=("psA0", "l3sgt"), writes=("l3acb",))
                    ctx.dma("sync", act_dr[half * 8 + mp, :, c0:c0 + 512], acb[:],
                            reads=("l3acb",), writes=("act_dr",))
        for g2 in range(2):
            for j in range(8):
                ctx.dma("gpsimd", w2b[:, j * 1024:(j + 1) * 1024],
                        W2T3[e, (g2 * 8 + j) * 128:(g2 * 8 + j + 1) * 128, :],
                        writes=("l3w1b",))
                ctx.dma("sync", hgb[:, j * TLOC:(j + 1) * TLOC],
                        act_dr[g2 * 8 + j, :, :], reads=("act_dr",), writes=("l3hgb",))
            for tt in range(16):
                we_ap = we_sb[:, tt * 8 + e:tt * 8 + e + 1]
                for hf in range(2):
                    pp = (tt * 2 + hf) % 2
                    pk = ["psA0", "psA1"][pp]
                    pv = [P["psA"][:, 0:512], P["psA"][:, 512:1024]][pp]
                    mm(ctx, pv,
                       [(hgb[:, j * TLOC + tt * 128:j * TLOC + (tt + 1) * 128],
                         w2b[:, j * 1024 + hf * 512:j * 1024 + (hf + 1) * 512])
                        for j in range(8)],
                       reads=("l3w1b", "l3hgb"), writes=(pk,))
                    msl = moe3[:, tt * 1024 + hf * 512:tt * 1024 + (hf + 1) * 512]
                    if e == 0 and g2 == 0:
                        vop(ctx, "tensor_scalar", msl, pv, we_ap, None, ALU.mult,
                            reads=(pk, "l3we"), writes=("l3moe",))
                    else:
                        vop(ctx, "scalar_tensor_tensor", msl, pv, we_ap, msl,
                            ALU.mult, ALU.add,
                            reads=(pk, "l3we", "l3moe"), writes=("l3moe",))
    for tt in range(16):
        for dm in range(8):
            ctx.op("tensor", lambda t, tt=tt, dm=dm: t.transpose(
                P["psT"][:, 0:128],
                moe3[:, tt * 1024 + dm * 128:tt * 1024 + (dm + 1) * 128], sb_id3[:]),
                reads=("l3moe", "l3id"), writes=("psT",))
            vop(ctx, "tensor_add", xt[dm][:, tt * 128:(tt + 1) * 128],
                xt[dm][:, tt * 128:(tt + 1) * 128], P["psT"][:, 0:128],
                reads=("psT", f"x{dm}"), writes=(f"x{dm}",))
    ctx.free(["l3moe", "l3w1b", "l3hgb", "l3sgt", "l3acb"])
